# version 14
# baseline (speedup 1.0000x reference)
import numpy as np
import concourse.bass as bass
import concourse.mybir as mybir
from concourse.bass_utils import run_bass_kernel_spmd

F32 = mybir.dt.float32
F32R = mybir.dt.float32r
BF16 = mybir.dt.bfloat16
I32 = mybir.dt.int32
AF = mybir.ActivationFunctionType
OP = mybir.AluOpType
AX = mybir.AxisListType

ENGINES = ("pe", "act", "dve", "pool", "sp")


def _keys_of(ap, region):
    name = ap.tensor.name
    return (name, region)


class MB:

    def __init__(self, nc):
        self.nc = nc
        self.ops = {e: [] for e in ENGINES}
        self.count = {e: 0 for e in ENGINES}
        self.NDMA = 4
        self.dcount = {(e, i): 0 for e in ENGINES for i in range(self.NDMA)}
        self.dma_rr = {e: 0 for e in ENGINES}
        self.known = {e: {} for e in ENGINES}
        self.writer = {}
        self.readers = {}
        self.sems = {}
        self._sem_handles = []

    def _need(self, eng, reads, writes):
        need = {}
        def add(pk, v):
            if pk == ("t", eng):
                if eng == "pe":
                    return
            if pk[0] == "d":
                v = 16 * self.dcount[(pk[1], pk[2])]
            if need.get(pk, 0) < v:
                need[pk] = v
        def wmatch(k):
            name, reg = k
            out = []
            for (n2, r2), wv in self.writer.items():
                if n2 == name and (reg == "*" or r2 == "*" or r2 == reg):
                    out.append(wv)
            return out
        def rmatch(k):
            name, reg = k
            out = []
            for (n2, r2), rd in self.readers.items():
                if n2 == name and (reg == "*" or r2 == "*" or r2 == reg):
                    out.extend(rd.items())
            return out
        for k in reads:
            for wv in wmatch(k):
                add(*wv)
        for k in writes:
            for wv in wmatch(k):
                add(*wv)
            for pk, v in rmatch(k):
                add(pk, v)
        waits = []
        kn = self.known[eng]
        for pk, v in sorted(need.items()):
            if pk == ("t", eng):
                if kn.get(pk, 0) >= v:
                    continue
                waits.append((pk, v))
                kn[pk] = v
            else:
                if kn.get(pk, 0) >= v:
                    continue
                waits.append((pk, v))
                kn[pk] = v
        return waits

    def _finish(self, eng, reads, writes, is_dma):
        if is_dma:
            slot = self.dma_rr[eng]
            self.dma_rr[eng] = (slot + 1) % self.NDMA
            self.dcount[(eng, slot)] += 1
            pk, v = ("d", eng, slot), 16 * self.dcount[(eng, slot)]
        else:
            self.count[eng] += 1
            pk, v = ("t", eng), self.count[eng]
        for k in writes:
            name, reg = k
            if reg == "*":
                for k2 in [x for x in self.writer if x[0] == name]:
                    del self.writer[k2]
                for k2 in [x for x in self.readers if x[0] == name]:
                    del self.readers[k2]
            self.writer[k] = (pk, v)
            self.readers[k] = {}
        for k in reads:
            self.readers.setdefault(k, {})[pk] = v
        return pk, v

    PSUM_BASE = {"psA": 0, "psB": 1, "psC": 2, "psE": 3, "psT": 7}

    def _bank_keys(self, x):
        ap = _ap(x)
        try:
            name = ap.tensor.name
        except Exception:
            return []
        if name not in self.PSUM_BASE:
            return []
        es = 2 if ap.dtype == BF16 else 4
        b0 = (ap.offset * es) // 2048
        b1 = (ap.offset * es + max(0, ap.free_nbytes() - 1)) // 2048
        base = self.PSUM_BASE[name]
        return [("#BANK", base + b) for b in range(int(b0), int(b1) + 1)]

    def op(self, eng, fn, reads=(), writes=(), region="*", rregions=None, wregions=None,
           is_dma=False):
        def key(x, i, regs):
            if isinstance(x, tuple):
                return _keys_of(x[0], x[1])
            r = regs[i] if regs else region
            return _keys_of(x, r)
        rk = [key(x, i, rregions) for i, x in enumerate(reads)]
        wk = [key(x, i, wregions) for i, x in enumerate(writes)]
        for x in reads:
            rk.extend(self._bank_keys(x))
        for x in writes:
            wk.extend(self._bank_keys(x))
        waits = self._need(eng, rk, wk)
        self._apply_fence(eng, waits)
        if is_dma:
            slot = self.dma_rr[eng]
            dk = ("d", eng, slot)
            prev = 16 * self.dcount[(eng, slot)]
            if prev > 0 and self.known[eng].get(dk, 0) < prev:
                waits.append((dk, prev))
                self.known[eng][dk] = prev
        pk, v = self._finish(eng, rk, wk, is_dma)
        self.ops[eng].append((waits, fn, pk, v))

    def build(self):
        nc = self.nc
        for e in ENGINES:
            self.sems[("t", e)] = nc.alloc_semaphore("tick_" + e)
            for i in range(self.NDMA):
                self.sems[("d", e, i)] = nc.alloc_semaphore(f"dt_{e}_{i}")
        with nc.Block() as blk:
            self.emit_into_block(blk)

    def barrier(self, eng="sp"):
        waits = []
        kn = self.known[eng]
        for e in ENGINES:
            pk = ("t", e)
            if self.count[e] > 0 and kn.get(pk, 0) < self.count[e] and not (e == eng):
                waits.append((pk, self.count[e])); kn[pk] = self.count[e]
            for i in range(self.NDMA):
                dk = ("d", e, i)
                v = 16 * self.dcount[(e, i)]
                if v > 0 and kn.get(dk, 0) < v:
                    waits.append((dk, v)); kn[dk] = v
        pk, v = self._finish(eng, [], [], False)
        self.ops[eng].append((waits, lambda E: E.nop(), pk, v))
        return v

    def barrier_all(self):
        v = self.barrier("sp")
        self.fence = (("t", "sp"), v)

    fence = None

    def _apply_fence(self, eng, waits):
        if self.fence is not None:
            pk, v = self.fence
            if not (pk == ("t", eng)) and self.known[eng].get(pk, 0) < v:
                waits.insert(0, (pk, v))
                self.known[eng][pk] = v

    def emit_into_block(self, blk):
        nc = self.nc
        for eng in ENGINES:
            body = self.ops[eng]
            if not body:
                continue
            def make(body=body, eng=eng):
                def run(engobj):
                    for waits, fn, pk, v in body:
                        for (wpk, wv) in waits:
                            engobj.wait_ge(self.sems[wpk], wv)
                        inst = fn(engobj)
                        inst.then_inc(self.sems[pk], 16 if pk[0] == "d" else 1)
                return run
            getattr(blk, {"pe": "tensor", "act": "scalar", "dve": "vector",
                          "pool": "gpsimd", "sp": "sync"}[eng])(make())

    def final_wait(self):
        pass

    def dma(self, dst, src, eng="sp"):
        self.op(eng, lambda e: e.dma_start(out=_ap(dst), in_=_ap(src)),
                reads=[src], writes=[dst], is_dma=True)

    def mm(self, out, lhsT, rhs, start=True, stop=True, tile_position=None):
        assert _ap(lhsT).dtype == _ap(rhs).dtype, \
            f"matmul dtype mismatch {_ap(lhsT).dtype} vs {_ap(rhs).dtype}"
        self.op("pe", lambda e: e.matmul(_ap(out), _ap(lhsT), _ap(rhs), start=start,
                                         stop=stop, tile_position=tile_position),
                reads=[lhsT, rhs], writes=[out])

    def transpose(self, out, in_, ident):
        self.op("pe", lambda e: e.transpose(_ap(out), _ap(in_), _ap(ident)),
                reads=[in_, ident], writes=[out])

    def act(self, out, in_, func, bias=None, scale=None, accum_out=None, eng="act"):
        if func == AF.Sqrt:
            assert bias is None and scale is None and accum_out is None
            self.ts(out, in_, 1e-37, None, OP.max)
            self.act(out, out, AF.Ln)
            self.act(out, out, AF.Exp, scale=0.5)
            return
        if func == AF.Silu:
            tmp = self.cp_tmp
            n = _ap(in_).shape[-1]
            assert n <= 512 and scale is None and accum_out is None
            if bias is not None:
                self.ts(tmp[:, 0:n], in_, bias, None, OP.add)
            else:
                self.copy(tmp[:, 0:n], in_)
            self.act(out, tmp[:, 0:n], AF.Sigmoid)
            self.tt(out, tmp[:, 0:n], out, OP.mult)
            return
        if func == AF.Square:
            assert bias is None and scale is None
            try:
                inpsum = _ap(in_).tensor.name.startswith("ps")
            except Exception:
                inpsum = False
            if accum_out is not None:
                if inpsum:
                    self.copy(out, in_)
                    self.stt(out, out, 1.0, out, OP.mult, OP.mult, accum_out=accum_out)
                else:
                    self.stt(out, in_, 1.0, in_, OP.mult, OP.mult, accum_out=accum_out)
                return
            if inpsum:
                self.copy(out, in_)
                self.tt(out, out, out, OP.mult)
            else:
                self.tt(out, in_, in_, OP.mult)
            return
        kw = {}
        if bias is not None:
            kw["bias"] = _ap(bias) if not isinstance(bias, (int, float)) else bias
        if scale is not None:
            kw["scale"] = scale
        if accum_out is not None:
            kw["accum_out"] = _ap(accum_out)
        reads = [in_] + ([bias] if hasattr(bias, "tensor") or isinstance(bias, tuple) else [])
        writes = [out] + ([accum_out] if accum_out is not None else [])
        self.op(eng, lambda e: e.activation(_ap(out), _ap(in_), func, **kw),
                reads=reads, writes=writes)

    def tt(self, out, a, b, op, eng="dve"):
        self.op(eng, lambda e: e.tensor_tensor(_ap(out), _ap(a), _ap(b), op),
                reads=[a, b], writes=[out])

    def ts(self, out, a, s1, s2, op0, op1=None, accum_out=None, eng="dve"):
        reads = [a] + [x for x in (s1, s2) if hasattr(x, "tensor") or isinstance(x, tuple)]
        writes = [out] + ([accum_out] if accum_out is not None else [])
        kw = {}
        if accum_out is not None:
            kw["accum_out"] = _ap(accum_out)
        o1 = op1 if op1 is not None else OP.bypass
        def f(e):
            return e.tensor_scalar(_ap(out), _ap(a), _maybe_ap(s1), _maybe_ap(s2), op0, o1, **kw)
        self.op(eng, f, reads=reads, writes=writes)

    def stt(self, out, a, s, b, op0, op1, accum_out=None, eng="dve"):
        reads = [a, b] + ([s] if hasattr(s, "tensor") or isinstance(s, tuple) else [])
        writes = [out] + ([accum_out] if accum_out is not None else [])
        kw = {}
        if accum_out is not None:
            kw["accum_out"] = _ap(accum_out)
        self.op(eng, lambda e: e.scalar_tensor_tensor(_ap(out), _ap(a), _maybe_ap(s), _ap(b), op0, op1, **kw),
                reads=reads, writes=writes)

    def copy(self, out, in_, eng="dve"):
        if eng == "act":
            self.op(eng, lambda e: e.activation(_ap(out), _ap(in_), AF.Copy), reads=[in_], writes=[out])
        else:
            self.op(eng, lambda e: e.tensor_copy(_ap(out), _ap(in_)), reads=[in_], writes=[out])

    def reduce(self, out, in_, op, axis=AX.X, negate=None, eng="dve"):
        self.op(eng, lambda e: e.tensor_reduce(_ap(out), _ap(in_), axis, op, negate=negate),
                reads=[in_], writes=[out])

    def reciprocal(self, out, in_):
        self.op("dve", lambda e: e.reciprocal(_ap(out), _ap(in_)), reads=[in_], writes=[out])

    def select(self, out, mask, a, b):
        self.op("dve", lambda e: e.select(_ap(out), _ap(mask), _ap(a), _ap(b), add_drain=True),
                reads=[mask, a, b], writes=[out])

    def memset(self, out, val, eng="dve"):
        self.op(eng, lambda e: e.memset(_ap(out), val), reads=[], writes=[out])

    def bn_stats(self, out, in_):
        self.op("dve", lambda e: e.bn_stats(_ap(out), _ap(in_)), reads=[in_], writes=[out])

    def bn_aggr(self, out, in_):
        self.op("dve", lambda e: e.bn_aggr(_ap(out), _ap(in_)), reads=[in_], writes=[out])

    def scan(self, out, d0, d1, initial, op0, op1):
        rd = [d0, d1] + ([initial] if hasattr(initial, "tensor") or isinstance(initial, tuple) else [])
        self.op("dve", lambda e: e.tensor_tensor_scan(_ap(out), _ap(d0), _ap(d1),
                                                      _maybe_ap(initial), op0, op1),
                reads=rd, writes=[out])


def _ap(x):
    if isinstance(x, tuple):
        x = x[0]
    return x


def _maybe_ap(x):
    if x is None:
        return None
    if isinstance(x, (int, float)):
        return x
    return _ap(x)


from contextlib import ExitStack, contextmanager


class StopStage(Exception):
    pass

DM = 1024
DI = 2048
NH = 16
HD = 128


def ceil_div(a, b):
    return -(-a // b)


class Prog:

    def __init__(self, T):
        self.T = T
        self.NT = T // 128
        self.nc = bass.Bass("TRN2", target_bir_lowering=False)
        self.mb = MB(self.nc)
        self.inputs = {}
        self._uid = 0
        self.psA = self.nc.alloc_psum_tensor("psA", [128, 512], F32).ap()
        self.psB = self.nc.alloc_psum_tensor("psB", [128, 512], F32).ap()
        self.psC = self.nc.alloc_psum_tensor("psC", [128, 512], F32).ap()
        self.psE = self.nc.alloc_psum_tensor("psE", [128, 2048], F32).ap()
        self.psT = self.nc.alloc_psum_tensor("psT", [128, 512], BF16).ap()

    def uid(self, p):
        self._uid += 1
        return f"{p}{self._uid}"

    def din(self, name, shape, dt=F32):
        self.inputs[name] = (list(shape), dt)
        return self.nc.dram_tensor(name, list(shape), dt, kind="ExternalInput").ap()

    def dout(self, name, shape, dt=F32):
        return self.nc.dram_tensor(name, list(shape), dt, kind="ExternalOutput").ap()

    def dscratch(self, name, shape, dt=F32):
        return self.nc.dram_tensor(name, list(shape), dt, kind="Internal").ap()

    def sb(self, name, shape, dt=F32):
        return self.nc.alloc_sbuf_tensor(name, list(shape), dt).ap()

    @contextmanager
    def phase(self):
        try:
            with ExitStack() as es:
                def alloc(name, shape, dt=F32):
                    t = es.enter_context(self.nc.sbuf_tensor(self.uid("ph_") + name, list(shape), dt))
                    return t.ap()
                yield alloc
                self.mb.barrier_all()
        except StopStage:
            pass

    @contextmanager
    def stage(self):
        try:
            with ExitStack() as es:
                def alloc(name, shape, dt=F32):
                    t = es.enter_context(self.nc.sbuf_tensor(self.uid("st_") + name, list(shape), dt))
                    return t.ap()
                yield alloc
                self.mb.barrier_all()
        except StopStage:
            pass


def host_constants(T):
    C = {}
    i, j = np.mgrid[0:128, 0:128]
    C["c_ident"] = np.eye(128, dtype=np.float32)
    C["c_cumul_st"] = (i <= j).astype(np.float32)
    C["c_mask_lt_neg"] = np.where(i > j, -1.0, 0.0).astype(np.float32)
    C["c_mask_st_neg"] = np.where(i < j, -1.0, 0.0).astype(np.float32)
    C["c_mask_le"] = (i <= j).astype(np.float32)
    C["c_ones_col"] = np.ones((128, 1), np.float32)
    C["c_ones_row"] = np.ones((1, 128), np.float32)
    f64 = 1.0 / (10000.0 ** (np.arange(16) / 16.0))
    C["c_l0_freqs"] = np.tile(f64.astype(np.float32)[None, :], (128, T // 128))
    C["c_mask_st_neg16"] = np.tile(np.where(i < j, -1.0, 0.0).astype(np.float32), (1, 16))
    C["c_mask_le16"] = np.tile((i <= j).astype(np.float32), (1, 16))
    blk = ((i // 64) == (j // 64)).astype(np.float32)
    C["c_blkdiag16"] = np.tile(blk, (1, 16))
    C["c_identrep16"] = np.tile(np.eye(128, dtype=np.float32), (1, 16))
    DIM_HEAD, WINDOW = 64, 256
    half = np.arange(0, DIM_HEAD, 2, dtype=np.float64)
    inv_freq = 1.0 / 10000.0 ** (half / DIM_HEAD)
    t = np.arange(T, dtype=np.float64)
    freqs = t[:, None] * inv_freq[None, :]
    freqs = np.concatenate([freqs, freqs], -1)
    base_scale = (half + 0.4 * DIM_HEAD) / (1.4 * DIM_HEAD)
    power = (t - T // 2) / (WINDOW // 2)
    scale = np.concatenate([base_scale[None, :] ** power[:, None]] * 2, -1)
    cos, sin = np.cos(freqs), np.sin(freqs)
    sgn = np.ones((1, 64)); sgn[0, :32] = -1.0
    C["c_rotq_c"] = (cos * scale).astype(np.float32)
    C["c_rotq_s"] = (sin * scale * sgn).astype(np.float32)
    C["c_rotk_c"] = (cos / scale).astype(np.float32)
    C["c_rotk_s"] = (sin / scale * sgn).astype(np.float32)
    ii, jj = np.mgrid[0:128, 0:384]
    C["c_am_full"] = np.where((jj >= ii) & (jj <= ii + 256), 0.0, -1e30).astype(np.float32)
    ii, jj = np.mgrid[0:128, 0:256]
    C["c_am_t1"] = np.where(jj <= ii + 128, 0.0, -1e30).astype(np.float32)
    ii, jj = np.mgrid[0:128, 0:128]
    C["c_am_t0"] = np.where(jj <= ii, 0.0, -1e30).astype(np.float32)
    for nm in ("full", "t1", "t0"):
        arr = C["c_am_" + nm]
        for kc in range(arr.shape[1] // 128):
            C[f"c_amT_{nm}{kc}"] = np.ascontiguousarray(arr[:, kc * 128:(kc + 1) * 128].T) - 16.0
    return C


class CPool:
    def __init__(self, p, cs):
        mb = p.mb
        self.ident = p.sb("k_ident", [128, 128], F32)
        self.ident_bf = p.sb("k_ident_bf", [128, 128], BF16)
        self.cumul = p.sb("k_cumul", [128, 128], F32)
        self.mask_lt_neg = p.sb("k_mlt", [128, 128], F32)
        self.mask_st_neg = p.sb("k_mst", [128, 128], F32)
        self.mask_le = p.sb("k_mle", [128, 128], F32)
        self.ones_col = p.sb("k_ones_col", [128, 1], F32)
        self.ones_row = p.sb("k_ones_row", [1, 128], F32)
        self.ones_row_bf = p.sb("k_ones_row_bf", [1, 128], BF16)
        self.ones_col_bf = p.sb("k_ones_col_bf", [128, 1], BF16)
        self.l0_freqs = p.sb("k_l0f", [128, (p.T // 128) * 16], F32)
        for dst, src in [(self.ident, cs["c_ident"]), (self.cumul, cs["c_cumul_st"]),
                         (self.mask_lt_neg, cs["c_mask_lt_neg"]), (self.mask_st_neg, cs["c_mask_st_neg"]),
                         (self.mask_le, cs["c_mask_le"]), (self.ones_col, cs["c_ones_col"]),
                         (self.ones_row, cs["c_ones_row"]), (self.l0_freqs, cs["c_l0_freqs"])]:
            mb.dma(dst, src)
        mb.copy(self.ident_bf, self.ident)
        mb.copy(self.ones_row_bf, self.ones_row)
        mb.copy(self.ones_col_bf, self.ones_col)
        mb.cp_tmp = p.sb("k_tmp512", [128, 512], F32)
        self.kconst = p.sb("k_consts", [128, 2], F32)
        mb.memset(self.kconst[:, 0:1], 1.0)
        mb.memset(self.kconst[:, 1:2], float(np.pi / 2))
        p.nc.const_aps.register(F32, 1.0, self.kconst[:, 0:1])
        p.nc.const_aps.register(F32, float(np.pi / 2), self.kconst[:, 1:2])


def dma_slow(p, dst, src, eng="sp"):
    p.mb.op(eng, lambda e: e.dma_start(out=_ap(dst), in_=_ap(src), allow_slow_non_contiguous=True),
            reads=[src], writes=[dst], is_dma=True)


def wload_bf(p, alloc, wdram, Kdim, Odim, name, chunk=1024):
    mb = p.mb
    nk = ceil_div(Kdim, 128)
    pk = min(128, Kdim)
    wbf = alloc("w" + name, [pk, nk * Odim], BF16)
    chunk = min(Odim, chunk)
    stg = alloc("ws" + name, [pk, 2 * chunk], F32)
    idx = 0
    for k in range(nk):
        r0, r1 = k * 128, min((k + 1) * 128, Kdim)
        for o0 in range(0, Odim, chunk):
            o1 = min(o0 + chunk, Odim)
            sl = (idx % 2) * chunk
            idx += 1
            s = stg[0:r1 - r0, sl:sl + o1 - o0]
            mb.dma((s, f"g{sl}"), wdram[r0:r1, o0:o1])
            mb.copy(wbf[0:r1 - r0, k * Odim + o0:k * Odim + o1], (s, f"g{sl}"))
    return wbf


def bias_cols(p, alloc, vdram, n, name):
    nt = ceil_div(n, 128)
    out = alloc("bc" + name, [128, nt], F32)
    dma_slow(p, out[0:min(128, n), 0:nt], vdram.rearrange("(a b) -> b a", b=min(128, n)))
    return out


def bias_col64(p, alloc, vdram, n, name):
    out = alloc("b64" + name, [n, 1], F32)
    p.mb.dma(out, vdram.rearrange("(n o) -> n o", o=1))
    return out


def rep_row(p, alloc, vdram, n, name):
    row = alloc("rr" + name, [1, n], F32)
    p.mb.dma(row, vdram.rearrange("(o n) -> o n", o=1))
    out = alloc("rp" + name, [128, n], F32)
    for o0 in range(0, n, 512):
        o1 = min(o0 + 512, n)
        p.mb.mm(p.psC[:, 0:o1 - o0], p.cp.ones_row, row[:, o0:o1], start=True, stop=True)
        p.mb.copy(out[:, o0:o1], p.psC[:, 0:o1 - o0])
    return out


def emit_ln0(p, x_tm, xnfm, xrfm, xn_tm=None):
    mb = p.mb
    with p.stage() as alloc:
        xts = [alloc(f"x{j}", [128, DM]) for j in range(2)]
        bss = [alloc(f"bs{j}", [128, 6 * 2]) for j in range(2)]
        mvs = [alloc(f"mv{j}", [128, 2]) for j in range(2)]
        colss = [alloc(f"cols{j}", [128, 4]) for j in range(2)]
        xns = [alloc(f"xn{j}", [128, DM]) for j in range(2)]
        sq = alloc("sq", [128, DM])
        ss = alloc("ss", [128, 1])
        xnbs = [alloc(f"xnb{j}", [128, DM], BF16) for j in range(2)]
        xrbs = [alloc(f"xrb{j}", [128, DM], BF16) for j in range(2)]
        tps = [alloc(f"tp{j}", [128, 128], BF16) for j in range(4)]
        for t in range(p.NT):
            j = t % 2
            xt, bs, mv, cols, xn, xnb, xrb = xts[j], bss[j], mvs[j], colss[j], xns[j], xnbs[j], xrbs[j]
            mb.dma(xt, x_tm[t * 128:(t + 1) * 128, :])
            for s in range(2):
                mb.bn_stats(bs[:, 6 * s:6 * s + 6], xt[:, 512 * s:512 * (s + 1)])
            mb.bn_aggr(mv, bs)
            mb.ts(cols[:, 0:1], mv[:, 1:2], 1e-5, None, OP.add)
            mb.act(cols[:, 1:2], cols[:, 0:1], AF.Sqrt)
            mb.reciprocal(cols[:, 2:3], cols[:, 1:2])
            mb.ts(xn, xt, mv[:, 0:1], cols[:, 2:3], OP.subtract, OP.mult)
            if xn_tm is not None:
                mb.dma(xn_tm[t * 128:(t + 1) * 128, :], xn, eng="act")
            mb.act(sq, xn, AF.Square, accum_out=ss)
            mb.ts(cols[:, 0:1], ss, 1.0 / DM, 1e-6, OP.mult, OP.add)
            mb.act(cols[:, 1:2], cols[:, 0:1], AF.Sqrt)
            mb.reciprocal(cols[:, 3:4], cols[:, 1:2])
            mb.copy(xnb, xn)
            mb.ts(xrb, xn, cols[:, 3:4], None, OP.mult)
            for c in range(DM // 128):
                for si, (src, dstd) in enumerate(((xnb, xnfm), (xrb, xrfm))):
                    sl = (c % 2) * 2 + si
                    ps = p.psT[:, sl * 128:(sl + 1) * 128]
                    mb.transpose((ps, f"t{sl}"), src[:, c * 128:(c + 1) * 128], p.cp.ident_bf)
                    mb.copy(tps[sl], (ps, f"t{sl}"))
                    mb.dma(dstd[c * 128:(c + 1) * 128, t * 128:(t + 1) * 128], tps[sl], eng="pool")


def emit_l0_ffn(p, w, xnfm, f0_tm, spike_bf, spike2_bf):
    mb, T, NT = p.mb, p.T, p.NT
    H = DI
    with p.stage() as alloc:
        encw = wload_bf(p, alloc, w["enc_W"], DM, H, "encW")
        encb = bias_cols(p, alloc, w["enc_b"], H, "encb")
        xs = alloc("x", [128, 8 * 512], BF16)
        ot = alloc("o", [128, 512], BF16)
        ot2 = alloc("o2", [128, 512], BF16)
        for s in range(0, T, 512):
            for k in range(8):
                mb.dma(xs[:, k * 512:(k + 1) * 512], xnfm[k * 128:(k + 1) * 128, s:s + 512])
            for o in range(16):
                pse = p.psA if o % 2 == 0 else p.psB
                for k in range(8):
                    mb.mm(pse, encw[:, k * H + o * 128:k * H + o * 128 + 128],
                          xs[:, k * 512:(k + 1) * 512], start=(k == 0), stop=(k == 7))
                mb.act(ot if o % 2 == 0 else ot2, pse, AF.Sigmoid, bias=encb[:, o:o + 1])
                mb.dma(spike_bf[o * 128:(o + 1) * 128, s:s + 512], ot if o % 2 == 0 else ot2, eng="pool")
    p.cont_bf = p.salloc("l0f_cont", [64, T], BF16)
    p.gatesw = p.salloc("l0f_gatesw", [128, p.NT * 8], F32)
    with p.stage() as alloc:
        s2cw = wload_bf(p, alloc, w["s2c_W"], H, 64, "s2cW")
        s2cb = bias_col64(p, alloc, w["s2c_b"], 64, "s2cb")
        xs = alloc("x", [128, 16 * 512], BF16)
        cont = alloc("cont", [64, 512])
        for s in range(0, T, 512):
            for k in range(16):
                mb.dma(xs[:, k * 512:(k + 1) * 512], spike_bf[k * 128:(k + 1) * 128, s:s + 512])
            for k in range(16):
                mb.mm(p.psA[0:64, :], s2cw[:, k * 64:(k + 1) * 64],
                      xs[:, k * 512:(k + 1) * 512], start=(k == 0), stop=(k == 15))
            mb.ts(cont, p.psA[0:64, :], s2cb, None, OP.add)
            mb.copy(p.cont_bf[:, s:s + 512], cont)
        rw1 = wload_bf(p, alloc, w["rW1"], 64, 64, "rW1")
        rw2 = wload_bf(p, alloc, w["rW2"], 64, 8, "rW2")
        rb1 = bias_col64(p, alloc, w["rb1"], 64, "rb1")
        rb2rep = rep_row(p, alloc, w["rb2"], 8, "rb2")
        th_bf = alloc("th", [64, T], BF16)
        for s in range(0, T, 512):
            mb.mm(p.psA[0:64, :], rw1, p.cont_bf[:, s:s + 512], start=True, stop=True)
            mb.act(th_bf[:, s:s + 512], p.psA[0:64, :], AF.Tanh, bias=rb1)
        logits = alloc("logits", [128, NT * 8])
        for t in range(NT):
            mb.mm(p.psB[:, 0:8], th_bf[:, t * 128:(t + 1) * 128], rw2, start=True, stop=True)
            mb.tt(logits[:, t * 8:(t + 1) * 8], p.psB[:, 0:8], rb2rep[:, 0:8], OP.add)
        E8 = 8
        lgv = logits.rearrange("p (t e) -> p t e", e=E8)
        mx = alloc("mx", [128, NT]); sums = alloc("sums", [128, NT])
        sm = alloc("sm", [128, NT * 8]); smv = sm.rearrange("p (t e) -> p t e", e=E8)
        m1 = alloc("m1", [128, NT]); m2 = alloc("m2", [128, NT])
        ism = alloc("ism", [128, NT * 8]); ismv = ism.rearrange("p (t e) -> p t e", e=E8)
        cum = alloc("cum", [128, NT * 8]); cumv = cum.rearrange("p (t e) -> p t e", e=E8)
        first = alloc("first", [128, NT * 8])
        tmp = alloc("tmp", [128, NT * 8]); tmpv = tmp.rearrange("p (t e) -> p t e", e=E8)
        sec = alloc("sec", [128, NT * 8])
        den = alloc("den", [128, NT])
        mb.reduce(mx, lgv, OP.max)
        for t in range(NT):
            mb.ts(sm[:, t * 8:(t + 1) * 8], logits[:, t * 8:(t + 1) * 8], mx[:, t:t + 1], None, OP.subtract)
        mb.act(sm, sm, AF.Exp)
        mb.reduce(m1, smv, OP.max)
        for t in range(NT):
            mb.ts(ism[:, t * 8:(t + 1) * 8], sm[:, t * 8:(t + 1) * 8], m1[:, t:t + 1], None, OP.is_ge)
        mb.memset(cum, 0.0)
        for e in range(1, E8):
            mb.tt(cumv[:, :, e:e + 1], cumv[:, :, e - 1:e], ismv[:, :, e - 1:e], OP.add)
        mb.ts(tmp, cum, 0.5, None, OP.is_lt)
        mb.tt(first, ism, tmp, OP.mult)
        mb.ts(tmp, first, -1e30, None, OP.mult)
        mb.tt(tmp, sm, tmp, OP.add)
        mb.reduce(m2, tmpv, OP.max)
        for t in range(NT):
            mb.ts(ism[:, t * 8:(t + 1) * 8], tmp[:, t * 8:(t + 1) * 8], m2[:, t:t + 1], None, OP.is_ge)
        mb.memset(cum, 0.0)
        for e in range(1, E8):
            mb.tt(cumv[:, :, e:e + 1], cumv[:, :, e - 1:e], ismv[:, :, e - 1:e], OP.add)
        mb.ts(cum, cum, 0.5, None, OP.is_lt)
        mb.tt(sec, ism, cum, OP.mult)
        mb.tt(den, m1, m2, OP.add)
        mb.reciprocal(den, den)
        mb.tt(tmp, first, sec, OP.add)
        mb.tt(tmp, tmp, sm, OP.mult)
        for t in range(NT):
            mb.ts(p.gatesw[:, t * 8:(t + 1) * 8], tmp[:, t * 8:(t + 1) * 8], den[:, t:t + 1], None, OP.mult)
    p.acc_tm = p.salloc("l0f_acc", [128, NT * 64], F32)
    mb.memset(p.acc_tm, 0.0)
    with p.stage() as alloc:
        h = alloc("h", [128, 8 * 512], BF16)
        eo = alloc("eo", [64, 512])
        for e in range(8):
            we1 = wload_bf(p, alloc, w["We1"][e], 64, 1024, f"We1_{e}")
            we2 = wload_bf(p, alloc, w["We2"][e], 1024, 64, f"We2_{e}")
            be1 = bias_cols(p, alloc, w["be1"][e], 1024, f"be1_{e}")
            be2 = bias_col64(p, alloc, w["be2"][e], 64, f"be2_{e}")
            for s in range(0, T, 512):
                for o in range(8):
                    mb.mm(p.psA, we1[:, o * 128:(o + 1) * 128], p.cont_bf[:, s:s + 512],
                          start=True, stop=True)
                    mb.act(h[:, o * 512:(o + 1) * 512], p.psA, AF.Relu, bias=be1[:, o:o + 1])
                for k in range(8):
                    mb.mm(p.psB[0:64, :], we2[:, k * 64:(k + 1) * 64], h[:, k * 512:(k + 1) * 512],
                          start=(k == 0), stop=(k == 7))
                mb.ts(eo, p.psB[0:64, :], be2, None, OP.add)
                for q in range(4):
                    ti = s // 128 + q
                    mb.transpose(p.psC[:, 0:64], eo[:, q * 128:(q + 1) * 128], p.cp.ident[0:64, 0:64])
                    mb.stt(p.acc_tm[:, ti * 64:(ti + 1) * 64], p.psC[:, 0:64],
                           p.gatesw[:, ti * 8 + e:ti * 8 + e + 1],
                           p.acc_tm[:, ti * 64:(ti + 1) * 64], OP.mult, OP.add)
    with p.stage() as alloc:
        accfm = alloc("accfm", [64, T], BF16)
        for t in range(NT):
            mb.transpose(p.psA[0:64, 0:128], p.acc_tm[:, t * 64:(t + 1) * 64], p.cp.ident)
            mb.copy(accfm[:, t * 128:(t + 1) * 128], p.psA[0:64, 0:128])
        c2sw = wload_bf(p, alloc, w["c2s_W"], 64, H, "c2sW")
        c2sb = bias_cols(p, alloc, w["c2s_b"], H, "c2sb")
        ot = alloc("o", [128, 512], BF16)
        for s in range(0, T, 512):
            for o in range(16):
                mb.mm(p.psA, c2sw[:, o * 128:(o + 1) * 128], accfm[:, s:s + 512],
                      start=True, stop=True)
                mb.act(ot, p.psA, AF.Sigmoid, bias=c2sb[:, o:o + 1])
                mb.dma(spike2_bf[o * 128:(o + 1) * 128, s:s + 512], ot, eng="pool")
    with p.stage() as alloc:
        decw = wload_bf(p, alloc, w["dec_W"], H, DM, "decW")
        decb = bias_cols(p, alloc, w["dec_b"], DM, "decb")
        lng = rep_row(p, alloc, w["ln_g"], DM, "lng")
        lnb = rep_row(p, alloc, w["ln_b"], DM, "lnb")
        xs = alloc("x", [128, 16 * 512], BF16)
        df = alloc("dec", [128, 8 * 512])
        dtm = alloc("dtm", [128, DM])
        st = alloc("st", [128, 12])
        st2 = alloc("st2", [128, 8])
        for s in range(0, T, 512):
            for k in range(16):
                mb.dma(xs[:, k * 512:(k + 1) * 512], spike2_bf[k * 128:(k + 1) * 128, s:s + 512])
            for o in range(8):
                psd = p.psA if o % 2 == 0 else p.psB
                for k in range(16):
                    mb.mm(psd, decw[:, k * DM + o * 128:k * DM + o * 128 + 128],
                          xs[:, k * 512:(k + 1) * 512], start=(k == 0), stop=(k == 15))
                mb.act(df[:, o * 512:(o + 1) * 512], psd, AF.Sigmoid, bias=decb[:, o:o + 1])
            for q in range(4):
                for c in range(8):
                    mb.transpose(p.psC[:, 0:128], df[:, c * 512 + q * 128:c * 512 + (q + 1) * 128], p.cp.ident)
                    mb.copy(dtm[:, c * 128:(c + 1) * 128], p.psC[:, 0:128])
                for g in range(2):
                    mb.bn_stats(st[:, 6 * g:6 * g + 6], dtm[:, 512 * g:512 * (g + 1)])
                mb.bn_aggr(st2[:, 0:2], st)
                mb.ts(st2[:, 2:3], st2[:, 1:2], 1e-5, None, OP.add)
                mb.act(st2[:, 3:4], st2[:, 2:3], AF.Sqrt)
                mb.reciprocal(st2[:, 4:5], st2[:, 3:4])
                mb.ts(dtm, dtm, st2[:, 0:1], st2[:, 4:5], OP.subtract, OP.mult)
                mb.tt(dtm, dtm, lng, OP.mult)
                mb.tt(dtm, dtm, lnb, OP.add)
                mb.dma(f0_tm[(s // 128 + q) * 128:(s // 128 + q + 1) * 128, :], dtm, eng="pool")


def emit_l1_ffn_layer(p, w, i, xin_tm, xout_tm, cs, parts=("attn", "ffn")):
    mb, T, NT = p.mb, p.T, p.NT
    HDm, INNER = 64, 1024
    DIN2, DIN = 5460, 2730
    QKS = 8.0

    def fm_tile(alloc, with_f32=True):
        return (alloc("xfmt", [128, 8 * 129], BF16),
                alloc("xfmtf", [128, 8 * 129]) if with_f32 else None)

    def load_fm_tile(xfmt2, xt, first_tile):
        xb, xf = xfmt2
        for buf in (xb, xf):
            if buf is None:
                continue
            for c in range(8):
                if first_tile:
                    mb.memset(buf[:, c * 129:c * 129 + 1], 0.0)
                else:
                    mb.copy(buf[:, c * 129:c * 129 + 1], buf[:, c * 129 + 128:c * 129 + 129])
        for c in range(8):
            mb.transpose(p.psA[:, 0:128], xt[:, c * 128:(c + 1) * 128], p.cp.ident)
            mb.copy(xb[:, c * 129 + 1:c * 129 + 129], p.psA[:, 0:128], eng="act")
            if xf is not None:
                mb.copy(xf[:, c * 129 + 1:c * 129 + 129], p.psA[:, 0:128], eng="act")

    def stt_tile(xfmt2, c, t, f32=False):
        off = 1 if c < 4 else 0
        buf = xfmt2[1] if f32 else xfmt2[0]
        return buf[:, c * 129 + off:c * 129 + off + 128]

    with p.stage() as alloc:
        if "attn" not in parts:
            raise StopStage
        wqkv = wload_bf(p, alloc, w["Wqkv"][i], DM, 3 * INNER, "wqkv", chunk=512)
        wo = wload_bf(p, alloc, w["Wo"][i], INNER, DM, "wo", chunk=512)
        qs_rep = rep_row(p, alloc, w["q_scale"][i], HDm, "qs")
        ks_rep = rep_row(p, alloc, w["k_scale"][i], HDm, "ks")
        qs_sw = alloc("qssw", [128, HDm]); ks_sw = alloc("kssw", [128, HDm])
        for hh in range(2):
            mb.copy(qs_sw[:, hh * 32:(hh + 1) * 32], qs_rep[:, (1 - hh) * 32:(2 - hh) * 32])
            mb.copy(ks_sw[:, hh * 32:(hh + 1) * 32], ks_rep[:, (1 - hh) * 32:(2 - hh) * 32])
        rtab = {nm: alloc(nm, [128, NT * 64], BF16) for nm in ("qc", "qs", "kc", "ks")}
        tf = alloc("tf", [128, 4 * 64])
        for t in range(NT):
            for ni, (nm, cst, sc) in enumerate((("qc", "c_rotq_c", qs_rep), ("qs", "c_rotq_s", qs_sw),
                                                ("kc", "c_rotk_c", ks_rep), ("ks", "c_rotk_s", ks_sw))):
                tfs = tf[:, ni * 64:(ni + 1) * 64]
                mb.dma((tfs, f"f{ni}"), cs[cst][t * 128:(t + 1) * 128, :])
                mb.tt(rtab[nm][:, t * 64:(t + 1) * 64], (tfs, f"f{ni}"), sc, OP.mult)
        amT = {}
        for nm, nch_ in (("full", 3), ("t1", 2), ("t0", 1)):
            for kc in range(nch_):
                amT[(nm, kc)] = alloc(f"amT{nm}{kc}", [128, 128])
                mb.dma(amT[(nm, kc)], cs[f"c_amT_{nm}{kc}"])
        KR = 5
        xts = [alloc(f"xt{j}", [128, DM]) for j in range(3)]
        xfmts = [alloc(f"xfmt{j}", [128, 8 * 129], BF16) for j in range(2)]
        kring = alloc("kring", [128, 8 * KR * 128], BF16)
        vring = alloc("vring", [128, KR * 16 * 65], BF16)
        mb.memset(kring, 0.0)
        mb.memset(vring, 1.0)
        qkv = alloc("qkv", [128, 3 * INNER])
        swp = alloc("swp", [128, INNER], BF16)
        tmpq = alloc("tmpq", [128, INNER], BF16)
        sumsq = alloc("sumsq", [128, 16])
        rcol = alloc("rcol", [128, 16])
        qn = alloc("qn", [128, INNER], BF16)
        q2 = alloc("q2", [128, INNER])
        k2 = alloc("k2", [128, INNER])
        qfms = [alloc(f"qfm{j}", [128, INNER], BF16) for j in range(2)]
        PT_f = alloc("PTf", [128, 2 * 384])
        PT_bf = alloc("PTbf", [128, 2 * 384], BF16)
        rcps = alloc("rcps", [128, 4])
        ofmt = alloc("ofmt", [128, INNER])
        ofm = alloc("ofm", [128, INNER], BF16)
        otm = alloc("otm", [128, DM])

        def l2n(srcsl, dst):
            mb.act(swp, srcsl, AF.Square)
            mb.reduce(sumsq, swp.rearrange("p (h d) -> p h d", d=64), OP.add)
            mb.act(sumsq, sumsq, AF.Sqrt)
            mb.ts(sumsq, sumsq, 1e-12, None, OP.max)
            mb.reciprocal(rcol, sumsq)
            for h in range(16):
                mb.ts(dst[:, h * 64:(h + 1) * 64], srcsl[:, h * 64:(h + 1) * 64], rcol[:, h:h + 1], None, OP.mult)

        def rot(dst, src, ct, st_, t):
            sv = swp.rearrange("p (h u d) -> p h u d", u=2, d=32)
            qv = src.rearrange("p (h u d) -> p h u d", u=2, d=32)
            mb.copy(sv, qv[:, :, ::-1, :])
            cb = ct[:, t * 64:(t + 1) * 64].unsqueeze(1).broadcast_to([128, 16, 64])
            sb = st_[:, t * 64:(t + 1) * 64].unsqueeze(1).broadcast_to([128, 16, 64])
            mb.tt(tmpq.rearrange("p (h d) -> p h d", d=64), src.rearrange("p (h d) -> p h d", d=64), cb, OP.mult)
            mb.tt(swp.rearrange("p (h d) -> p h d", d=64), swp.rearrange("p (h d) -> p h d", d=64), sb, OP.mult)
            mb.tt(dst, tmpq, swp, OP.add)

        def psx(b, n=128):
            return (p.psE[:, b * 512:b * 512 + n], f"e{b}")

        def front(t):
            j2 = t % 2
            xt, xfmt = xts[t % 3], xfmts[j2]
            mb.dma(xt, (xin_tm[t * 128:(t + 1) * 128, :], f"d{t}"))
            for c in range(8):
                if t == 0:
                    mb.memset(xfmt[:, c * 129:c * 129 + 1], 0.0)
                else:
                    mb.copy(xfmt[:, c * 129:c * 129 + 1], xfmts[1 - j2][:, c * 129 + 128:c * 129 + 129])
            for c in range(8):
                pb = psx(c % 3)
                mb.transpose(pb, xt[:, c * 128:(c + 1) * 128], p.cp.ident)
                mb.copy(xfmt[:, c * 129 + 1:c * 129 + 129], pb, eng="act")
            for ch in range(6):
                psq = p.psB if ch % 2 == 0 else p.psA
                for c in range(8):
                    o = 1 if c < 4 else 0
                    mb.mm(psq, xfmt[:, c * 129 + o:c * 129 + o + 128],
                          wqkv[:, c * 3 * INNER + ch * 512:c * 3 * INNER + (ch + 1) * 512],
                          start=(c == 0), stop=(c == 7))
                mb.copy(qkv[:, ch * 512:(ch + 1) * 512], psq, eng="act")
            l2n(qkv[:, 0:INNER], qn)
            rot(q2, qn, rtab["qc"], rtab["qs"], t)
            l2n(qkv[:, INNER:2 * INNER], qn)
            rot(k2, qn, rtab["kc"], rtab["ks"], t)
            vs = t % KR
            dstv = vring[:, vs * 1040:(vs + 1) * 1040].rearrange("p (h e) -> p h e", e=65)[:, :, 0:64]
            mb.copy((dstv, f"v{vs}"), qkv[:, 2 * INNER:3 * INNER].rearrange("p (h d) -> p h d", d=64))

        def mid(t):
            qfm = qfms[t % 2]
            ks_ = t % KR
            for c in range(8):
                pb = psx((2 * c) % 3)
                mb.transpose(pb, k2[:, c * 128:(c + 1) * 128], p.cp.ident)
                mb.copy((kring[:, (c * KR + ks_) * 128:(c * KR + ks_) * 128 + 128], f"s{ks_}"), pb, eng="act")
                pb = psx((2 * c + 1) % 3)
                mb.transpose(pb, q2[:, c * 128:(c + 1) * 128], p.cp.ident)
                mb.copy(qfm[:, c * 128:(c + 1) * 128], pb, eng="act")

        def back(t):
            j2 = t % 2
            xt, qfm = xts[t % 3], qfms[j2]
            if t == 0:
                jlist = [(0, ("t0", 0))]
            elif t == 1:
                jlist = [(0, ("t1", 0)), (1, ("t1", 1))]
            else:
                jlist = [(t - 2, ("full", 0)), (t - 1, ("full", 1)), (t, ("full", 2))]
            nch = len(jlist)
            pvb = [(p.psE[:, 3 * 512:3 * 512 + 65], "e3"), (p.psC[:, 0:65], "c0")]

            def scT(h):
                c, half = h // 2, (h % 2) * 64
                hs = (h % 2) * 384
                for kc, (jt, mk) in enumerate(jlist):
                    ks_ = jt % KR
                    pb = psx((h + kc) % 3)
                    mb.mm(pb, (kring[half:half + 64, (c * KR + ks_) * 128:(c * KR + ks_) * 128 + 128], f"s{ks_}"),
                          qfm[half:half + 64, c * 128:c * 128 + 128], start=True, stop=True)
                    mb.stt((PT_f[:, hs + kc * 128:hs + (kc + 1) * 128], f"pf{h % 2}"),
                           pb, QKS, amT[mk], OP.mult, OP.add)
                mb.act((PT_bf[:, hs:hs + nch * 128], f"pb{h % 2}"),
                       (PT_f[:, hs:hs + nch * 128], f"pf{h % 2}"), AF.Exp)

            def pv(h):
                hs = (h % 2) * 384
                pb = pvb[h % 2]
                for kc, (jt, mk) in enumerate(jlist):
                    vs = jt % KR
                    mb.mm(pb, (PT_bf[:, hs + kc * 128:hs + (kc + 1) * 128], f"pb{h % 2}"),
                          (vring[:, vs * 1040 + h * 65:vs * 1040 + (h + 1) * 65], f"v{vs}"),
                          start=(kc == 0), stop=(kc == nch - 1))
                mb.copy((rcps[:, h % 2:h % 2 + 1], f"rc{h % 2}"), (pb[0][:, 64:65], pb[1]))
                mb.reciprocal((rcps[:, 2 + h % 2:3 + h % 2], f"rq{h % 2}"),
                              (rcps[:, h % 2:h % 2 + 1], f"rc{h % 2}"))
                mb.ts((ofmt[:, h * 64:(h + 1) * 64], f"o{h % 2}"), (pb[0][:, 0:64], pb[1]),
                      (rcps[:, 2 + h % 2:3 + h % 2], f"rq{h % 2}"), None, OP.mult)

            scT(0)
            for h in range(1, 16):
                scT(h)
                pv(h - 1)
            pv(15)
            for oc in range(8):
                pb = psx(oc % 3)
                mb.transpose(pb, ofmt[:, oc * 128:(oc + 1) * 128], p.cp.ident)
                mb.copy(ofm[:, oc * 128:(oc + 1) * 128], pb, eng="act")
            for oc in range(2):
                pso = p.psA if oc == 0 else p.psB
                for c in range(8):
                    mb.mm(pso, ofm[:, c * 128:(c + 1) * 128],
                          wo[:, c * DM + oc * 512:c * DM + oc * 512 + 512],
                          start=(c == 0), stop=(c == 7))
                mb.copy(otm[:, oc * 512:(oc + 1) * 512], pso)
            mb.tt(otm, otm, xt, OP.add)
            mb.dma((xout_tm[t * 128:(t + 1) * 128, :], f"d{t}"), otm, eng="pool")

        front(0)
        mid(0)
        front(1)
        for t in range(NT):
            back(t)
            if t + 1 < NT:
                mid(t + 1)
            if t + 2 < NT:
                front(t + 2)
    with p.stage() as alloc:
        if "ffn" not in parts:
            raise StopStage
        NJ = ceil_div(DIN, 128)
        w1 = wload_bf(p, alloc, w["W1"][i], DM, DIN2, "w1")
        lngc = bias_cols(p, alloc, w["ln_g"][i], DM, "lng")
        for k in range(8):
            mb.ts(w1[:, k * DIN2:(k + 1) * DIN2], w1[:, k * DIN2:(k + 1) * DIN2], lngc[:, k:k + 1], None, OP.mult)
        w2 = wload_bf(p, alloc, w["W2"][i], DIN, DM, "w2")
        lnbc = bias_cols(p, alloc, w["ln_b"][i], DM, "lnb")
        lnbc_bf = alloc("lnbcb", [128, 8], BF16)
        mb.copy(lnbc_bf, lnbc)
        rtmp = alloc("rtmp", [1, 512])
        btmp = alloc("btmp", [1, 512])
        brow = alloc("brow", [1, 512])
        b2row = alloc("b2row", [1, DM], BF16)
        for o0 in (0, 512):
            mb.dma(rtmp, w["b2"][i].rearrange("(o n) -> o n", o=1)[:, o0:o0 + 512])
            mb.copy(b2row[:, o0:o0 + 512], rtmp)
        crow = alloc("crow", [1, DIN2], BF16)
        b1cols = alloc("b1cols", [128, 2 * NJ])
        beffd = p.dscratch(f"beff{i}", [DIN2])
        for o0 in range(0, DIN2, 512):
            o1 = min(o0 + 512, DIN2)
            n = o1 - o0
            for c in range(8):
                mb.mm((p.psA[0:1, 0:n], "cb"), p.cp.ones_col_bf, w1[:, c * DIN2 + o0:c * DIN2 + o1],
                      start=(c == 0), stop=(c == 7))
            mb.ts(crow[:, o0:o1], (p.psA[0:1, 0:n], "cb"), -1.0, None, OP.mult)
            for c in range(8):
                mb.mm((p.psB[0:1, 0:n], "bb"), lnbc_bf[:, c:c + 1], w1[:, c * DIN2 + o0:c * DIN2 + o1],
                      start=(c == 0), stop=(c == 7))
            mb.dma(btmp[:, 0:n], w["b1"][i].rearrange("(o n) -> o n", o=1)[:, o0:o1])
            mb.tt(brow[:, 0:n], (p.psB[0:1, 0:n], "bb"), btmp[:, 0:n], OP.add)
            mb.dma(beffd.rearrange("(o n) -> o n", o=1)[:, o0:o1], brow[:, 0:n])
        bd2 = beffd.rearrange("(n o) -> n o", o=1)
        for half, q0 in ((0, 0), (DIN, NJ)):
            for jj in range(NJ):
                kw = min(128, DIN - jj * 128)
                mb.dma(b1cols[0:kw, q0 + jj:q0 + jj + 1], bd2[half + jj * 128:half + jj * 128 + kw, :])
        xraw = alloc("xraw", [128, 8 * 257], BF16)
        xt2 = [alloc(f"xt2{u}", [128, DM]) for u in range(2)]
        outb = [alloc(f"outb{u}", [128, DM]) for u in range(2)]
        sqb = [alloc(f"sqb{u}", [128, 256], BF16) for u in range(2)]
        tub = [alloc(f"tub{u}", [128, 256]) for u in range(2)]
        tgb = [alloc(f"tgb{u}", [128, 256]) for u in range(2)]
        ggb = [alloc(f"ggb{u}", [128, 256]) for u in range(2)]
        hbb = [alloc(f"hbb{u}", [128, 256], BF16) for u in range(3)]
        rstdb = alloc("rstdb", [128, 256])
        murow = alloc("murow", [1, 256], BF16)
        qrow = alloc("qrow", [1, 256])
        vrow = alloc("vrow", [1, 256])
        rrow = alloc("rrow", [1, 256])
        for c in range(8):
            mb.memset(xraw[:, c * 257:c * 257 + 1], 0.0)

        def xv(c):
            off = 1 if c < 4 else 0
            return xraw[:, c * 257 + off:c * 257 + off + 256]

        psH = [p.psA, p.psB, p.psC]
        for s in range(T // 256):
            if s > 0:
                for c in range(8):
                    mb.copy(xraw[:, c * 257:c * 257 + 1], xraw[:, c * 257 + 256:c * 257 + 257], eng="pool")
            for u in range(2):
                mb.dma(xt2[u], xout_tm[(s * 2 + u) * 128:(s * 2 + u + 1) * 128, :])
                for c in range(8):
                    ph = psH[(u * 8 + c) % 3]
                    mb.transpose((ph[:, 0:128], "xT"), xt2[u][:, c * 128:(c + 1) * 128], p.cp.ident)
                    mb.copy(xraw[:, c * 257 + 1 + u * 128:c * 257 + 129 + u * 128], (ph[:, 0:128], "xT"),
                            eng="act")
            for c in range(8):
                mb.tt(sqb[c % 2], xv(c), xv(c), OP.mult)
                mb.mm((p.psA[0:1, 0:256], "srow"), p.cp.ones_col_bf, xv(c), start=(c == 0), stop=(c == 7))
                mb.mm((p.psB[0:1, 0:256], "qrow"), p.cp.ones_col_bf, sqb[c % 2], start=(c == 0), stop=(c == 7))
            mb.ts(murow, (p.psA[0:1, 0:256], "srow"), 1.0 / DM, None, OP.mult)
            mb.ts(qrow, (p.psB[0:1, 0:256], "qrow"), 1.0 / DM, None, OP.mult)
            mb.ts(vrow, (p.psA[0:1, 0:256], "srow"), 1.0 / DM, None, OP.mult)
            mb.tt(vrow, vrow, vrow, OP.mult)
            mb.tt(vrow, qrow, vrow, OP.subtract)
            mb.ts(vrow, vrow, 1e-5, None, OP.add)
            mb.act(rrow, vrow, AF.Sqrt)
            mb.reciprocal(rrow, rrow)
            mb.mm((p.psC[:, 0:256], "rb"), p.cp.ones_row, rrow, start=True, stop=True)
            mb.copy(rstdb, (p.psC[:, 0:256], "rb"), eng="act")
            for jj in range(NJ + 1):
                if jj < NJ:
                    kw = min(128, DIN - jj * 128)
                    ph = psH[jj % 3]
                    for half, pc0 in ((0, 0), (DIN, 256)):
                        for c in range(8):
                            mb.mm((ph[0:kw, pc0:pc0 + 256], "h"),
                                  w1[:, c * DIN2 + half + jj * 128:c * DIN2 + half + jj * 128 + kw],
                                  xv(c), start=(c == 0), stop=False)
                        mb.mm((ph[0:kw, pc0:pc0 + 256], "h"),
                              crow[:, half + jj * 128:half + jj * 128 + kw], murow, start=False, stop=True)
                    u2, g2, e2, hbx = tub[jj % 2], tgb[jj % 2], ggb[jj % 2], hbb[jj % 3]
                    mb.tt(u2[0:kw, :], (ph[0:kw, 0:256], "h"), rstdb[0:kw, :], OP.mult)
                    mb.tt(g2[0:kw, :], (ph[0:kw, 256:512], "h"), rstdb[0:kw, :], OP.mult)
                    mb.act(e2[0:kw, :], g2[0:kw, :], AF.Gelu, bias=b1cols[0:kw, NJ + jj:NJ + jj + 1])
                    mb.stt(hbx[0:kw, :], u2[0:kw, :], b1cols[0:kw, jj:jj + 1], e2[0:kw, :], OP.add, OP.mult)
                jjp = jj - 1
                if jjp >= 0:
                    kwp = min(128, DIN - jjp * 128)
                    hbp = hbb[jjp % 3]
                    for ts_ in range(2):
                        for oc in range(2):
                            bnk = ts_ * 2 + oc
                            mb.mm((p.psE[:, bnk * 512:(bnk + 1) * 512], f"o{bnk}"),
                                  hbp[0:kwp, ts_ * 128:(ts_ + 1) * 128],
                                  w2[0:kwp, jjp * DM + oc * 512:jjp * DM + oc * 512 + 512],
                                  start=(jjp == 0), stop=False)
            for ts_ in range(2):
                for oc in range(2):
                    bnk = ts_ * 2 + oc
                    mb.mm((p.psE[:, bnk * 512:(bnk + 1) * 512], f"o{bnk}"),
                          p.cp.ones_row_bf, b2row[:, oc * 512:(oc + 1) * 512], start=False, stop=True)
            for ts_ in range(2):
                for oc in range(2):
                    bnk = ts_ * 2 + oc
                    mb.tt(outb[ts_][:, oc * 512:(oc + 1) * 512], (p.psE[:, bnk * 512:(bnk + 1) * 512], f"o{bnk}"),
                          xt2[ts_][:, oc * 512:(oc + 1) * 512], OP.add)
                mb.dma(xout_tm[(s * 2 + ts_) * 128:(s * 2 + ts_ + 1) * 128, :], outb[ts_], eng="pool")


def emit_l1_attn(p, w, xnfm, a1_tm, o_bf, cs):
    mb, T, NT = p.mb, p.T, p.NT
    CQ = 128
    with p.stage() as alloc:
        wq = wload_bf(p, alloc, w["Wq"], DM, CQ, "wq")
        wk = wload_bf(p, alloc, w["Wk"], DM, CQ, "wk")
        wv = wload_bf(p, alloc, w["Wv"], DM, CQ, "wv")
        wg = wload_bf(p, alloc, w["Wg"], DM, CQ, "wg")
        wab = alloc("wab", [128, 16], BF16)
        stg = alloc("stgw", [128, 2])
        for k in range(8):
            mb.dma(stg[:, 0:1], w["Wa"][k * 128:(k + 1) * 128, :])
            mb.dma(stg[:, 1:2], w["Wb"][k * 128:(k + 1) * 128, :])
            mb.copy(wab[:, 2 * k:2 * k + 2], stg)
        brow = alloc("brow", [1, 2 * CQ + 2])
        mb.dma(brow[:, 0:CQ], w["bq"].rearrange("(o n) -> o n", o=1))
        mb.dma(brow[:, CQ:2 * CQ], w["bk"].rearrange("(o n) -> o n", o=1))
        mb.dma(brow[:, 2 * CQ:2 * CQ + 1], w["ba"].rearrange("(o n) -> o n", o=1))
        mb.dma(brow[:, 2 * CQ + 1:2 * CQ + 2], w["bb"].rearrange("(o n) -> o n", o=1))
        bq_rep = alloc("bqrep", [128, CQ]); bk_rep = alloc("bkrep", [128, CQ])
        bab_rep = alloc("babrep", [128, 2])
        mb.mm(p.psC[:, 0:CQ], p.cp.ones_row, brow[:, 0:CQ], start=True, stop=True)
        mb.copy(bq_rep, p.psC[:, 0:CQ])
        mb.mm(p.psC[:, 0:CQ], p.cp.ones_row, brow[:, CQ:2 * CQ], start=True, stop=True)
        mb.copy(bk_rep, p.psC[:, 0:CQ])
        mb.mm(p.psC[:, 0:2], p.cp.ones_row, brow[:, 2 * CQ:2 * CQ + 2], start=True, stop=True)
        mb.copy(bab_rep, p.psC[:, 0:2])
        bv_rep = rep_row(p, alloc, w["bv"], CQ, "bv")
        bg_rep = rep_row(p, alloc, w["bg"], CQ, "bg")
        qtm = alloc("qtm", [128, NT * CQ])
        ktm = alloc("ktm", [128, NT * CQ], BF16)
        vtm = alloc("vtm", [128, NT * CQ])
        gtm = alloc("gtm", [128, NT * CQ])
        abtm = alloc("abtm", [128, NT * 2])
        xfms = [alloc(f"xfm{j}", [128, 1024], BF16) for j in range(2)]
        tqs = [alloc(f"tq{j}", [128, CQ]) for j in range(2)]
        colss = [alloc(f"cols{j}", [128, 4]) for j in range(2)]
        for t in range(NT):
            xfm, tq, cols = xfms[t % 2], tqs[t % 2], colss[t % 2]
            for k in range(8):
                mb.dma(xfm[:, k * 128:(k + 1) * 128], xnfm[k * 128:(k + 1) * 128, t * 128:(t + 1) * 128])
            for k in range(8):
                for dst, wt, nc_ in ((p.psA, wq, CQ), (p.psB, wk, CQ), (p.psC, wv, CQ)):
                    mb.mm(dst[:, 0:nc_], xfm[:, k * 128:(k + 1) * 128], wt[:, k * CQ:(k + 1) * CQ],
                          start=(k == 0), stop=(k == 7))
                mb.mm((p.psE[:, 0:CQ], "g"), xfm[:, k * 128:(k + 1) * 128], wg[:, k * CQ:(k + 1) * CQ],
                      start=(k == 0), stop=(k == 7))
                mb.mm((p.psE[:, 512:514], "ab"), xfm[:, k * 128:(k + 1) * 128], wab[:, 2 * k:2 * k + 2],
                      start=(k == 0), stop=(k == 7))
            mb.tt(qtm[:, t * CQ:(t + 1) * CQ], p.psA[:, 0:CQ], bq_rep, OP.add)
            mb.tt(tq, p.psB[:, 0:CQ], bk_rep, OP.add)
            mb.act(tq if False else (tq, "sq"), tq, AF.Square, accum_out=cols[:, 0:1]) if False else None
            mb.reduce(cols[:, 0:1], tq.rearrange("p (a n) -> p a n", a=1), OP.add, negate=False) if False else None
            mb.act(vtm[:, t * CQ:(t + 1) * CQ], tq, AF.Square, accum_out=cols[:, 0:1])
            mb.act(cols[:, 1:2], cols[:, 0:1], AF.Sqrt)
            mb.ts(cols[:, 1:2], cols[:, 1:2], 1e-12, None, OP.max)
            mb.reciprocal(cols[:, 2:3], cols[:, 1:2])
            mb.ts(ktm[:, t * CQ:(t + 1) * CQ], tq, cols[:, 2:3], None, OP.mult)
            mb.tt(vtm[:, t * CQ:(t + 1) * CQ], p.psC[:, 0:CQ], bv_rep, OP.add)
            mb.stt(gtm[:, t * CQ:(t + 1) * CQ], (p.psE[:, 0:CQ], "g"), 1.0, bg_rep, OP.mult, OP.add)
            mb.act(gtm[:, t * CQ:(t + 1) * CQ], gtm[:, t * CQ:(t + 1) * CQ], AF.Sigmoid)
            mb.tt(abtm[:, t * 2:(t + 1) * 2], (p.psE[:, 512:514], "ab"), bab_rep, OP.add)
        lna = alloc("lna", [128, NT])
        lnb = alloc("lnb", [128, NT])
        bcol = alloc("bcol", [128, NT])
        Lm = alloc("Lm", [128, NT])
        Ecol = alloc("Ecol", [128, NT])
        bE = alloc("bE", [128, NT])
        Dfac = alloc("Dfac", [128, NT])
        tmp = alloc("tmp", [128, NT])
        tmp2 = alloc("tmp2", [128, NT])
        def neg_softplus_neg(dst, src):
            mb.act(tmp, src, AF.Abs)
            mb.act(tmp, tmp, AF.Exp, scale=-1.0)
            mb.act(tmp, tmp, AF.Ln, bias=1.0)
            mb.ts(tmp2, src, 0.0, None, OP.min)
            mb.tt(dst, tmp2, tmp, OP.subtract)
        ua = abtm.rearrange("p (t c) -> p c t", c=2)[:, 0]
        ub = abtm.rearrange("p (t c) -> p c t", c=2)[:, 1]
        neg_softplus_neg(lna, ua)
        neg_softplus_neg(lnb, ub)
        mb.act(bcol, ub, AF.Sigmoid)
        lna_bf = alloc("lnabf", [128, NT], BF16)
        mb.copy(lna_bf, lna)
        mb.mm(p.psA[:, 0:NT], p.cp.cumul_bf if False else p.cp.cumul, lna_bf, start=True, stop=True) if False else None
        mb.mm(p.psB[:, 0:NT], p.cp.cumul, lna, start=True, stop=True)
        mb.copy(Lm, p.psB[:, 0:NT])
        mb.act(Ecol, Lm, AF.Exp)
        mb.tt(bE, bcol, Ecol, OP.mult)
        LmCrow = alloc("LmCrow", [1, NT])
        mb.mm(p.psA[0:1, 0:NT], p.cp.ones_col, lna, start=True, stop=True)
        mb.copy(LmCrow, p.psA[0:1, 0:NT])
        LmCrep = alloc("LmCrep", [128, NT])
        mb.mm(p.psA[:, 0:NT], p.cp.ones_row, LmCrow, start=True, stop=True)
        mb.copy(LmCrep, p.psA[:, 0:NT])
        mb.tt(tmp, LmCrep, Lm, OP.subtract)
        mb.act(Dfac, tmp, AF.Exp)
        ECrow = alloc("ECrow", [1, NT])
        mb.act(ECrow, LmCrow, AF.Exp)
        qe_bf = alloc("qebf", [128, NT * CQ], BF16)
        for t in range(NT):
            mb.ts(qe_bf[:, t * CQ:(t + 1) * CQ], qtm[:, t * CQ:(t + 1) * CQ], Ecol[:, t:t + 1], None, OP.mult)
        Z = alloc("Z", [128, 128])
        Zbf = alloc("Zbf", [128, 128], BF16)
        Ztbf = alloc("Ztbf", [128, 128], BF16)
        mb.memset(Z, 0.0)
        mb.memset(Zbf, 0.0)
        mb.memset(Ztbf, 0.0)
        kfm = alloc("kfm", [128, 128], BF16)
        qfm = alloc("qfm", [128, 128], BF16)
        qfmp = alloc("qfmp", [128, 128], BF16)
        qtmb = alloc("qtmb", [128, 128], BF16)
        colb = alloc("colb", [128, 128])
        E2a = alloc("E2a", [128, 128])
        E2b = alloc("E2b", [128, 128])
        E2p = alloc("E2p", [128, 128])
        kkb = alloc("kkb", [128, 128], BF16)
        Nmat = alloc("Nmat", [128, 128], BF16)
        NTm = alloc("NTm", [128, 128], BF16)
        lbcol = alloc("lbcol", [128, 1])
        nLm = alloc("nLm", [128, 1])
        G = [alloc(f"G{b}", [64, 64], BF16) for b in range(2)]
        Gt = [alloc(f"Gt{b}", [64, 64], BF16) for b in range(2)]
        Spow = [alloc(f"Spow{b}", [64, 2 * 64], BF16) for b in range(2)]
        Ncp = [alloc(f"Ncp{b}", [64, 2 * 64], BF16) for b in range(2)]
        ptmp = alloc("ptmp", [128, 128])
        bhat = alloc("bhat", [128, 128])
        bhat_bf = alloc("bbf", [128, 128], BF16)
        uhat_bf = alloc("ubf", [128, 128], BF16)
        du_bf = alloc("dubf", [128, 128], BF16)
        ofm_t = alloc("ofmt", [128, 128])
        obf = alloc("obf", [128, 128], BF16)
        Ut_bf = alloc("Utbf", [128, 128], BF16)
        Wt_bf = alloc("Wtbf", [128, 128], BF16)
        eccol = alloc("eccol", [128, 1])
        ident64 = p.cp.ident_bf[0:64, 0:64]
        for c in range(NT):
            ksl = ktm[:, c * CQ:(c + 1) * CQ]
            mb.transpose(p.psT[:, 0:128], ksl, p.cp.ident_bf)
            mb.copy(kfm, p.psT[:, 0:128])
            mb.transpose(p.psT[:, 0:128], qe_bf[:, c * CQ:(c + 1) * CQ], p.cp.ident_bf)
            mb.copy(qfm, p.psT[:, 0:128])
            mb.copy(qtmb, qtm[:, c * CQ:(c + 1) * CQ])
            mb.transpose(p.psT[:, 0:128], qtmb, p.cp.ident_bf)
            mb.copy(qfmp, p.psT[:, 0:128])
            mb.tt(tmp[:, 0:1], Lm[:, c:c + 1], lnb[:, c:c + 1], OP.add)
            mb.ts(colb, p.cp.ident if False else colb, 0.0, None, OP.mult) if False else None
            mb.ts(colb, p.cp.mask_le, 0.0, Lm[:, c:c + 1], OP.mult, OP.add)
            mb.transpose(p.psA[:, 0:128], colb, p.cp.ident)
            mb.copy(E2a, p.psA[:, 0:128])
            mb.ts(colb, p.cp.mask_le, 0.0, tmp[:, 0:1], OP.mult, OP.add)
            mb.transpose(p.psA[:, 0:128], colb, p.cp.ident)
            mb.copy(E2p, p.psA[:, 0:128])
            mb.ts(nLm, Lm[:, c:c + 1], -1.0, None, OP.mult)
            mb.ts(ptmp, E2a, -1.0, None, OP.mult)
            mb.ts(ptmp, ptmp, tmp[:, 0:1], 0.0, OP.add, OP.min)
            mb.act(E2b, ptmp, AF.Exp)
            mb.ts(ptmp, E2p, nLm, 0.0, OP.add, OP.min)
            mb.act(E2p, ptmp, AF.Exp)
            mb.ts(ptmp, E2a, nLm, 0.0, OP.add, OP.min)
            mb.act(E2a, ptmp, AF.Exp)
            mb.mm(p.psA[:, 0:128], kfm, kfm, start=True, stop=True)
            mb.copy(kkb, p.psA[:, 0:128])
            mb.tt(ptmp, E2b, p.cp.mask_lt_neg, OP.mult)
            mb.tt(Nmat, ptmp, kkb, OP.mult)
            mb.tt(ptmp, E2p, p.cp.mask_st_neg, OP.mult)
            mb.tt(NTm, ptmp, kkb, OP.mult)
            mb.tt(E2a, E2a, p.cp.mask_le, OP.mult)
            for b in range(2):
                o = b * 64
                if b == 0:
                    mb.copy(Ncp[b][:, 0:64], NTm[0:64, 0:64])
                    mb.copy(Ncp[b][:, 64:128], Nmat[0:64, 0:64])
                else:
                    mb.dma(Ncp[b][:, 0:64], NTm[64:128, 64:128])
                    mb.dma(Ncp[b][:, 64:128], Nmat[64:128, 64:128])
                Yt = Ncp[b][:, 0:64]
                Y = Ncp[b][:, 64:128]
                mb.tt(G[b], Yt, ident64, OP.add)
                mb.tt(Gt[b], Y, ident64, OP.add)
                mb.copy(Spow[b][:, 0:64], Yt)
                mb.copy(Spow[b][:, 64:128], Y)
                for lev in range(5):
                    mb.mm(p.psB[0:64, 0:64], Spow[b][:, 64:128], Spow[b][:, 0:64], start=True, stop=True)
                    mb.mm(p.psB[0:64, 64:128], Spow[b][:, 0:64], Spow[b][:, 64:128], start=True, stop=True)
                    mb.copy(Spow[b][:, :], p.psB[0:64, 0:128])
                    mb.mm(p.psC[0:64, 0:64], Gt[b], Spow[b][:, 0:64], start=True, stop=True)
                    mb.mm(p.psC[0:64, 64:128], Spow[b][:, 0:64], Gt[b], start=True, stop=True)
                    mb.tt(G[b], G[b], p.psC[0:64, 0:64], OP.add)
                    mb.tt(Gt[b], Gt[b], p.psC[0:64, 64:128], OP.add)
            mb.mm(p.psA[:, 0:128], kfm, Zbf, start=True, stop=True)
            mb.ts(bhat, vtm[:, c * CQ:(c + 1) * CQ], bcol[:, c:c + 1], None, OP.mult)
            mb.stt(bhat, p.psA[:, 0:128], bE[:, c:c + 1], bhat, OP.mult, OP.subtract)
            mb.ts(bhat, bhat, -1.0, None, OP.mult)
            mb.copy(bhat_bf, bhat)
            mb.mm(p.psB[0:64, 0:128], G[0], bhat_bf[0:64, :], start=True, stop=True)
            mb.copy(uhat_bf[0:64, :], p.psB[0:64, 0:128])
            mb.mm(p.psC[0:64, 0:128], NTm[0:64, 64:128], uhat_bf[0:64, :], start=True, stop=True)
            mb.dma(ptmp[0:64, :], bhat[64:128, :])
            mb.tt(ptmp[0:64, :], p.psC[0:64, 0:128], ptmp[0:64, :], OP.add)
            mb.copy(bhat_bf[0:64, :], ptmp[0:64, :])
            mb.mm(p.psB[0:64, 0:128], G[1], bhat_bf[0:64, :], start=True, stop=True)
            mb.copy(uhat_bf[64:128, :], p.psB[0:64, 0:128])
            mb.transpose(p.psT[:, 0:128], uhat_bf, p.cp.ident_bf)
            mb.copy(Ut_bf, p.psT[:, 0:128])
            mb.mm(p.psA[:, 0:128], Ut_bf, qfmp, start=True, stop=True)
            mb.tt(ptmp, p.psA[:, 0:128], E2a, OP.mult)
            mb.copy(Wt_bf, ptmp)
            mb.mm((p.psE[:, 0:128], "o"), Wt_bf, ktm[:, c * CQ:(c + 1) * CQ], start=True, stop=False)
            mb.mm((p.psE[:, 0:128], "o"), qfm, Ztbf, start=False, stop=True)
            mb.tt(ofm_t, (p.psE[:, 0:128], "o"), gtm[:, c * CQ:(c + 1) * CQ], OP.mult)
            mb.copy(obf, ofm_t)
            mb.dma(o_bf[c * 128:(c + 1) * 128, :], obf, eng="pool")
            mb.ts(du_bf, uhat_bf, Dfac[:, c:c + 1], None, OP.mult)
            mb.mm(p.psB[:, 0:128], ksl, du_bf, start=True, stop=True)
            mb.mm(p.psC[:, 0:1], p.cp.ones_row, ECrow[:, c:c + 1], start=True, stop=True)
            mb.copy(eccol, p.psC[:, 0:1])
            mb.stt(Z, Z, eccol, p.psB[:, 0:128], OP.mult, OP.add)
            mb.copy(Zbf, Z)
            mb.transpose(p.psT[:, 0:128], Zbf, p.cp.ident_bf)
            mb.copy(Ztbf, p.psT[:, 0:128])
    with p.stage() as alloc:
        wo = wload_bf(p, alloc, w["Wo"], CQ, DM, "wo")
        borep = rep_row(p, alloc, w["bo"], DM, "bo")
        oin = alloc("oin", [128, 128], BF16)
        ofm2 = alloc("ofm2", [128, 128], BF16)
        xo = alloc("xo", [128, DM])
        for t in range(NT):
            mb.dma(oin, o_bf[t * 128:(t + 1) * 128, :])
            mb.transpose(p.psT[:, 0:128], oin, p.cp.ident_bf)
            mb.copy(ofm2, p.psT[:, 0:128])
            for oc in range(2):
                mb.mm(p.psA, ofm2, wo[:, oc * 512:(oc + 1) * 512], start=True, stop=True)
                mb.tt(xo[:, oc * 512:(oc + 1) * 512], p.psA, borep[:, oc * 512:(oc + 1) * 512], OP.add)
            mb.dma(a1_tm[t * 128:(t + 1) * 128, :], xo, eng="pool")


def emit_l0_attn(p, w, x_tm, xrfm, a0_tm, cs, scr):
    mb, T, NT = p.mb, p.T, p.NT
    H, HD2 = NH, 2 * NH
    p.vfm = p.salloc("l0_vfm", [32, T])
    with p.stage() as alloc:
        rmsc = bias_cols(p, alloc, w["rms_w"], DM, "rms")
        win = wload_bf(p, alloc, w["in_proj_w"], DM, 6176, "inprj")
        for k in range(8):
            mb.ts(win[:, k * 6176:(k + 1) * 6176], win[:, k * 6176:(k + 1) * 6176], rmsc[:, k:k + 1], None, OP.mult)
        bcols = alloc("bcols", [128, 49])
        dma_slow(p, bcols[:, 0:48], w["in_proj_b"][0:6144].rearrange("(a b) -> b a", b=128))
        dma_slow(p, bcols[0:32, 48:49], w["in_proj_b"][6144:6176].rearrange("(a b) -> b a", b=32))
        cwc = alloc("cwc", [128, 16 * 4])
        for o in range(16):
            mb.dma(cwc[:, o * 4:(o + 1) * 4], w["conv_w"][o * 128:(o + 1) * 128, :])
        cbc = bias_cols(p, alloc, w["conv_b"], DI, "convb")
        xss = [alloc(f"x{j}", [128, 8 * 512], BF16) for j in range(2)]
        zb = alloc("zb", [128, 512])
        zb2 = alloc("zb2", [128, 512])
        xgs = alloc("xgs", [128, 16 * 515])
        xcb = alloc("xcb", [128, 512], BF16)
        accs = [alloc(f"acc{j}", [128, 512]) for j in range(2)]
        ktmps = [alloc(f"ktmp{j}", [128, 512]) for j in range(2)]
        ktts = [alloc(f"ktt{j}", [128, 128]) for j in range(2)]
        knos = [alloc(f"kno{j}", [128, 128], BF16) for j in range(2)]
        kno2s = [alloc(f"kno2{j}", [128, 128], BF16) for j in range(2)]
        ksq = alloc("ksq", [128, 2 * 128])
        ssq = alloc("ssq", [128, 2])
        scl = alloc("scl", [128, 8])
        for o in range(16):
            mb.memset(xgs[:, o * 515:o * 515 + 3], 0.0)
        for s0 in range(0, T, 512):
            xs = xss[(s0 // 512) % 2]
            for k in range(8):
                mb.dma(xs[:, k * 512:(k + 1) * 512], xrfm[k * 128:(k + 1) * 128, s0:s0 + 512])
            for o in range(16):
                psz = p.psA if o % 2 == 0 else p.psE[:, 1536:2048]
                zbx = zb if o % 2 == 0 else zb2
                for k in range(8):
                    mb.mm(psz, win[:, k * 6176 + o * 128:k * 6176 + o * 128 + 128],
                          xs[:, k * 512:(k + 1) * 512], start=(k == 0), stop=(k == 7))
                mb.ts(zbx, psz, bcols[:, o:o + 1], None, OP.add)
                mb.dma(scr["zfm"][o * 128:(o + 1) * 128, s0:s0 + 512], zbx, eng="pool")
            for o in range(16):
                psg = p.psB if o % 2 == 0 else p.psE[:, 0:512]
                acc = accs[o % 2]
                for k in range(8):
                    mb.mm(psg, win[:, k * 6176 + 2048 + o * 128:k * 6176 + 2048 + o * 128 + 128],
                          xs[:, k * 512:(k + 1) * 512], start=(k == 0), stop=(k == 7))
                mb.ts(xgs[:, o * 515 + 3:o * 515 + 515], psg, bcols[:, 16 + o:17 + o], None, OP.add)
                mb.ts(acc, xgs[:, o * 515 + 3:o * 515 + 515], cwc[:, o * 4 + 3:o * 4 + 4], None, OP.mult)
                for j in range(3):
                    mb.stt(acc, xgs[:, o * 515 + j:o * 515 + j + 512], cwc[:, o * 4 + j:o * 4 + j + 1],
                           acc, OP.mult, OP.add)
                mb.copy(xgs[:, o * 515:o * 515 + 3], xgs[:, o * 515 + 512:o * 515 + 515])
                mb.act(xcb, acc, AF.Silu, bias=cbc[:, o:o + 1])
                mb.dma(scr["xcfm"][o * 128:(o + 1) * 128, s0:s0 + 512], xcb, eng="pool")
            for o in range(16):
                ktmp = ktmps[o % 2]
                for k in range(8):
                    mb.mm(p.psC, win[:, k * 6176 + 4096 + o * 128:k * 6176 + 4096 + o * 128 + 128],
                          xs[:, k * 512:(k + 1) * 512], start=(k == 0), stop=(k == 7))
                mb.ts(ktmp, p.psC, bcols[:, 32 + o:33 + o], None, OP.add)
                for q in range(4):
                    jq = q % 2
                    ktt, kno, kno2 = ktts[jq], knos[jq], kno2s[jq]
                    pst = p.psE[:, 512 + jq * 512:512 + jq * 512 + 128]
                    mb.transpose((pst, f"kt{jq}"), ktmp[:, q * 128:(q + 1) * 128], p.cp.ident)
                    mb.copy(ktt, (pst, f"kt{jq}"))
                    mb.act(ksq[:, jq * 128:(jq + 1) * 128], ktt, AF.Square, accum_out=ssq[:, jq:jq + 1])
                    mb.act(scl[:, jq * 4:jq * 4 + 1], ssq[:, jq:jq + 1], AF.Sqrt)
                    mb.ts(scl[:, jq * 4 + 1:jq * 4 + 2], scl[:, jq * 4:jq * 4 + 1], 1e-6, None, OP.add)
                    mb.reciprocal(scl[:, jq * 4 + 2:jq * 4 + 3], scl[:, jq * 4 + 1:jq * 4 + 2])
                    mb.ts(kno, ktt, scl[:, jq * 4 + 2:jq * 4 + 3], None, OP.mult)
                    mb.dma(scr["kntm"][(s0 + q * 128):(s0 + (q + 1) * 128), o * 128:(o + 1) * 128],
                           kno, eng="pool")
                    ps2 = p.psT[:, jq * 128:(jq + 1) * 128]
                    mb.transpose((ps2, f"t{jq}"), kno, p.cp.ident_bf)
                    mb.copy(kno2, (ps2, f"t{jq}"))
                    mb.dma(scr["knfm"][o * 128:(o + 1) * 128, s0 + q * 128:s0 + (q + 1) * 128],
                           kno2, eng="pool")
            for k in range(8):
                mb.mm(p.psB[0:32, :], win[:, k * 6176 + 6144:k * 6176 + 6176],
                      xs[:, k * 512:(k + 1) * 512], start=(k == 0), stop=(k == 7))
            mb.ts(p.vfm[:, s0:s0 + 512], p.psB[0:32, :], bcols[0:32, 48:49], None, OP.add)
    with p.stage() as alloc:
        wq = wload_bf(p, alloc, w["Q_w"], DI, DI, "qw")
        xss = [alloc(f"x{j}", [128, 16 * 512], BF16) for j in range(2)]
        ktmps = [alloc(f"ktmp{j}", [128, 512]) for j in range(2)]
        ktts = [alloc(f"ktt{j}", [128, 128]) for j in range(2)]
        knos = [alloc(f"kno{j}", [128, 128], BF16) for j in range(2)]
        kno2s = [alloc(f"kno2{j}", [128, 128], BF16) for j in range(2)]
        ssq = alloc("ssq", [128, 2])
        scl = alloc("scl", [128, 8])
        acc = alloc("acc", [128, 2 * 128])
        for s0 in range(0, T, 512):
            xs = xss[(s0 // 512) % 2]
            for k in range(16):
                mb.dma(xs[:, k * 512:(k + 1) * 512], scr["xcfm"][k * 128:(k + 1) * 128, s0:s0 + 512])
            for o in range(16):
                psq = p.psA if o % 2 == 0 else p.psE[:, 1536:2048]
                ktmp = ktmps[o % 2]
                for k in range(16):
                    mb.mm(psq, wq[:, k * DI + o * 128:k * DI + o * 128 + 128],
                          xs[:, k * 512:(k + 1) * 512], start=(k == 0), stop=(k == 15))
                mb.copy(ktmp, psq)
                for q in range(4):
                    jq = q % 2
                    ktt, kno, kno2 = ktts[jq], knos[jq], kno2s[jq]
                    pst = p.psB if jq == 0 else p.psC
                    mb.transpose(pst[:, 0:128], ktmp[:, q * 128:(q + 1) * 128], p.cp.ident)
                    mb.copy(ktt, pst[:, 0:128])
                    mb.act(acc[:, jq * 128:(jq + 1) * 128], ktt, AF.Square, accum_out=ssq[:, jq:jq + 1])
                    mb.act(scl[:, jq * 4:jq * 4 + 1], ssq[:, jq:jq + 1], AF.Sqrt)
                    mb.ts(scl[:, jq * 4 + 1:jq * 4 + 2], scl[:, jq * 4:jq * 4 + 1], 1e-6, None, OP.add)
                    mb.reciprocal(scl[:, jq * 4 + 2:jq * 4 + 3], scl[:, jq * 4 + 1:jq * 4 + 2])
                    mb.ts(kno, ktt, scl[:, jq * 4 + 2:jq * 4 + 3], None, OP.mult)
                    ps2 = p.psT[:, jq * 128:(jq + 1) * 128]
                    mb.transpose((ps2, f"t{jq}"), kno, p.cp.ident_bf)
                    mb.copy(kno2, (ps2, f"t{jq}"))
                    mb.dma(scr["qnfm"][o * 128:(o + 1) * 128, s0 + q * 128:s0 + (q + 1) * 128],
                           kno2, eng="pool")


def emit_l0_gates_and_scan(p, w, cs, scr, retr_tm):
    mb, T, NT = p.mb, p.T, p.NT
    NT16 = NT * 16
    g_lm = p.salloc("l0_lm", [128, NT16])
    g_E = p.salloc("l0_E", [128, NT16])
    g_cos = p.salloc("l0_cos", [128, NT16])
    g_sin = p.salloc("l0_sin", [128, NT16])
    g_beta = p.salloc("l0_beta", [128, NT16])
    g_selC2 = p.salloc("l0_selC2", [128, NT * 32])
    g_bv = p.salloc("l0_bv", [128, NT * 32])
    g_bE2 = p.salloc("l0_bE2", [128, NT * 32])
    g_E2d = p.salloc("l0_E2d", [128, NT * 32])
    g_Df2 = p.salloc("l0_Df2", [128, NT * 32])
    g_ECrow = p.salloc("l0_ECrow", [1, NT16])
    g_cosC = p.salloc("l0_cosC", [1, NT16])
    g_sinC = p.salloc("l0_sinC", [1, NT16])
    with p.stage() as alloc:
        wg = wload_bf(p, alloc, w["gate_w"], DI, 112, "gw")
        gb_rep = rep_row(p, alloc, w["gate_b"], 112, "gb")
        gates = alloc("gates", [128, NT * 112])
        xtg = [alloc(f"xt{j}", [128, 16 * 128], BF16) for j in range(2)]
        for t in range(NT):
            xt = xtg[t % 2]
            psg = p.psA if t % 2 == 0 else p.psB
            for k in range(16):
                mb.dma(xt[:, k * 128:(k + 1) * 128], scr["xcfm"][k * 128:(k + 1) * 128, t * 128:(t + 1) * 128])
            for k in range(16):
                mb.mm(psg[:, 0:112], xt[:, k * 128:(k + 1) * 128], wg[:, k * 112:(k + 1) * 112],
                      start=(k == 0), stop=(k == 15))
            mb.tt(gates[:, t * 112:(t + 1) * 112], psg[:, 0:112], gb_rep[:, 0:112], OP.add)
        gv = gates.rearrange("p (t g h) -> p t g h", g=7, h=16)
        ty = {}
        for gi, nm in enumerate(["alpha", "omega", "selB", "selC", "dt", "betar", "rr"]):
            ty[nm] = alloc(nm, [128, NT16])
            mb.copy(ty[nm].rearrange("p (t h) -> p t h", h=16), gv[:, :, gi, :])
        t1 = alloc("t1", [128, NT16]); t2 = alloc("t2", [128, NT16]); t3 = alloc("t3", [128, NT16])
        t4 = alloc("t4", [128, NT16]); t5 = alloc("t5", [128, NT16])
        fr = alloc("fr", [128, NT16])
        mb.dma(fr, cs["c_l0_freqs"])
        def softplus(dst, src):
            mb.act(t1, src, AF.Abs)
            mb.act(t1, t1, AF.Exp, scale=-1.0)
            mb.act(t1, t1, AF.Ln, bias=1.0)
            mb.ts(t2, src, 0.0, None, OP.max)
            mb.tt(dst, t2, t1, OP.add)
        softplus(t3, ty["dt"])
        ldsrow = alloc("ldsrow", [1, 16])
        mb.dma(ldsrow, w["log_dt_scale"].rearrange("(o n) -> o n", o=1))
        mb.act(ldsrow, ldsrow, AF.Exp)
        ldsrep = alloc("ldsrep", [128, NT16])
        mb.mm(p.psC[:, 0:16], p.cp.ones_row, ldsrow, start=True, stop=True)
        mb.copy(ldsrep[:, 0:16], p.psC[:, 0:16])
        wdt = 16
        while wdt < NT16:
            cc = min(wdt, NT16 - wdt)
            mb.copy(ldsrep[:, wdt:wdt + cc], ldsrep[:, 0:cc])
            wdt += cc
        dt = alloc("dtv", [128, NT16])
        mb.tt(dt, t3, ldsrep, OP.mult)
        mb.ts(dt, dt, 1e-4, None, OP.add)
        al = alloc("al", [128, NT16])
        softplus(al, ty["alpha"])
        mb.act(t3, ty["omega"], AF.Tanh)
        mb.ts(t3, t3, 1.0, None, OP.add)
        om = alloc("om", [128, NT16])
        mb.tt(om, t3, fr, OP.mult)
        ah = alloc("ah", [128, NT16]); wh = alloc("wh", [128, NT16])
        mb.stt(ah, dt, 0.5, al, OP.mult, OP.mult)
        mb.stt(wh, dt, 0.5, om, OP.mult, OP.mult)
        mb.act(t1, ah, AF.Square)
        mb.act(t2, wh, AF.Square)
        mb.ts(t3, ah, 1.0, None, OP.add)
        mb.act(t3, t3, AF.Square)
        det = alloc("det", [128, NT16])
        mb.tt(det, t3, t2, OP.add)
        mb.reciprocal(det, det)
        mb.tt(t3, t1, t2, OP.add)
        mb.ts(t3, t3, -1.0, 1.0, OP.mult, OP.add)
        cc_ = alloc("cc", [128, NT16]); ss_ = alloc("ss", [128, NT16])
        mb.tt(cc_, t3, det, OP.mult)
        mb.stt(ss_, wh, 2.0, det, OP.mult, OP.mult)
        mb.act(t1, cc_, AF.Square)
        mb.act(t2, ss_, AF.Square)
        mb.tt(t3, t1, t2, OP.add)
        mb.ts(t3, t3, 1e-16, None, OP.max)
        mb.act(t4, t3, AF.Ln)
        mb.ts(t5, t4, 0.5, None, OP.mult)
        mb.ts(t5, t5, -18.420680743952367, 0.0, OP.max, OP.min)
        mb.act(t1, ty["rr"], AF.Sigmoid)
        lm2 = alloc("lm2", [128, NT16])
        mb.stt(lm2, t1, 8.0, t5, OP.mult, OP.mult)
        mag2 = alloc("mag2", [128, NT16])
        mb.act(mag2, lm2, AF.Exp)
        mb.act(t1, mag2, AF.Square)
        mb.ts(t1, t1, -1.0, 1.0, OP.mult, OP.add)
        mb.ts(t1, t1, 0.0, 1.0, OP.max, OP.min)
        mb.ts(t1, t1, 1e-6, None, OP.add)
        mb.act(t1, t1, AF.Sqrt)
        mb.act(t2, ty["selB"], AF.Sigmoid)
        vp = alloc("vp", [128, NT16])
        mb.tt(vp, t2, t1, OP.mult)
        mb.act(g_beta, ty["betar"], AF.Sigmoid)
        mb.act(t1, ty["selC"], AF.Sigmoid)
        for h2 in range(2):
            mb.copy(g_selC2.rearrange("p (t h c) -> p t h c", h=16, c=2)[:, :, :, h2],
                    t1.rearrange("p (t h) -> p t h", h=16))
        rmag = alloc("rmag", [128, NT16])
        mb.act(rmag, t4, AF.Exp, scale=-0.5)
        cth = alloc("cth", [128, NT16]); sth = alloc("sth", [128, NT16])
        mb.tt(cth, cc_, rmag, OP.mult)
        mb.tt(sth, ss_, rmag, OP.mult)
        mb.ts(t1, cth, 1.0, None, OP.add)
        mb.act(t2, t1, AF.Square)
        mb.act(t3, sth, AF.Square)
        mb.tt(t2, t2, t3, OP.add)
        mb.ts(t2, t2, 1e-30, None, OP.max)
        mb.act(t2, t2, AF.Sqrt)
        mb.tt(t1, t1, t2, OP.add)
        mb.ts(t2, t1, 1e-20, None, OP.is_gt)
        mb.ts(t1, t1, 1e-30, None, OP.max)
        mb.reciprocal(t1, t1)
        mb.tt(t1, sth, t1, OP.mult)
        mb.act(t1, t1, AF.Arctan)
        th = alloc("th", [128, NT16])
        mb.ts(th, t1, -4.0, None, OP.mult)
        mb.ts(t3, t2, -1.0, 1.0, OP.mult, OP.add)
        mb.ts(t3, t3, -np.pi, None, OP.mult)
        mb.tt(th, th, t2, OP.mult)
        mb.tt(th, th, t3, OP.add)
        mb.mm(p.psA[:, 0:NT16] if NT16 <= 512 else p.psA, p.cp.cumul, lm2, start=True, stop=True)
        mb.copy(g_lm, p.psA[:, 0:NT16])
        mb.act(g_E, g_lm, AF.Exp)
        Th = alloc("Th", [128, NT16])
        mb.mm(p.psB[:, 0:NT16], p.cp.cumul, th, start=True, stop=True)
        mb.copy(Th, p.psB[:, 0:NT16])
        iT = alloc("iT", [128, NT16], I32)
        kf = alloc("kf", [128, NT16])
        mb.ts(t1, Th, 1.0 / (2 * np.pi), None, OP.mult)
        mb.copy(iT, t1)
        mb.copy(kf, iT)
        C1, C2 = 6.28125, 2 * np.pi - 6.28125
        mb.stt(t2, kf, -C1, Th, OP.mult, OP.add)
        mb.stt(t2, kf, -C2, t2, OP.mult, OP.add)
        mb.ts(t3, t2, np.pi, None, OP.is_gt)
        mb.stt(t2, t3, -2 * np.pi, t2, OP.mult, OP.add)
        mb.ts(t3, t2, -np.pi, None, OP.is_lt)
        mb.stt(t2, t3, 2 * np.pi, t2, OP.mult, OP.add)
        mb.act(g_sin, t2, AF.Sin)
        mb.act(t3, t2, AF.Abs)
        mb.act(g_cos, t3, AF.Sin, scale=-1.0, bias=np.pi / 2)
        mb.mm(p.psA[0:1, 0:NT16], p.cp.ones_col, lm2, start=True, stop=True)
        mb.copy(g_ECrow, p.psA[0:1, 0:NT16])
        mb.act(g_ECrow, g_ECrow, AF.Exp)
        mb.mm(p.psB[0:1, 0:NT16], p.cp.ones_col, lm2, start=True, stop=True)
        LmCrow = alloc("LmCrow", [1, NT16])
        mb.copy(LmCrow, p.psB[0:1, 0:NT16])
        LmCrep = alloc("LmCrep", [128, NT16])
        mb.mm(p.psC[:, 0:NT16], p.cp.ones_row, LmCrow, start=True, stop=True)
        mb.copy(LmCrep, p.psC[:, 0:NT16])
        mb.tt(t1, LmCrep, g_lm, OP.subtract)
        Df = alloc("Df", [128, NT16])
        mb.act(Df, t1, AF.Exp)
        mb.mm(p.psA[0:1, 0:NT16], p.cp.ones_col, th, start=True, stop=True)
        ThC = alloc("ThC", [1, NT16])
        mb.copy(ThC, p.psA[0:1, 0:NT16])
        r1 = alloc("r1", [1, NT16]); r2 = alloc("r2", [1, NT16]); r3 = alloc("r3", [1, NT16])
        iR = alloc("iR", [1, NT16], I32)
        mb.ts(r1, ThC, 1.0 / (2 * np.pi), None, OP.mult)
        mb.copy(iR, r1)
        mb.copy(r1, iR)
        mb.stt(r2, r1, -C1, ThC, OP.mult, OP.add)
        mb.stt(r2, r1, -C2, r2, OP.mult, OP.add)
        mb.ts(r3, r2, np.pi, None, OP.is_gt)
        mb.stt(r2, r3, -2 * np.pi, r2, OP.mult, OP.add)
        mb.ts(r3, r2, -np.pi, None, OP.is_lt)
        mb.stt(r2, r3, 2 * np.pi, r2, OP.mult, OP.add)
        mb.act(g_sinC, r2, AF.Sin)
        mb.act(r3, r2, AF.Abs)
        mb.act(g_cosC, r3, AF.Sin, scale=-1.0, bias=np.pi / 2)
        for h2 in range(2):
            mb.copy(g_bE2.rearrange("p (t h c) -> p t h c", h=16, c=2)[:, :, :, h2],
                    g_E.rearrange("p (t h) -> p t h", h=16))
            mb.copy(g_Df2.rearrange("p (t h c) -> p t h c", h=16, c=2)[:, :, :, h2],
                    Df.rearrange("p (t h) -> p t h", h=16))
        mb.copy(g_E2d, g_bE2)
        bdup = alloc("bdup", [128, NT * 32])
        for h2 in range(2):
            mb.copy(bdup.rearrange("p (t h c) -> p t h c", h=16, c=2)[:, :, :, h2],
                    g_beta.rearrange("p (t h) -> p t h", h=16))
        mb.tt(g_bE2, g_bE2, bdup, OP.mult)
        vg = alloc("vg", [128, NT * 32])
        for c in range(NT):
            mb.transpose(p.psA[:, 0:32], p.vfm[:, c * 128:(c + 1) * 128], p.cp.ident[0:32, 0:32])
            mb.copy(vg[:, c * 32:(c + 1) * 32], p.psA[:, 0:32])
        vpd = alloc("vpd", [128, NT * 32])
        for h2 in range(2):
            mb.copy(vpd.rearrange("p (t h c) -> p t h c", h=16, c=2)[:, :, :, h2],
                    vp.rearrange("p (t h) -> p t h", h=16))
        mb.tt(vg, vg, vpd, OP.mult)
        vre = vg.rearrange("p (t h c) -> p c t h", c=2, h=16)[:, 0]
        vim = vg.rearrange("p (t h c) -> p c t h", c=2, h=16)[:, 1]
        wre = g_bv.rearrange("p (t h c) -> p c t h", c=2, h=16)[:, 0]
        wim = g_bv.rearrange("p (t h c) -> p c t h", c=2, h=16)[:, 1]
        cthv = g_cos.rearrange("p (t h) -> p t h", h=16)
        sthv = g_sin.rearrange("p (t h) -> p t h", h=16)
        mb.tt(wre, vre, cthv, OP.mult)
        mb.tt(t1.rearrange("p (t h) -> p t h", h=16), vim, sthv, OP.mult)
        mb.tt(wre, wre, t1.rearrange("p (t h) -> p t h", h=16), OP.add)
        mb.tt(wim, vim, cthv, OP.mult)
        mb.tt(t1.rearrange("p (t h) -> p t h", h=16), vre, sthv, OP.mult)
        mb.tt(wim, wim, t1.rearrange("p (t h) -> p t h", h=16), OP.subtract)
        mb.tt(g_bv, g_bv, bdup, OP.mult)
    with p.stage() as alloc:
        NT16 = NT * 16
        Kfm = alloc("Kfm", [128, 16 * 128], BF16)
        Qfm = alloc("Qfm", [128, 16 * 128], BF16)
        Ktm = alloc("Ktm", [128, 16 * 128], BF16)
        Khtm = alloc("Khtm", [128, 16 * 128], BF16)
        Khfm = alloc("Khfm", [128, 16 * 128], BF16)
        colb = alloc("colb", [128, 128])
        Atile = alloc("Atile", [128, 2048])
        nrep = alloc("nrep", [128, 2048])
        e2a = alloc("e2a", [128, 2048])
        kkall = alloc("kkall", [128, 2048])
        NTd = alloc("NTd", [128, 2048], BF16)
        NT12 = alloc("NT12", [128, 16 * 64], BF16)
        Nall = alloc("Nall", [128, 2048], BF16)
        Lh = alloc("Lh", [128, 2048], BF16)
        Gall = alloc("Gall", [128, 2048], BF16)
        Gtall = alloc("Gtall", [128, 2048], BF16)
        Sall = alloc("Sall", [128, 2048], BF16)
        Stall = alloc("Stall", [128, 2048], BF16)
        G2b0 = alloc("G2b0", [64, 16 * 64], BF16)
        mkneg = alloc("mkneg", [128, 2048])
        mkle = alloc("mkle", [128, 2048])
        mkblk = alloc("mkblk", [128, 2048])
        idrep = alloc("idrep", [128, 2048], BF16)
        idrepf = alloc("idrepf", [128, 2048])
        mb.dma(mkneg, cs["c_mask_st_neg16"])
        mb.dma(mkle, cs["c_mask_le16"])
        mb.dma(mkblk, cs["c_blkdiag16"])
        mb.dma(idrepf, cs["c_identrep16"])
        mb.copy(idrep, idrepf)
        s0 = alloc("s0", [128, 32])
        s0bf = alloc("s0bf", [128, 32], BF16)
        mb.memset(s0, 0.0)
        mb.memset(s0bf, 0.0)
        bb = alloc("bb", [128, 32])
        bbbf = alloc("bbbf", [128, 32], BF16)
        uu = alloc("uu", [128, 32])
        uubf = alloc("uubf", [128, 32], BF16)
        w2c = alloc("w2c", [64, 32], BF16)
        h64 = alloc("h64", [64, 32])
        h128 = alloc("h128", [128, 32])
        yv = alloc("yv", [128, 32])
        y2 = alloc("y2", [128, 32])
        dub = alloc("dub", [128, 32], BF16)
        ecd = alloc("ecd", [128, 32])
        ccd = alloc("ccd", [128, 32])
        ssd = alloc("ssd", [128, 32])
        rows3 = alloc("rows3", [1, 96])
        snew = alloc("snew", [128, 32])
        for c in range(NT):
            for h in range(16):
                mb.dma(Kfm[:, h * 128:(h + 1) * 128], scr["knfm"][h * 128:(h + 1) * 128, c * 128:(c + 1) * 128])
                mb.dma(Qfm[:, h * 128:(h + 1) * 128], scr["qnfm"][h * 128:(h + 1) * 128, c * 128:(c + 1) * 128])
                mb.dma(Ktm[:, h * 128:(h + 1) * 128], scr["kntm"][c * 128:(c + 1) * 128, h * 128:(h + 1) * 128])
            for h in range(16):
                mb.ts(Khtm[:, h * 128:(h + 1) * 128], Ktm[:, h * 128:(h + 1) * 128],
                      g_beta[:, c * 16 + h:c * 16 + h + 1], None, OP.mult)
                pst = p.psT[:, (h % 4) * 128:(h % 4 + 1) * 128]
                mb.transpose((pst, f"t{h % 4}"), Khtm[:, h * 128:(h + 1) * 128], p.cp.ident_bf)
                mb.copy(Khfm[:, h * 128:(h + 1) * 128], (pst, f"t{h % 4}"), eng="act")
            for h in range(16):
                mb.ts(colb, mkle[:, 0:128], 0.0, g_lm[:, c * 16 + h:c * 16 + h + 1], OP.mult, OP.add)
                mb.transpose((p.psE[:, h * 128:(h + 1) * 128], f"q{h % 4}"), colb, p.cp.ident)
                mb.ts(nrep[:, h * 128:(h + 1) * 128], mkle[:, 0:128], 0.0,
                      g_lm[:, c * 16 + h:c * 16 + h + 1], OP.mult, OP.add)
            mb.copy(Atile, p.psE, eng="act")
            mb.tt(e2a, Atile, nrep, OP.subtract)
            mb.ts(e2a, e2a, 0.0, None, OP.min)
            mb.act(e2a, e2a, AF.Exp)
            for h in range(16):
                mb.mm((p.psE[:, h * 128:(h + 1) * 128], f"q{h % 4}"),
                      Kfm[:, h * 128:(h + 1) * 128], Khfm[:, h * 128:(h + 1) * 128],
                      start=True, stop=True)
            mb.copy(kkall, p.psE, eng="act")
            mb.tt(kkall, kkall, e2a, OP.mult)
            mb.tt(NTd, kkall, mkneg, OP.mult)
            for h in range(16):
                mb.copy(NT12[0:64, h * 64:(h + 1) * 64], NTd[0:64, h * 128 + 64:h * 128 + 128])
            mb.tt(NTd, NTd, mkblk, OP.mult)
            for g4 in range(4):
                for hh in range(4):
                    h = g4 * 4 + hh
                    mb.transpose((p.psT[:, hh * 128:(hh + 1) * 128], f"t{hh}"),
                                 NTd[:, h * 128:(h + 1) * 128], p.cp.ident_bf)
                mb.copy(Nall[:, g4 * 512:(g4 + 1) * 512], p.psT, eng="act")
            for h in range(16):
                mb.mm((p.psE[:, h * 128:(h + 1) * 128], f"q{h % 4}"),
                      Kfm[:, h * 128:(h + 1) * 128], Qfm[:, h * 128:(h + 1) * 128],
                      start=True, stop=True)
            mb.copy(kkall, p.psE, eng="act")
            mb.tt(kkall, kkall, e2a, OP.mult)
            mb.tt(Lh, kkall, mkle, OP.mult)
            mb.tt(Gall, NTd, idrep, OP.add)
            mb.tt(Gtall, Nall, idrep, OP.add)
            mb.copy(Sall, NTd)
            mb.copy(Stall, Nall)
            for lev in range(5):
                for h in range(16):
                    mb.mm((p.psE[:, h * 128:(h + 1) * 128], f"q{h % 4}"),
                          Stall[:, h * 128:(h + 1) * 128], Sall[:, h * 128:(h + 1) * 128],
                          start=True, stop=True)
                mb.copy(nrep, p.psE, eng="act")
                for h in range(16):
                    mb.mm((p.psE[:, h * 128:(h + 1) * 128], f"q{h % 4}"),
                          Sall[:, h * 128:(h + 1) * 128], Stall[:, h * 128:(h + 1) * 128],
                          start=True, stop=True)
                mb.copy(Stall, p.psE, eng="act")
                mb.copy(Sall, nrep)
                for h in range(16):
                    mb.mm((p.psE[:, h * 128:(h + 1) * 128], f"q{h % 4}"),
                          Gtall[:, h * 128:(h + 1) * 128], Sall[:, h * 128:(h + 1) * 128],
                          start=True, stop=True)
                mb.stt(Gall, p.psE, 1.0, Gall, OP.mult, OP.add)
                for h in range(16):
                    mb.mm((p.psE[:, h * 128:(h + 1) * 128], f"q{h % 4}"),
                          Sall[:, h * 128:(h + 1) * 128], Gtall[:, h * 128:(h + 1) * 128],
                          start=True, stop=True)
                mb.stt(Gtall, p.psE, 1.0, Gtall, OP.mult, OP.add)
            for h in range(16):
                mb.dma(G2b0[:, h * 64:(h + 1) * 64], Gall[64:128, h * 128 + 64:h * 128 + 128])
            for h in range(16):
                mb.mm(p.psA[:, h * 2:h * 2 + 2], Kfm[:, h * 128:(h + 1) * 128],
                      s0bf[:, h * 2:h * 2 + 2], start=True, stop=True)
            mb.tt(bb, p.psA[:, 0:32], g_bE2[:, c * 32:(c + 1) * 32], OP.mult)
            mb.tt(bb, g_bv[:, c * 32:(c + 1) * 32], bb, OP.subtract)
            mb.copy(bbbf, bb)
            for h in range(16):
                mb.mm(p.psB[:, h * 2:h * 2 + 2], Gall[:, h * 128:(h + 1) * 128],
                      bbbf[:, h * 2:h * 2 + 2], start=True, stop=True)
            mb.copy(uu, p.psB[:, 0:32])
            mb.copy(uubf, uu)
            for h in range(16):
                mb.mm(p.psC[0:64, h * 2:h * 2 + 2], NT12[0:64, h * 64:(h + 1) * 64],
                      uubf[0:64, h * 2:h * 2 + 2], start=True, stop=True)
            mb.copy(w2c, p.psC[0:64, 0:32])
            for h in range(16):
                mb.mm(p.psB[0:64, h * 2:h * 2 + 2], G2b0[:, h * 64:(h + 1) * 64],
                      w2c[:, h * 2:h * 2 + 2], start=True, stop=True)
            mb.copy(h64, p.psB[0:64, 0:32])
            mb.dma(h128[64:128, :], h64)
            mb.tt(uu[64:128, :], uu[64:128, :], h128[64:128, :], OP.add)
            mb.copy(uubf, uu)
            for h in range(16):
                mb.mm(p.psA[:, h * 2:h * 2 + 2], Qfm[:, h * 128:(h + 1) * 128],
                      s0bf[:, h * 2:h * 2 + 2], start=True, stop=True)
            mb.tt(yv, p.psA[:, 0:32], g_E2d[:, c * 32:(c + 1) * 32], OP.mult)
            for h in range(16):
                mb.mm(p.psB[:, h * 2:h * 2 + 2], Lh[:, h * 128:(h + 1) * 128],
                      uubf[:, h * 2:h * 2 + 2], start=True, stop=True)
            mb.tt(yv, yv, p.psB[:, 0:32], OP.add)
            cv = g_cos.rearrange("p (t h) -> p t h", h=16)[:, c]
            sv = g_sin.rearrange("p (t h) -> p t h", h=16)[:, c]
            yre = yv.rearrange("p (h k) -> p k h", k=2, h=16)[:, 0]
            yim = yv.rearrange("p (h k) -> p k h", k=2, h=16)[:, 1]
            ore = y2.rearrange("p (h k) -> p k h", k=2, h=16)[:, 0]
            oim = y2.rearrange("p (h k) -> p k h", k=2, h=16)[:, 1]
            tt16 = snew.rearrange("p (h k) -> p k h", k=2, h=16)[:, 0]
            mb.tt(ore, yre, cv, OP.mult)
            mb.tt(tt16, yim, sv, OP.mult)
            mb.tt(ore, ore, tt16, OP.subtract)
            mb.tt(oim, yim, cv, OP.mult)
            mb.tt(tt16, yre, sv, OP.mult)
            mb.tt(oim, oim, tt16, OP.add)
            mb.tt(y2, y2, g_selC2[:, c * 32:(c + 1) * 32], OP.mult)
            mb.copy(retr_tm[:, c * 32:(c + 1) * 32], y2)
            mb.ts(dub, uubf, 1.0, None, OP.mult) if False else None
            mb.tt(dub, uubf, g_Df2[:, c * 32:(c + 1) * 32], OP.mult)
            for h in range(16):
                mb.mm(p.psC[:, h * 2:h * 2 + 2], Ktm[:, h * 128:(h + 1) * 128],
                      dub[:, h * 2:h * 2 + 2], start=True, stop=True)
            mb.copy(rows3[:, 0:32][:, 0:16] if False else rows3[:, 0:16], g_ECrow[:, c * 16:(c + 1) * 16])
            mb.copy(rows3[:, 16:32], g_cosC[:, c * 16:(c + 1) * 16])
            mb.copy(rows3[:, 32:48], g_sinC[:, c * 16:(c + 1) * 16])
            mb.mm(p.psB[:, 0:48], p.cp.ones_row, rows3[:, 0:48], start=True, stop=True)
            for h2 in range(2):
                mb.copy(ecd.rearrange("p (h k) -> p k h", k=2, h=16)[:, h2], p.psB[:, 0:16])
                mb.copy(ccd.rearrange("p (h k) -> p k h", k=2, h=16)[:, h2], p.psB[:, 16:32])
                mb.copy(ssd.rearrange("p (h k) -> p k h", k=2, h=16)[:, h2], p.psB[:, 32:48])
            mb.stt(snew, s0, 1.0, snew, OP.mult, OP.bypass) if False else None
            mb.tt(snew, s0, ecd, OP.mult)
            mb.tt(snew, snew, p.psC[:, 0:32], OP.add)
            sre = snew.rearrange("p (h k) -> p k h", k=2, h=16)[:, 0]
            sim_ = snew.rearrange("p (h k) -> p k h", k=2, h=16)[:, 1]
            zre = s0.rearrange("p (h k) -> p k h", k=2, h=16)[:, 0]
            zim = s0.rearrange("p (h k) -> p k h", k=2, h=16)[:, 1]
            c16 = ccd.rearrange("p (h k) -> p k h", k=2, h=16)[:, 0]
            s16 = ssd.rearrange("p (h k) -> p k h", k=2, h=16)[:, 0]
            t16b = uu.rearrange("p (h k) -> p k h", k=2, h=16)[:, 0]
            mb.tt(zre, sre, c16, OP.mult)
            mb.tt(t16b, sim_, s16, OP.mult)
            mb.tt(zre, zre, t16b, OP.subtract)
            mb.tt(zim, sim_, c16, OP.mult)
            mb.tt(t16b, sre, s16, OP.mult)
            mb.tt(zim, zim, t16b, OP.add)
            mb.copy(s0bf, s0)
    return


def emit_l0_post(p, w, cs, scr, retr_tm, xn_tm, a0_tm):
    mb, T, NT = p.mb, p.T, p.NT
    with p.stage() as alloc:
        retrfm = alloc("retrfm", [32, T], BF16)
        rtb = alloc("rtb", [128, 32], BF16)
        for c in range(NT):
            mb.copy(rtb, retr_tm[:, c * 32:(c + 1) * 32])
            mb.transpose(p.psT[0:32, 0:128], rtb, p.cp.ident_bf)
            mb.copy(retrfm[:, c * 128:(c + 1) * 128], p.psT[0:32, 0:128])
        rw = wload_bf(p, alloc, w["readout_w"], 32, DI, "rw")
        ssum = alloc("ssum", [128, 16])
        ssq = alloc("ssq", [128, 16])
        cA = alloc("cA", [128, 1]); cB = alloc("cB", [128, 1])
        ybuf = alloc("ybuf", [128, 512], BF16)
        sqs = alloc("sqs", [128, 512])
        mb.memset(ssum, 0.0)
        mb.memset(ssq, 0.0)
        for s0 in range(0, T, 512):
            for o in range(16):
                mb.mm(p.psA, rw[0:32, o * 128:(o + 1) * 128], retrfm[:, s0:s0 + 512],
                      start=True, stop=True)
                mb.ts(ybuf, p.psA, 1.0, 0.0, OP.mult, OP.add, accum_out=cA)
                mb.act(sqs, p.psA, AF.Square, accum_out=cB)
                mb.dma(scr["yfm"][o * 128:(o + 1) * 128, s0:s0 + 512], ybuf, eng="pool")
                mb.tt(ssum[:, o:o + 1], ssum[:, o:o + 1], cA, OP.add)
                mb.tt(ssq[:, o:o + 1], ssq[:, o:o + 1], cB, OP.add)
        mb.mm(p.psB[0:1, 0:16], p.cp.ones_col, ssum, start=True, stop=True)
        mb.mm(p.psC[0:1, 0:16], p.cp.ones_col, ssq, start=True, stop=True)
        grow = alloc("grow", [1, 48])
        mb.ts(grow[:, 0:16], p.psB[0:1, 0:16], 1.0 / (T * 128), None, OP.mult)
        mb.ts(grow[:, 16:32], p.psC[0:1, 0:16], 1.0 / (T * 128), None, OP.mult)
        mb.act(grow[:, 32:48], grow[:, 0:16], AF.Square)
        mb.tt(grow[:, 16:32], grow[:, 16:32], grow[:, 32:48], OP.subtract)
        mb.ts(grow[:, 16:32], grow[:, 16:32], 1e-5, None, OP.add)
        mb.act(grow[:, 16:32], grow[:, 16:32], AF.Sqrt)
        mb.reciprocal(grow[:, 16:32], grow[:, 16:32])
        gcols = alloc("gcols", [128, 32])
        mb.mm(p.psB[:, 0:32], p.cp.ones_row, grow[:, 0:32], start=True, stop=True)
        mb.copy(gcols, p.psB[:, 0:32])
        wo = wload_bf(p, alloc, w["out_w"], DI, DM, "ow")
        gnw = bias_cols(p, alloc, w["gn_w"], DI, "gnw")
        gnb = bias_cols(p, alloc, w["gn_b"], DI, "gnb")
        dpc = bias_cols(p, alloc, w["D_param"], DI, "dp")
        y5 = alloc("y5", [128, 16 * 512], BF16)
        zb = alloc("zbuf", [128, 512])
        xcb = alloc("xcbuf", [128, 512], BF16)
        t0_ = alloc("t0", [128, 512])
        xnt = alloc("xnt", [128, DM])
        a0b = alloc("a0b", [128, DM])
        for s0 in range(0, T, 512):
            for o in range(16):
                mb.dma(ybuf, scr["yfm"][o * 128:(o + 1) * 128, s0:s0 + 512])
                mb.dma(zb, scr["zfm"][o * 128:(o + 1) * 128, s0:s0 + 512])
                mb.dma(xcb, scr["xcfm"][o * 128:(o + 1) * 128, s0:s0 + 512])
                mb.ts(t0_, ybuf, gcols[:, o:o + 1], gcols[:, 16 + o:17 + o], OP.subtract, OP.mult)
                mb.ts(t0_, t0_, gnw[:, o:o + 1], gnb[:, o:o + 1], OP.mult, OP.add)
                mb.act(zb, zb, AF.Silu)
                mb.tt(t0_, t0_, zb, OP.mult)
                mb.stt(t0_, xcb, dpc[:, o:o + 1], t0_, OP.mult, OP.add)
                mb.copy(y5[:, o * 512:(o + 1) * 512], t0_)
            for q in range(4):
                t = s0 // 128 + q
                mb.dma(xnt, xn_tm[t * 128:(t + 1) * 128, :])
                for oc in range(2):
                    for k in range(16):
                        mb.mm((p.psE[:, oc * 512:(oc + 1) * 512], f"o{oc}"),
                              y5[:, k * 512 + q * 128:k * 512 + q * 128 + 128],
                              wo[:, k * DM + oc * 512:k * DM + oc * 512 + 512],
                              start=(k == 0), stop=(k == 15))
                    mb.tt(a0b[:, oc * 512:(oc + 1) * 512], (p.psE[:, oc * 512:(oc + 1) * 512], f"o{oc}"),
                          xnt[:, oc * 512:(oc + 1) * 512], OP.add)
                mb.dma(a0_tm[t * 128:(t + 1) * 128, :], a0b, eng="pool")


def emit_ln_simple(p, xin_tm, out_fm, out_tm):
    mb = p.mb
    with p.stage() as alloc:
        xt = alloc("x", [128, DM])
        bs = alloc("bs", [128, 12])
        mv = alloc("mv", [128, 2])
        cols = alloc("cols", [128, 3])
        xn = alloc("xn", [128, DM])
        xnb = alloc("xnb", [128, DM], BF16)
        tp = alloc("tp", [128, 128], BF16)
        for t in range(p.NT):
            mb.dma(xt, xin_tm[t * 128:(t + 1) * 128, :])
            for s in range(2):
                mb.bn_stats(bs[:, 6 * s:6 * s + 6], xt[:, 512 * s:512 * (s + 1)])
            mb.bn_aggr(mv, bs)
            mb.ts(cols[:, 0:1], mv[:, 1:2], 1e-5, None, OP.add)
            mb.act(cols[:, 1:2], cols[:, 0:1], AF.Sqrt)
            mb.reciprocal(cols[:, 2:3], cols[:, 1:2])
            mb.ts(xn, xt, mv[:, 0:1], cols[:, 2:3], OP.subtract, OP.mult)
            if out_tm is not None:
                mb.dma(out_tm[t * 128:(t + 1) * 128, :], xn, eng="pool")
            mb.copy(xnb, xn)
            for c in range(DM // 128):
                mb.transpose(p.psT[:, 0:128], xnb[:, c * 128:(c + 1) * 128], p.cp.ident_bf)
                mb.copy(tp, p.psT[:, 0:128])
                mb.dma(out_fm[c * 128:(c + 1) * 128, t * 128:(t + 1) * 128], tp, eng="pool")


def emit_add3(p, a_tm, b_tm, c_tm, out_tm):
    mb = p.mb
    with p.stage() as alloc:
        ta = alloc("ta", [128, DM]); tb = alloc("tb", [128, DM]); tc = alloc("tc", [128, DM])
        for t in range(p.NT):
            mb.dma(ta, a_tm[t * 128:(t + 1) * 128, :])
            mb.dma(tb, b_tm[t * 128:(t + 1) * 128, :], eng="act")
            mb.dma(tc, c_tm[t * 128:(t + 1) * 128, :], eng="sp")
            mb.tt(ta, ta, tb, OP.add)
            mb.tt(ta, ta, tc, OP.add)
            mb.dma(out_tm[t * 128:(t + 1) * 128, :], ta, eng="pool")


def build_full(T):
    p = Prog(T)
    NT = p.NT
    x_d = p.din("x", [T, DM])
    names_l0a = ["rms_w", "in_proj_w", "in_proj_b", "conv_w", "conv_b", "gate_w", "gate_b",
                 "log_dt_scale", "Q_w", "readout_w", "out_w", "gn_w", "gn_b", "D_param"]
    names_l0f = ["enc_W", "enc_b", "s2c_W", "s2c_b", "rW1", "rb1", "rW2", "rb2", "We1", "be1",
                 "We2", "be2", "c2s_W", "c2s_b", "dec_W", "dec_b", "ln_g", "ln_b"]
    names_l1a = ["Wq", "bq", "Wk", "bk", "Wv", "bv", "Wa", "ba", "Wb", "bb", "Wg", "bg", "Wo", "bo"]
    names_l1f = ["Wqkv", "q_scale", "k_scale", "Wo", "ln_g", "ln_b", "W1", "b1", "W2", "b2"]
    SH = {
        "rms_w": (DM,), "in_proj_w": (DM, 6176), "in_proj_b": (6176,), "conv_w": (DI, 4),
        "conv_b": (DI,), "gate_w": (DI, 112), "gate_b": (112,), "log_dt_scale": (16,),
        "Q_w": (DI, DI), "readout_w": (32, DI), "out_w": (DI, DM), "gn_w": (DI,), "gn_b": (DI,),
        "D_param": (DI,),
        "enc_W": (DM, DI), "enc_b": (DI,), "s2c_W": (DI, 64), "s2c_b": (64,), "rW1": (64, 64),
        "rb1": (64,), "rW2": (64, 8), "rb2": (8,), "We1": (8, 64, 1024), "be1": (8, 1024),
        "We2": (8, 1024, 64), "be2": (8, 64), "c2s_W": (64, DI), "c2s_b": (DI,),
        "dec_W": (DI, DM), "dec_b": (DM,), "ln_g": (DM,), "ln_b": (DM,),
        "Wq": (DM, 128), "bq": (128,), "Wk": (DM, 128), "bk": (128,), "Wv": (DM, 128),
        "bv": (128,), "Wa": (DM, 1), "ba": (1,), "Wb": (DM, 1), "bb": (1,), "Wg": (DM, 128),
        "bg": (128,), "Wo": (128, DM), "bo": (DM,),
    }
    SH1F = {"Wqkv": (6, DM, 3072), "q_scale": (6, 64), "k_scale": (6, 64), "Wo": (6, 1024, DM),
            "ln_g": (6, DM), "ln_b": (6, DM), "W1": (6, DM, 5460), "b1": (6, 5460),
            "W2": (6, 2730, DM), "b2": (6, DM)}
    w0a = {k: p.din("L0_attn_" + k, SH[k]) for k in names_l0a}
    w0f = {k: p.din("L0_ffn_" + k, SH[k]) for k in names_l0f}
    w1a = {k: p.din("L1_attn_" + k, SH[k]) for k in names_l1a}
    w1f = {k: p.din("L1_ffn_" + k, SH1F[k]) for k in names_l1f}
    cs = {k: p.din(k, v.shape) for k, v in host_constants(T).items()}
    out_d = p.dout("out", [T, DM])
    scr = {
        "zfm": p.dscratch("zfm", [DI, T]),
        "xcfm": p.dscratch("xcfm", [DI, T], BF16),
        "knfm": p.dscratch("knfm", [DI, T], BF16),
        "kntm": p.dscratch("kntm", [T, DI], BF16),
        "qnfm": p.dscratch("qnfm", [DI, T], BF16),
        "yfm": p.dscratch("yfm", [DI, T], BF16),
    }
    xn_tm = p.dscratch("xn_tm", [T, DM])
    xnfm = p.dscratch("xnfm", [DM, T], BF16)
    xrfm = p.dscratch("xrfm", [DM, T], BF16)
    spike = p.dscratch("spike", [DI, T], BF16)
    spike2 = p.dscratch("spike2", [DI, T], BF16)
    a0_tm = p.dscratch("a0_tm", [T, DM])
    f0_tm = p.dscratch("f0_tm", [T, DM])
    x1_tm = p.dscratch("x1_tm", [T, DM])
    xn1fm = p.dscratch("xn1fm", [DM, T], BF16)
    a1_tm = p.dscratch("a1_tm", [T, DM])
    l1o = p.dscratch("l1o", [T, 128], BF16)
    l1x_tm = p.dscratch("l1x_tm", [T, DM])
    p.cp = CPool(p, cs)
    emit_ln0(p, x_d, xnfm, xrfm, xn_tm)
    with p.phase() as ph:
        p.salloc = ph
        retr = ph("retr", [128, NT * 32])
        emit_l0_attn(p, w0a, xn_tm, xrfm, a0_tm, cs, scr)
        emit_l0_gates_and_scan(p, w0a, cs, scr, retr)
        emit_l0_post(p, w0a, cs, scr, retr, xn_tm, a0_tm)
    with p.phase() as ph:
        p.salloc = ph
        emit_l0_ffn(p, w0f, xnfm, f0_tm, spike, spike2)
    emit_add3(p, x_d, a0_tm, f0_tm, x1_tm)
    emit_ln_simple(p, x1_tm, xn1fm, l1x_tm)
    emit_l1_attn(p, w1a, xn1fm, a1_tm, l1o, cs)
    for i in range(6):
        emit_l1_ffn_layer(p, w1f, i, l1x_tm, l1x_tm, cs)
    emit_add3(p, x1_tm, a1_tm, l1x_tm, out_d)
    p.mb.barrier("sp")
    return p


def kernel(**inputs):
    T = 4096
    p = build_full(T)
    p.mb.build()
    consts = host_constants(T)
    base = {}
    for name in p.inputs:
        if name == "x":
            continue
        if name in consts:
            base[name] = consts[name]
        else:
            base[name] = np.ascontiguousarray(np.asarray(inputs[name], dtype=np.float32))
    x = np.asarray(inputs["L0_attn_x"], dtype=np.float32)
    in_maps = []
    for c in range(8):
        m = dict(base)
        m["x"] = np.ascontiguousarray(x[c])
        in_maps.append(m)
    res = run_bass_kernel_spmd(p.nc, in_maps, core_ids=list(range(8)))
    out = np.stack([np.asarray(r["out"], dtype=np.float32) for r in res.results], axis=0)
    return out


# version 15 (final)
# speedup vs baseline: 1.2039x; 1.2039x over previous
import numpy as np
import concourse.bass as bass
import concourse.mybir as mybir
from concourse.bass_utils import run_bass_kernel_spmd

F32 = mybir.dt.float32
F32R = mybir.dt.float32r
BF16 = mybir.dt.bfloat16
I32 = mybir.dt.int32
AF = mybir.ActivationFunctionType
OP = mybir.AluOpType
AX = mybir.AxisListType

ENGINES = ("pe", "act", "dve", "pool", "sp")


def _keys_of(ap, region):
    name = ap.tensor.name
    return (name, region)


class MB:

    def __init__(self, nc):
        self.nc = nc
        self.ops = {e: [] for e in ENGINES}
        self.count = {e: 0 for e in ENGINES}
        self.NDMA = 4
        self.dcount = {(e, i): 0 for e in ENGINES for i in range(self.NDMA)}
        self.dma_rr = {e: 0 for e in ENGINES}
        self.known = {e: {} for e in ENGINES}
        self.writer = {}
        self.readers = {}
        self.sems = {}
        self._sem_handles = []

    def _need(self, eng, reads, writes):
        need = {}
        def add(pk, v):
            if pk == ("t", eng):
                if eng == "pe":
                    return
            if pk[0] == "d":
                v = 16 * self.dcount[(pk[1], pk[2])]
            if need.get(pk, 0) < v:
                need[pk] = v
        def wmatch(k):
            name, reg = k
            out = []
            for (n2, r2), wv in self.writer.items():
                if n2 == name and (reg == "*" or r2 == "*" or r2 == reg):
                    out.append(wv)
            return out
        def rmatch(k):
            name, reg = k
            out = []
            for (n2, r2), rd in self.readers.items():
                if n2 == name and (reg == "*" or r2 == "*" or r2 == reg):
                    out.extend(rd.items())
            return out
        for k in reads:
            for wv in wmatch(k):
                add(*wv)
        for k in writes:
            for wv in wmatch(k):
                add(*wv)
            for pk, v in rmatch(k):
                add(pk, v)
        waits = []
        kn = self.known[eng]
        for pk, v in sorted(need.items()):
            if pk == ("t", eng):
                if kn.get(pk, 0) >= v:
                    continue
                waits.append((pk, v))
                kn[pk] = v
            else:
                if kn.get(pk, 0) >= v:
                    continue
                waits.append((pk, v))
                kn[pk] = v
        return waits

    def _finish(self, eng, reads, writes, is_dma):
        if is_dma:
            slot = self.dma_rr[eng]
            self.dma_rr[eng] = (slot + 1) % self.NDMA
            self.dcount[(eng, slot)] += 1
            pk, v = ("d", eng, slot), 16 * self.dcount[(eng, slot)]
        else:
            self.count[eng] += 1
            pk, v = ("t", eng), self.count[eng]
        for k in writes:
            name, reg = k
            if reg == "*":
                for k2 in [x for x in self.writer if x[0] == name]:
                    del self.writer[k2]
                for k2 in [x for x in self.readers if x[0] == name]:
                    del self.readers[k2]
            self.writer[k] = (pk, v)
            self.readers[k] = {}
        for k in reads:
            self.readers.setdefault(k, {})[pk] = v
        return pk, v

    PSUM_BASE = {"psA": 0, "psB": 1, "psC": 2, "psE": 3, "psT": 7}

    def _bank_keys(self, x):
        ap = _ap(x)
        try:
            name = ap.tensor.name
        except Exception:
            return []
        if name not in self.PSUM_BASE:
            return []
        es = 2 if ap.dtype == BF16 else 4
        b0 = (ap.offset * es) // 2048
        b1 = (ap.offset * es + max(0, ap.free_nbytes() - 1)) // 2048
        base = self.PSUM_BASE[name]
        return [("#BANK", base + b) for b in range(int(b0), int(b1) + 1)]

    def op(self, eng, fn, reads=(), writes=(), region="*", rregions=None, wregions=None,
           is_dma=False):
        def key(x, i, regs):
            if isinstance(x, tuple):
                return _keys_of(x[0], x[1])
            r = regs[i] if regs else region
            return _keys_of(x, r)
        rk = [key(x, i, rregions) for i, x in enumerate(reads)]
        wk = [key(x, i, wregions) for i, x in enumerate(writes)]
        for x in reads:
            rk.extend(self._bank_keys(x))
        for x in writes:
            wk.extend(self._bank_keys(x))
        waits = self._need(eng, rk, wk)
        self._apply_fence(eng, waits)
        if is_dma:
            slot = self.dma_rr[eng]
            dk = ("d", eng, slot)
            prev = 16 * self.dcount[(eng, slot)]
            if prev > 0 and self.known[eng].get(dk, 0) < prev:
                waits.append((dk, prev))
                self.known[eng][dk] = prev
        pk, v = self._finish(eng, rk, wk, is_dma)
        self.ops[eng].append((waits, fn, pk, v))

    def build(self):
        nc = self.nc
        for e in ENGINES:
            self.sems[("t", e)] = nc.alloc_semaphore("tick_" + e)
            for i in range(self.NDMA):
                self.sems[("d", e, i)] = nc.alloc_semaphore(f"dt_{e}_{i}")
        with nc.Block() as blk:
            self.emit_into_block(blk)

    def barrier(self, eng="sp"):
        waits = []
        kn = self.known[eng]
        for e in ENGINES:
            pk = ("t", e)
            if self.count[e] > 0 and kn.get(pk, 0) < self.count[e] and not (e == eng):
                waits.append((pk, self.count[e])); kn[pk] = self.count[e]
            for i in range(self.NDMA):
                dk = ("d", e, i)
                v = 16 * self.dcount[(e, i)]
                if v > 0 and kn.get(dk, 0) < v:
                    waits.append((dk, v)); kn[dk] = v
        pk, v = self._finish(eng, [], [], False)
        self.ops[eng].append((waits, lambda E: E.nop(), pk, v))
        return v

    def barrier_all(self):
        v = self.barrier("sp")
        self.fence = (("t", "sp"), v)

    fence = None

    def _apply_fence(self, eng, waits):
        if self.fence is not None:
            pk, v = self.fence
            if not (pk == ("t", eng)) and self.known[eng].get(pk, 0) < v:
                waits.insert(0, (pk, v))
                self.known[eng][pk] = v

    def emit_into_block(self, blk):
        nc = self.nc
        for eng in ENGINES:
            body = self.ops[eng]
            if not body:
                continue
            def make(body=body, eng=eng):
                def run(engobj):
                    for waits, fn, pk, v in body:
                        for (wpk, wv) in waits:
                            engobj.wait_ge(self.sems[wpk], wv)
                        inst = fn(engobj)
                        inst.then_inc(self.sems[pk], 16 if pk[0] == "d" else 1)
                return run
            getattr(blk, {"pe": "tensor", "act": "scalar", "dve": "vector",
                          "pool": "gpsimd", "sp": "sync"}[eng])(make())

    def final_wait(self):
        pass

    def dma(self, dst, src, eng="sp"):
        self.op(eng, lambda e: e.dma_start(out=_ap(dst), in_=_ap(src)),
                reads=[src], writes=[dst], is_dma=True)

    def mm(self, out, lhsT, rhs, start=True, stop=True, tile_position=None):
        assert _ap(lhsT).dtype == _ap(rhs).dtype, \
            f"matmul dtype mismatch {_ap(lhsT).dtype} vs {_ap(rhs).dtype}"
        self.op("pe", lambda e: e.matmul(_ap(out), _ap(lhsT), _ap(rhs), start=start,
                                         stop=stop, tile_position=tile_position),
                reads=[lhsT, rhs], writes=[out])

    def transpose(self, out, in_, ident):
        self.op("pe", lambda e: e.transpose(_ap(out), _ap(in_), _ap(ident)),
                reads=[in_, ident], writes=[out])

    def act(self, out, in_, func, bias=None, scale=None, accum_out=None, eng="act"):
        if func == AF.Sqrt:
            assert bias is None and scale is None and accum_out is None
            self.ts(out, in_, 1e-37, None, OP.max)
            self.act(out, out, AF.Ln)
            self.act(out, out, AF.Exp, scale=0.5)
            return
        if func == AF.Silu:
            tmp = self.cp_tmp
            n = _ap(in_).shape[-1]
            assert n <= 512 and scale is None and accum_out is None
            if bias is not None:
                self.ts(tmp[:, 0:n], in_, bias, None, OP.add)
            else:
                self.copy(tmp[:, 0:n], in_)
            self.act(out, tmp[:, 0:n], AF.Sigmoid)
            self.tt(out, tmp[:, 0:n], out, OP.mult)
            return
        if func == AF.Square:
            assert bias is None and scale is None
            try:
                inpsum = _ap(in_).tensor.name.startswith("ps")
            except Exception:
                inpsum = False
            if accum_out is not None:
                if inpsum:
                    self.copy(out, in_)
                    self.stt(out, out, 1.0, out, OP.mult, OP.mult, accum_out=accum_out)
                else:
                    self.stt(out, in_, 1.0, in_, OP.mult, OP.mult, accum_out=accum_out)
                return
            if inpsum:
                self.copy(out, in_)
                self.tt(out, out, out, OP.mult)
            else:
                self.tt(out, in_, in_, OP.mult)
            return
        kw = {}
        if bias is not None:
            kw["bias"] = _ap(bias) if not isinstance(bias, (int, float)) else bias
        if scale is not None:
            kw["scale"] = scale
        if accum_out is not None:
            kw["accum_out"] = _ap(accum_out)
        reads = [in_] + ([bias] if hasattr(bias, "tensor") or isinstance(bias, tuple) else [])
        writes = [out] + ([accum_out] if accum_out is not None else [])
        self.op(eng, lambda e: e.activation(_ap(out), _ap(in_), func, **kw),
                reads=reads, writes=writes)

    def tt(self, out, a, b, op, eng="dve"):
        self.op(eng, lambda e: e.tensor_tensor(_ap(out), _ap(a), _ap(b), op),
                reads=[a, b], writes=[out])

    def ts(self, out, a, s1, s2, op0, op1=None, accum_out=None, eng="dve"):
        reads = [a] + [x for x in (s1, s2) if hasattr(x, "tensor") or isinstance(x, tuple)]
        writes = [out] + ([accum_out] if accum_out is not None else [])
        kw = {}
        if accum_out is not None:
            kw["accum_out"] = _ap(accum_out)
        o1 = op1 if op1 is not None else OP.bypass
        def f(e):
            return e.tensor_scalar(_ap(out), _ap(a), _maybe_ap(s1), _maybe_ap(s2), op0, o1, **kw)
        self.op(eng, f, reads=reads, writes=writes)

    def stt(self, out, a, s, b, op0, op1, accum_out=None, eng="dve"):
        reads = [a, b] + ([s] if hasattr(s, "tensor") or isinstance(s, tuple) else [])
        writes = [out] + ([accum_out] if accum_out is not None else [])
        kw = {}
        if accum_out is not None:
            kw["accum_out"] = _ap(accum_out)
        self.op(eng, lambda e: e.scalar_tensor_tensor(_ap(out), _ap(a), _maybe_ap(s), _ap(b), op0, op1, **kw),
                reads=reads, writes=writes)

    def copy(self, out, in_, eng="dve"):
        if eng == "act":
            self.op(eng, lambda e: e.activation(_ap(out), _ap(in_), AF.Copy), reads=[in_], writes=[out])
        else:
            self.op(eng, lambda e: e.tensor_copy(_ap(out), _ap(in_)), reads=[in_], writes=[out])

    def reduce(self, out, in_, op, axis=AX.X, negate=None, eng="dve"):
        self.op(eng, lambda e: e.tensor_reduce(_ap(out), _ap(in_), axis, op, negate=negate),
                reads=[in_], writes=[out])

    def reciprocal(self, out, in_):
        self.op("dve", lambda e: e.reciprocal(_ap(out), _ap(in_)), reads=[in_], writes=[out])

    def select(self, out, mask, a, b):
        self.op("dve", lambda e: e.select(_ap(out), _ap(mask), _ap(a), _ap(b), add_drain=True),
                reads=[mask, a, b], writes=[out])

    def memset(self, out, val, eng="dve"):
        self.op(eng, lambda e: e.memset(_ap(out), val), reads=[], writes=[out])

    def bn_stats(self, out, in_):
        self.op("dve", lambda e: e.bn_stats(_ap(out), _ap(in_)), reads=[in_], writes=[out])

    def bn_aggr(self, out, in_):
        self.op("dve", lambda e: e.bn_aggr(_ap(out), _ap(in_)), reads=[in_], writes=[out])

    def scan(self, out, d0, d1, initial, op0, op1):
        rd = [d0, d1] + ([initial] if hasattr(initial, "tensor") or isinstance(initial, tuple) else [])
        self.op("dve", lambda e: e.tensor_tensor_scan(_ap(out), _ap(d0), _ap(d1),
                                                      _maybe_ap(initial), op0, op1),
                reads=rd, writes=[out])


def _ap(x):
    if isinstance(x, tuple):
        x = x[0]
    return x


def _maybe_ap(x):
    if x is None:
        return None
    if isinstance(x, (int, float)):
        return x
    return _ap(x)


from contextlib import ExitStack, contextmanager


class StopStage(Exception):
    pass

DM = 1024
DI = 2048
NH = 16
HD = 128


def ceil_div(a, b):
    return -(-a // b)


class Prog:

    def __init__(self, T):
        self.T = T
        self.NT = T // 128
        self.nc = bass.Bass("TRN2", target_bir_lowering=False)
        self.mb = MB(self.nc)
        self.inputs = {}
        self._uid = 0
        self.psA = self.nc.alloc_psum_tensor("psA", [128, 512], F32).ap()
        self.psB = self.nc.alloc_psum_tensor("psB", [128, 512], F32).ap()
        self.psC = self.nc.alloc_psum_tensor("psC", [128, 512], F32).ap()
        self.psE = self.nc.alloc_psum_tensor("psE", [128, 2048], F32).ap()
        self.psT = self.nc.alloc_psum_tensor("psT", [128, 512], BF16).ap()

    def uid(self, p):
        self._uid += 1
        return f"{p}{self._uid}"

    def din(self, name, shape, dt=F32):
        self.inputs[name] = (list(shape), dt)
        return self.nc.dram_tensor(name, list(shape), dt, kind="ExternalInput").ap()

    def dout(self, name, shape, dt=F32):
        return self.nc.dram_tensor(name, list(shape), dt, kind="ExternalOutput").ap()

    def dscratch(self, name, shape, dt=F32):
        return self.nc.dram_tensor(name, list(shape), dt, kind="Internal").ap()

    def sb(self, name, shape, dt=F32):
        return self.nc.alloc_sbuf_tensor(name, list(shape), dt).ap()

    @contextmanager
    def phase(self):
        try:
            with ExitStack() as es:
                def alloc(name, shape, dt=F32):
                    t = es.enter_context(self.nc.sbuf_tensor(self.uid("ph_") + name, list(shape), dt))
                    return t.ap()
                yield alloc
                self.mb.barrier_all()
        except StopStage:
            pass

    @contextmanager
    def stage(self):
        try:
            with ExitStack() as es:
                def alloc(name, shape, dt=F32):
                    t = es.enter_context(self.nc.sbuf_tensor(self.uid("st_") + name, list(shape), dt))
                    return t.ap()
                yield alloc
                self.mb.barrier_all()
        except StopStage:
            pass


def host_constants(T):
    C = {}
    i, j = np.mgrid[0:128, 0:128]
    C["c_ident"] = np.eye(128, dtype=np.float32)
    C["c_cumul_st"] = (i <= j).astype(np.float32)
    C["c_mask_lt_neg"] = np.where(i > j, -1.0, 0.0).astype(np.float32)
    C["c_mask_st_neg"] = np.where(i < j, -1.0, 0.0).astype(np.float32)
    C["c_mask_le"] = (i <= j).astype(np.float32)
    C["c_ones_col"] = np.ones((128, 1), np.float32)
    C["c_ones_row"] = np.ones((1, 128), np.float32)
    f64 = 1.0 / (10000.0 ** (np.arange(16) / 16.0))
    C["c_l0_freqs"] = np.tile(f64.astype(np.float32)[None, :], (128, T // 128))
    C["c_mask_st_neg16"] = np.tile(np.where(i < j, -1.0, 0.0).astype(np.float32), (1, 16))
    C["c_mask_le16"] = np.tile((i <= j).astype(np.float32), (1, 16))
    blk = ((i // 64) == (j // 64)).astype(np.float32)
    C["c_blkdiag16"] = np.tile(blk, (1, 16))
    C["c_identrep16"] = np.tile(np.eye(128, dtype=np.float32), (1, 16))
    DIM_HEAD, WINDOW = 64, 256
    half = np.arange(0, DIM_HEAD, 2, dtype=np.float64)
    inv_freq = 1.0 / 10000.0 ** (half / DIM_HEAD)
    t = np.arange(T, dtype=np.float64)
    freqs = t[:, None] * inv_freq[None, :]
    freqs = np.concatenate([freqs, freqs], -1)
    base_scale = (half + 0.4 * DIM_HEAD) / (1.4 * DIM_HEAD)
    power = (t - T // 2) / (WINDOW // 2)
    scale = np.concatenate([base_scale[None, :] ** power[:, None]] * 2, -1)
    cos, sin = np.cos(freqs), np.sin(freqs)
    sgn = np.ones((1, 64)); sgn[0, :32] = -1.0
    C["c_rotq_c"] = (cos * scale).astype(np.float32)
    C["c_rotq_s"] = (sin * scale * sgn).astype(np.float32)
    C["c_rotk_c"] = (cos / scale).astype(np.float32)
    C["c_rotk_s"] = (sin / scale * sgn).astype(np.float32)
    ii, jj = np.mgrid[0:128, 0:384]
    C["c_am_full"] = np.where((jj >= ii) & (jj <= ii + 256), 0.0, -1e30).astype(np.float32)
    ii, jj = np.mgrid[0:128, 0:256]
    C["c_am_t1"] = np.where(jj <= ii + 128, 0.0, -1e30).astype(np.float32)
    ii, jj = np.mgrid[0:128, 0:128]
    C["c_am_t0"] = np.where(jj <= ii, 0.0, -1e30).astype(np.float32)
    for nm in ("full", "t1", "t0"):
        arr = C["c_am_" + nm]
        for kc in range(arr.shape[1] // 128):
            C[f"c_amT_{nm}{kc}"] = np.ascontiguousarray(arr[:, kc * 128:(kc + 1) * 128].T) - 16.0
    return C


class CPool:
    def __init__(self, p, cs):
        mb = p.mb
        self.ident = p.sb("k_ident", [128, 128], F32)
        self.ident_bf = p.sb("k_ident_bf", [128, 128], BF16)
        self.cumul = p.sb("k_cumul", [128, 128], F32)
        self.mask_lt_neg = p.sb("k_mlt", [128, 128], F32)
        self.mask_st_neg = p.sb("k_mst", [128, 128], F32)
        self.mask_le = p.sb("k_mle", [128, 128], F32)
        self.ones_col = p.sb("k_ones_col", [128, 1], F32)
        self.ones_row = p.sb("k_ones_row", [1, 128], F32)
        self.ones_row_bf = p.sb("k_ones_row_bf", [1, 128], BF16)
        self.ones_col_bf = p.sb("k_ones_col_bf", [128, 1], BF16)
        self.l0_freqs = p.sb("k_l0f", [128, (p.T // 128) * 16], F32)
        for dst, src in [(self.ident, cs["c_ident"]), (self.cumul, cs["c_cumul_st"]),
                         (self.mask_lt_neg, cs["c_mask_lt_neg"]), (self.mask_st_neg, cs["c_mask_st_neg"]),
                         (self.mask_le, cs["c_mask_le"]), (self.ones_col, cs["c_ones_col"]),
                         (self.ones_row, cs["c_ones_row"]), (self.l0_freqs, cs["c_l0_freqs"])]:
            mb.dma(dst, src)
        mb.copy(self.ident_bf, self.ident)
        mb.copy(self.ones_row_bf, self.ones_row)
        mb.copy(self.ones_col_bf, self.ones_col)
        mb.cp_tmp = p.sb("k_tmp512", [128, 512], F32)
        self.kconst = p.sb("k_consts", [128, 2], F32)
        mb.memset(self.kconst[:, 0:1], 1.0)
        mb.memset(self.kconst[:, 1:2], float(np.pi / 2))
        p.nc.const_aps.register(F32, 1.0, self.kconst[:, 0:1])
        p.nc.const_aps.register(F32, float(np.pi / 2), self.kconst[:, 1:2])


def dma_slow(p, dst, src, eng="sp"):
    p.mb.op(eng, lambda e: e.dma_start(out=_ap(dst), in_=_ap(src), allow_slow_non_contiguous=True),
            reads=[src], writes=[dst], is_dma=True)


def wload_bf(p, alloc, wdram, Kdim, Odim, name, chunk=1024):
    mb = p.mb
    nk = ceil_div(Kdim, 128)
    pk = min(128, Kdim)
    wbf = alloc("w" + name, [pk, nk * Odim], BF16)
    chunk = min(Odim, chunk)
    stg = alloc("ws" + name, [pk, 2 * chunk], F32)
    idx = 0
    for k in range(nk):
        r0, r1 = k * 128, min((k + 1) * 128, Kdim)
        for o0 in range(0, Odim, chunk):
            o1 = min(o0 + chunk, Odim)
            sl = (idx % 2) * chunk
            idx += 1
            s = stg[0:r1 - r0, sl:sl + o1 - o0]
            mb.dma((s, f"g{sl}"), wdram[r0:r1, o0:o1])
            mb.copy(wbf[0:r1 - r0, k * Odim + o0:k * Odim + o1], (s, f"g{sl}"))
    return wbf


def bias_cols(p, alloc, vdram, n, name):
    nt = ceil_div(n, 128)
    out = alloc("bc" + name, [128, nt], F32)
    dma_slow(p, out[0:min(128, n), 0:nt], vdram.rearrange("(a b) -> b a", b=min(128, n)))
    return out


def bias_col64(p, alloc, vdram, n, name):
    out = alloc("b64" + name, [n, 1], F32)
    p.mb.dma(out, vdram.rearrange("(n o) -> n o", o=1))
    return out


def rep_row(p, alloc, vdram, n, name):
    row = alloc("rr" + name, [1, n], F32)
    p.mb.dma(row, vdram.rearrange("(o n) -> o n", o=1))
    out = alloc("rp" + name, [128, n], F32)
    for o0 in range(0, n, 512):
        o1 = min(o0 + 512, n)
        p.mb.mm(p.psC[:, 0:o1 - o0], p.cp.ones_row, row[:, o0:o1], start=True, stop=True)
        p.mb.copy(out[:, o0:o1], p.psC[:, 0:o1 - o0])
    return out


def emit_ln0(p, x_tm, xnfm, xrfm, xn_tm=None):
    mb = p.mb
    with p.stage() as alloc:
        xts = [alloc(f"x{j}", [128, DM]) for j in range(2)]
        bss = [alloc(f"bs{j}", [128, 6 * 2]) for j in range(2)]
        mvs = [alloc(f"mv{j}", [128, 2]) for j in range(2)]
        colss = [alloc(f"cols{j}", [128, 4]) for j in range(2)]
        xns = [alloc(f"xn{j}", [128, DM]) for j in range(2)]
        sq = alloc("sq", [128, DM])
        ss = alloc("ss", [128, 1])
        xnbs = [alloc(f"xnb{j}", [128, DM], BF16) for j in range(2)]
        xrbs = [alloc(f"xrb{j}", [128, DM], BF16) for j in range(2)]
        tps = [alloc(f"tp{j}", [128, 128], BF16) for j in range(4)]
        for t in range(p.NT):
            j = t % 2
            xt, bs, mv, cols, xn, xnb, xrb = xts[j], bss[j], mvs[j], colss[j], xns[j], xnbs[j], xrbs[j]
            mb.dma(xt, x_tm[t * 128:(t + 1) * 128, :])
            for s in range(2):
                mb.bn_stats(bs[:, 6 * s:6 * s + 6], xt[:, 512 * s:512 * (s + 1)])
            mb.bn_aggr(mv, bs)
            mb.ts(cols[:, 0:1], mv[:, 1:2], 1e-5, None, OP.add)
            mb.act(cols[:, 1:2], cols[:, 0:1], AF.Sqrt)
            mb.reciprocal(cols[:, 2:3], cols[:, 1:2])
            mb.ts(xn, xt, mv[:, 0:1], cols[:, 2:3], OP.subtract, OP.mult)
            if xn_tm is not None:
                mb.dma(xn_tm[t * 128:(t + 1) * 128, :], xn, eng="act")
            mb.act(sq, xn, AF.Square, accum_out=ss)
            mb.ts(cols[:, 0:1], ss, 1.0 / DM, 1e-6, OP.mult, OP.add)
            mb.act(cols[:, 1:2], cols[:, 0:1], AF.Sqrt)
            mb.reciprocal(cols[:, 3:4], cols[:, 1:2])
            mb.copy(xnb, xn)
            mb.ts(xrb, xn, cols[:, 3:4], None, OP.mult)
            for c in range(DM // 128):
                for si, (src, dstd) in enumerate(((xnb, xnfm), (xrb, xrfm))):
                    sl = (c % 2) * 2 + si
                    ps = p.psT[:, sl * 128:(sl + 1) * 128]
                    mb.transpose((ps, f"t{sl}"), src[:, c * 128:(c + 1) * 128], p.cp.ident_bf)
                    mb.copy(tps[sl], (ps, f"t{sl}"))
                    mb.dma(dstd[c * 128:(c + 1) * 128, t * 128:(t + 1) * 128], tps[sl], eng="pool")


def emit_l0_ffn(p, w, xnfm, f0_tm, spike_bf, spike2_bf):
    mb, T, NT = p.mb, p.T, p.NT
    H = DI
    with p.stage() as alloc:
        encw = wload_bf(p, alloc, w["enc_W"], DM, H, "encW")
        encb = bias_cols(p, alloc, w["enc_b"], H, "encb")
        xs = alloc("x", [128, 8 * 512], BF16)
        ot = alloc("o", [128, 512], BF16)
        ot2 = alloc("o2", [128, 512], BF16)
        for s in range(0, T, 512):
            for k in range(8):
                mb.dma(xs[:, k * 512:(k + 1) * 512], xnfm[k * 128:(k + 1) * 128, s:s + 512])
            for o in range(16):
                pse = p.psA if o % 2 == 0 else p.psB
                for k in range(8):
                    mb.mm(pse, encw[:, k * H + o * 128:k * H + o * 128 + 128],
                          xs[:, k * 512:(k + 1) * 512], start=(k == 0), stop=(k == 7))
                mb.act(ot if o % 2 == 0 else ot2, pse, AF.Sigmoid, bias=encb[:, o:o + 1])
                mb.dma(spike_bf[o * 128:(o + 1) * 128, s:s + 512], ot if o % 2 == 0 else ot2, eng="pool")
    p.cont_bf = p.salloc("l0f_cont", [64, T], BF16)
    p.gatesw = p.salloc("l0f_gatesw", [128, p.NT * 8], F32)
    with p.stage() as alloc:
        s2cw = wload_bf(p, alloc, w["s2c_W"], H, 64, "s2cW")
        s2cb = bias_col64(p, alloc, w["s2c_b"], 64, "s2cb")
        xs = alloc("x", [128, 16 * 512], BF16)
        cont = alloc("cont", [64, 512])
        for s in range(0, T, 512):
            for k in range(16):
                mb.dma(xs[:, k * 512:(k + 1) * 512], spike_bf[k * 128:(k + 1) * 128, s:s + 512])
            for k in range(16):
                mb.mm(p.psA[0:64, :], s2cw[:, k * 64:(k + 1) * 64],
                      xs[:, k * 512:(k + 1) * 512], start=(k == 0), stop=(k == 15))
            mb.ts(cont, p.psA[0:64, :], s2cb, None, OP.add)
            mb.copy(p.cont_bf[:, s:s + 512], cont)
        rw1 = wload_bf(p, alloc, w["rW1"], 64, 64, "rW1")
        rw2 = wload_bf(p, alloc, w["rW2"], 64, 8, "rW2")
        rb1 = bias_col64(p, alloc, w["rb1"], 64, "rb1")
        rb2rep = rep_row(p, alloc, w["rb2"], 8, "rb2")
        th_bf = alloc("th", [64, T], BF16)
        for s in range(0, T, 512):
            mb.mm(p.psA[0:64, :], rw1, p.cont_bf[:, s:s + 512], start=True, stop=True)
            mb.act(th_bf[:, s:s + 512], p.psA[0:64, :], AF.Tanh, bias=rb1)
        logits = alloc("logits", [128, NT * 8])
        for t in range(NT):
            mb.mm(p.psB[:, 0:8], th_bf[:, t * 128:(t + 1) * 128], rw2, start=True, stop=True)
            mb.tt(logits[:, t * 8:(t + 1) * 8], p.psB[:, 0:8], rb2rep[:, 0:8], OP.add)
        E8 = 8
        lgv = logits.rearrange("p (t e) -> p t e", e=E8)
        mx = alloc("mx", [128, NT]); sums = alloc("sums", [128, NT])
        sm = alloc("sm", [128, NT * 8]); smv = sm.rearrange("p (t e) -> p t e", e=E8)
        m1 = alloc("m1", [128, NT]); m2 = alloc("m2", [128, NT])
        ism = alloc("ism", [128, NT * 8]); ismv = ism.rearrange("p (t e) -> p t e", e=E8)
        cum = alloc("cum", [128, NT * 8]); cumv = cum.rearrange("p (t e) -> p t e", e=E8)
        first = alloc("first", [128, NT * 8])
        tmp = alloc("tmp", [128, NT * 8]); tmpv = tmp.rearrange("p (t e) -> p t e", e=E8)
        sec = alloc("sec", [128, NT * 8])
        den = alloc("den", [128, NT])
        mb.reduce(mx, lgv, OP.max)
        for t in range(NT):
            mb.ts(sm[:, t * 8:(t + 1) * 8], logits[:, t * 8:(t + 1) * 8], mx[:, t:t + 1], None, OP.subtract)
        mb.act(sm, sm, AF.Exp)
        mb.reduce(m1, smv, OP.max)
        for t in range(NT):
            mb.ts(ism[:, t * 8:(t + 1) * 8], sm[:, t * 8:(t + 1) * 8], m1[:, t:t + 1], None, OP.is_ge)
        mb.memset(cum, 0.0)
        for e in range(1, E8):
            mb.tt(cumv[:, :, e:e + 1], cumv[:, :, e - 1:e], ismv[:, :, e - 1:e], OP.add)
        mb.ts(tmp, cum, 0.5, None, OP.is_lt)
        mb.tt(first, ism, tmp, OP.mult)
        mb.ts(tmp, first, -1e30, None, OP.mult)
        mb.tt(tmp, sm, tmp, OP.add)
        mb.reduce(m2, tmpv, OP.max)
        for t in range(NT):
            mb.ts(ism[:, t * 8:(t + 1) * 8], tmp[:, t * 8:(t + 1) * 8], m2[:, t:t + 1], None, OP.is_ge)
        mb.memset(cum, 0.0)
        for e in range(1, E8):
            mb.tt(cumv[:, :, e:e + 1], cumv[:, :, e - 1:e], ismv[:, :, e - 1:e], OP.add)
        mb.ts(cum, cum, 0.5, None, OP.is_lt)
        mb.tt(sec, ism, cum, OP.mult)
        mb.tt(den, m1, m2, OP.add)
        mb.reciprocal(den, den)
        mb.tt(tmp, first, sec, OP.add)
        mb.tt(tmp, tmp, sm, OP.mult)
        for t in range(NT):
            mb.ts(p.gatesw[:, t * 8:(t + 1) * 8], tmp[:, t * 8:(t + 1) * 8], den[:, t:t + 1], None, OP.mult)
    p.acc_tm = p.salloc("l0f_acc", [128, NT * 64], F32)
    mb.memset(p.acc_tm, 0.0)
    with p.stage() as alloc:
        h = alloc("h", [128, 8 * 512], BF16)
        eo = alloc("eo", [64, 512])
        for e in range(8):
            we1 = wload_bf(p, alloc, w["We1"][e], 64, 1024, f"We1_{e}")
            we2 = wload_bf(p, alloc, w["We2"][e], 1024, 64, f"We2_{e}")
            be1 = bias_cols(p, alloc, w["be1"][e], 1024, f"be1_{e}")
            be2 = bias_col64(p, alloc, w["be2"][e], 64, f"be2_{e}")
            for s in range(0, T, 512):
                for o in range(8):
                    mb.mm(p.psA, we1[:, o * 128:(o + 1) * 128], p.cont_bf[:, s:s + 512],
                          start=True, stop=True)
                    mb.act(h[:, o * 512:(o + 1) * 512], p.psA, AF.Relu, bias=be1[:, o:o + 1])
                for k in range(8):
                    mb.mm(p.psB[0:64, :], we2[:, k * 64:(k + 1) * 64], h[:, k * 512:(k + 1) * 512],
                          start=(k == 0), stop=(k == 7))
                mb.ts(eo, p.psB[0:64, :], be2, None, OP.add)
                for q in range(4):
                    ti = s // 128 + q
                    mb.transpose(p.psC[:, 0:64], eo[:, q * 128:(q + 1) * 128], p.cp.ident[0:64, 0:64])
                    mb.stt(p.acc_tm[:, ti * 64:(ti + 1) * 64], p.psC[:, 0:64],
                           p.gatesw[:, ti * 8 + e:ti * 8 + e + 1],
                           p.acc_tm[:, ti * 64:(ti + 1) * 64], OP.mult, OP.add)
    with p.stage() as alloc:
        accfm = alloc("accfm", [64, T], BF16)
        for t in range(NT):
            mb.transpose(p.psA[0:64, 0:128], p.acc_tm[:, t * 64:(t + 1) * 64], p.cp.ident)
            mb.copy(accfm[:, t * 128:(t + 1) * 128], p.psA[0:64, 0:128])
        c2sw = wload_bf(p, alloc, w["c2s_W"], 64, H, "c2sW")
        c2sb = bias_cols(p, alloc, w["c2s_b"], H, "c2sb")
        ot = alloc("o", [128, 512], BF16)
        for s in range(0, T, 512):
            for o in range(16):
                mb.mm(p.psA, c2sw[:, o * 128:(o + 1) * 128], accfm[:, s:s + 512],
                      start=True, stop=True)
                mb.act(ot, p.psA, AF.Sigmoid, bias=c2sb[:, o:o + 1])
                mb.dma(spike2_bf[o * 128:(o + 1) * 128, s:s + 512], ot, eng="pool")
    with p.stage() as alloc:
        decw = wload_bf(p, alloc, w["dec_W"], H, DM, "decW")
        decb = bias_cols(p, alloc, w["dec_b"], DM, "decb")
        lng = rep_row(p, alloc, w["ln_g"], DM, "lng")
        lnb = rep_row(p, alloc, w["ln_b"], DM, "lnb")
        xs = alloc("x", [128, 16 * 512], BF16)
        df = alloc("dec", [128, 8 * 512])
        dtm = alloc("dtm", [128, DM])
        st = alloc("st", [128, 12])
        st2 = alloc("st2", [128, 8])
        for s in range(0, T, 512):
            for k in range(16):
                mb.dma(xs[:, k * 512:(k + 1) * 512], spike2_bf[k * 128:(k + 1) * 128, s:s + 512])
            for o in range(8):
                psd = p.psA if o % 2 == 0 else p.psB
                for k in range(16):
                    mb.mm(psd, decw[:, k * DM + o * 128:k * DM + o * 128 + 128],
                          xs[:, k * 512:(k + 1) * 512], start=(k == 0), stop=(k == 15))
                mb.act(df[:, o * 512:(o + 1) * 512], psd, AF.Sigmoid, bias=decb[:, o:o + 1])
            for q in range(4):
                for c in range(8):
                    mb.transpose(p.psC[:, 0:128], df[:, c * 512 + q * 128:c * 512 + (q + 1) * 128], p.cp.ident)
                    mb.copy(dtm[:, c * 128:(c + 1) * 128], p.psC[:, 0:128])
                for g in range(2):
                    mb.bn_stats(st[:, 6 * g:6 * g + 6], dtm[:, 512 * g:512 * (g + 1)])
                mb.bn_aggr(st2[:, 0:2], st)
                mb.ts(st2[:, 2:3], st2[:, 1:2], 1e-5, None, OP.add)
                mb.act(st2[:, 3:4], st2[:, 2:3], AF.Sqrt)
                mb.reciprocal(st2[:, 4:5], st2[:, 3:4])
                mb.ts(dtm, dtm, st2[:, 0:1], st2[:, 4:5], OP.subtract, OP.mult)
                mb.tt(dtm, dtm, lng, OP.mult)
                mb.tt(dtm, dtm, lnb, OP.add)
                mb.dma(f0_tm[(s // 128 + q) * 128:(s // 128 + q + 1) * 128, :], dtm, eng="pool")


def emit_l1_ffn_layer(p, w, i, xin_tm, xout_tm, cs, parts=("attn", "ffn")):
    mb, T, NT = p.mb, p.T, p.NT
    HDm, INNER = 64, 1024
    DIN2, DIN = 5460, 2730
    QKS = 8.0

    def fm_tile(alloc, with_f32=True):
        return (alloc("xfmt", [128, 8 * 129], BF16),
                alloc("xfmtf", [128, 8 * 129]) if with_f32 else None)

    def load_fm_tile(xfmt2, xt, first_tile):
        xb, xf = xfmt2
        for buf in (xb, xf):
            if buf is None:
                continue
            for c in range(8):
                if first_tile:
                    mb.memset(buf[:, c * 129:c * 129 + 1], 0.0)
                else:
                    mb.copy(buf[:, c * 129:c * 129 + 1], buf[:, c * 129 + 128:c * 129 + 129])
        for c in range(8):
            mb.transpose(p.psA[:, 0:128], xt[:, c * 128:(c + 1) * 128], p.cp.ident)
            mb.copy(xb[:, c * 129 + 1:c * 129 + 129], p.psA[:, 0:128], eng="act")
            if xf is not None:
                mb.copy(xf[:, c * 129 + 1:c * 129 + 129], p.psA[:, 0:128], eng="act")

    def stt_tile(xfmt2, c, t, f32=False):
        off = 1 if c < 4 else 0
        buf = xfmt2[1] if f32 else xfmt2[0]
        return buf[:, c * 129 + off:c * 129 + off + 128]

    with p.stage() as alloc:
        if "attn" not in parts:
            raise StopStage
        wqkv = wload_bf(p, alloc, w["Wqkv"][i], DM, 3 * INNER, "wqkv", chunk=512)
        wo = wload_bf(p, alloc, w["Wo"][i], INNER, DM, "wo", chunk=512)
        qs_rep = rep_row(p, alloc, w["q_scale"][i], HDm, "qs")
        ks_rep = rep_row(p, alloc, w["k_scale"][i], HDm, "ks")
        qs_sw = alloc("qssw", [128, HDm]); ks_sw = alloc("kssw", [128, HDm])
        for hh in range(2):
            mb.copy(qs_sw[:, hh * 32:(hh + 1) * 32], qs_rep[:, (1 - hh) * 32:(2 - hh) * 32])
            mb.copy(ks_sw[:, hh * 32:(hh + 1) * 32], ks_rep[:, (1 - hh) * 32:(2 - hh) * 32])
        rtab = {nm: alloc(nm, [128, NT * 64], BF16) for nm in ("qc", "qs", "kc", "ks")}
        tf = alloc("tf", [128, 4 * 64])
        for t in range(NT):
            for ni, (nm, cst, sc) in enumerate((("qc", "c_rotq_c", qs_rep), ("qs", "c_rotq_s", qs_sw),
                                                ("kc", "c_rotk_c", ks_rep), ("ks", "c_rotk_s", ks_sw))):
                tfs = tf[:, ni * 64:(ni + 1) * 64]
                mb.dma((tfs, f"f{ni}"), cs[cst][t * 128:(t + 1) * 128, :])
                mb.tt(rtab[nm][:, t * 64:(t + 1) * 64], (tfs, f"f{ni}"), sc, OP.mult)
        amT = {}
        for nm, nch_ in (("full", 3), ("t1", 2), ("t0", 1)):
            for kc in range(nch_):
                amT[(nm, kc)] = alloc(f"amT{nm}{kc}", [128, 128])
                mb.dma(amT[(nm, kc)], cs[f"c_amT_{nm}{kc}"])
        KR = 5
        xts = [alloc(f"xt{j}", [128, DM]) for j in range(3)]
        xfmts = [alloc(f"xfmt{j}", [128, 8 * 129], BF16) for j in range(2)]
        kring = alloc("kring", [128, 8 * KR * 128], BF16)
        vring = alloc("vring", [128, KR * 16 * 65], BF16)
        mb.memset(kring, 0.0)
        mb.memset(vring, 1.0)
        qkv = alloc("qkv", [128, 3 * INNER])
        swp = alloc("swp", [128, INNER], BF16)
        tmpq = alloc("tmpq", [128, INNER], BF16)
        sumsq = alloc("sumsq", [128, 16])
        rcol = alloc("rcol", [128, 16])
        qn = alloc("qn", [128, INNER], BF16)
        q2s = [alloc(f"q2{j}", [128, INNER]) for j in range(2)]
        k2s = [alloc(f"k2{j}", [128, INNER]) for j in range(2)]
        qfms = [alloc(f"qfm{j}", [128, INNER], BF16) for j in range(2)]
        PT_f = alloc("PTf", [128, 3 * 384])
        PT_bf = alloc("PTbf", [128, 3 * 384], BF16)
        rcps = alloc("rcps", [128, 6])
        ofmt = alloc("ofmt", [128, INNER])
        ofm = alloc("ofm", [128, INNER], BF16)
        otm = alloc("otm", [128, DM])

        def l2n(srcsl, dst):
            mb.act(swp, srcsl, AF.Square)
            mb.reduce(sumsq, swp.rearrange("p (h d) -> p h d", d=64), OP.add)
            mb.act(sumsq, sumsq, AF.Sqrt)
            mb.ts(sumsq, sumsq, 1e-12, None, OP.max)
            mb.reciprocal(rcol, sumsq)
            rb = rcol.unsqueeze(2).broadcast_to([128, 16, 64])
            mb.tt(dst.rearrange("p (h d) -> p h d", d=64), srcsl.rearrange("p (h d) -> p h d", d=64), rb, OP.mult)

        def rot(dst, src, ct, st_, t):
            sv = swp.rearrange("p (h u d) -> p h u d", u=2, d=32)
            qv = src.rearrange("p (h u d) -> p h u d", u=2, d=32)
            mb.copy(sv, qv[:, :, ::-1, :])
            cb = ct[:, t * 64:(t + 1) * 64].unsqueeze(1).broadcast_to([128, 16, 64])
            sb = st_[:, t * 64:(t + 1) * 64].unsqueeze(1).broadcast_to([128, 16, 64])
            mb.tt(tmpq.rearrange("p (h d) -> p h d", d=64), src.rearrange("p (h d) -> p h d", d=64), cb, OP.mult)
            mb.tt(swp.rearrange("p (h d) -> p h d", d=64), swp.rearrange("p (h d) -> p h d", d=64), sb, OP.mult)
            mb.tt(dst, tmpq, swp, OP.add)

        def psx(b, n=128):
            return (p.psE[:, b * 512:b * 512 + n], f"e{b}")

        def front(t):
            j2 = t % 2
            xt, xfmt = xts[t % 3], xfmts[j2]
            mb.dma(xt, (xin_tm[t * 128:(t + 1) * 128, :], f"d{t}"))
            for c in range(8):
                if t == 0:
                    mb.memset(xfmt[:, c * 129:c * 129 + 1], 0.0)
                else:
                    mb.copy(xfmt[:, c * 129:c * 129 + 1], xfmts[1 - j2][:, c * 129 + 128:c * 129 + 129])
            for c in range(8):
                pb = psx(c % 3)
                mb.transpose(pb, xt[:, c * 128:(c + 1) * 128], p.cp.ident)
                mb.copy(xfmt[:, c * 129 + 1:c * 129 + 129], pb, eng="act")
            for ch in range(6):
                psq = p.psB if ch % 2 == 0 else p.psA
                for c in range(8):
                    o = 1 if c < 4 else 0
                    mb.mm(psq, xfmt[:, c * 129 + o:c * 129 + o + 128],
                          wqkv[:, c * 3 * INNER + ch * 512:c * 3 * INNER + (ch + 1) * 512],
                          start=(c == 0), stop=(c == 7))
                mb.copy(qkv[:, ch * 512:(ch + 1) * 512], psq, eng="act")
            l2n(qkv[:, 0:INNER], qn)
            rot(q2s[j2], qn, rtab["qc"], rtab["qs"], t)
            l2n(qkv[:, INNER:2 * INNER], qn)
            rot(k2s[j2], qn, rtab["kc"], rtab["ks"], t)
            vs = t % KR
            dstv = vring[:, vs * 1040:(vs + 1) * 1040].rearrange("p (h e) -> p h e", e=65)[:, :, 0:64]
            mb.copy((dstv, f"v{vs}"), qkv[:, 2 * INNER:3 * INNER].rearrange("p (h d) -> p h d", d=64))

        def mid(t):
            qfm, q2, k2 = qfms[t % 2], q2s[t % 2], k2s[t % 2]
            ks_ = t % KR
            for c in range(8):
                pb = psx((2 * c) % 3)
                mb.transpose(pb, k2[:, c * 128:(c + 1) * 128], p.cp.ident)
                mb.copy((kring[:, (c * KR + ks_) * 128:(c * KR + ks_) * 128 + 128], f"s{ks_}"), pb, eng="act")
                pb = psx((2 * c + 1) % 3)
                mb.transpose(pb, q2[:, c * 128:(c + 1) * 128], p.cp.ident)
                mb.copy(qfm[:, c * 128:(c + 1) * 128], pb, eng="act")

        def back(t):
            j2 = t % 2
            xt, qfm = xts[t % 3], qfms[j2]
            if t == 0:
                jlist = [(0, ("t0", 0))]
            elif t == 1:
                jlist = [(0, ("t1", 0)), (1, ("t1", 1))]
            else:
                jlist = [(t - 2, ("full", 0)), (t - 1, ("full", 1)), (t, ("full", 2))]
            nch = len(jlist)
            pvb = [(p.psE[:, 3 * 512:3 * 512 + 65], "e3"), (p.psC[:, 0:65], "c0")]

            def scT(h):
                c, half = h // 2, (h % 2) * 64
                hs = (h % 3) * 384
                for kc, (jt, mk) in enumerate(jlist):
                    ks_ = jt % KR
                    pb = psx((h + kc) % 3)
                    mb.mm(pb, (kring[half:half + 64, (c * KR + ks_) * 128:(c * KR + ks_) * 128 + 128], f"s{ks_}"),
                          qfm[half:half + 64, c * 128:c * 128 + 128], start=True, stop=True)
                    mb.stt((PT_f[:, hs + kc * 128:hs + (kc + 1) * 128], f"pf{h % 3}"),
                           pb, QKS, amT[mk], OP.mult, OP.add)
                mb.act((PT_bf[:, hs:hs + nch * 128], f"pb{h % 3}"),
                       (PT_f[:, hs:hs + nch * 128], f"pf{h % 3}"), AF.Exp)

            def pv(h):
                hs = (h % 3) * 384
                pb = pvb[h % 2]
                for kc, (jt, mk) in enumerate(jlist):
                    vs = jt % KR
                    mb.mm(pb, (PT_bf[:, hs + kc * 128:hs + (kc + 1) * 128], f"pb{h % 3}"),
                          (vring[:, vs * 1040 + h * 65:vs * 1040 + (h + 1) * 65], f"v{vs}"),
                          start=(kc == 0), stop=(kc == nch - 1))
                mb.copy((rcps[:, h % 3:h % 3 + 1], f"rc{h % 3}"), (pb[0][:, 64:65], pb[1]))
                mb.reciprocal((rcps[:, 3 + h % 3:4 + h % 3], f"rq{h % 3}"),
                              (rcps[:, h % 3:h % 3 + 1], f"rc{h % 3}"))
                mb.ts((ofmt[:, h * 64:(h + 1) * 64], f"o{h % 2}"), (pb[0][:, 0:64], pb[1]),
                      (rcps[:, 3 + h % 3:4 + h % 3], f"rq{h % 3}"), None, OP.mult)

            scT(0)
            for h in range(1, 16):
                scT(h)
                pv(h - 1)
            pv(15)
            for oc in range(8):
                pb = psx(oc % 3)
                mb.transpose(pb, ofmt[:, oc * 128:(oc + 1) * 128], p.cp.ident)
                mb.copy(ofm[:, oc * 128:(oc + 1) * 128], pb, eng="act")
            for oc in range(2):
                pso = p.psA if oc == 0 else p.psB
                for c in range(8):
                    mb.mm(pso, ofm[:, c * 128:(c + 1) * 128],
                          wo[:, c * DM + oc * 512:c * DM + oc * 512 + 512],
                          start=(c == 0), stop=(c == 7))
                mb.copy(otm[:, oc * 512:(oc + 1) * 512], pso)
            mb.tt(otm, otm, xt, OP.add)
            mb.dma((xout_tm[t * 128:(t + 1) * 128, :], f"d{t}"), otm, eng="pool")

        front(0)
        mid(0)
        front(1)
        for t in range(NT):
            back(t)
            if t + 1 < NT:
                mid(t + 1)
            if t + 2 < NT:
                front(t + 2)
    with p.stage() as alloc:
        if "ffn" not in parts:
            raise StopStage
        NJ = ceil_div(DIN, 128)
        w1 = wload_bf(p, alloc, w["W1"][i], DM, DIN2, "w1")
        lngc = bias_cols(p, alloc, w["ln_g"][i], DM, "lng")
        for k in range(8):
            mb.ts(w1[:, k * DIN2:(k + 1) * DIN2], w1[:, k * DIN2:(k + 1) * DIN2], lngc[:, k:k + 1], None, OP.mult)
        w2 = wload_bf(p, alloc, w["W2"][i], DIN, DM, "w2")
        lnbc = bias_cols(p, alloc, w["ln_b"][i], DM, "lnb")
        lnbc_bf = alloc("lnbcb", [128, 8], BF16)
        mb.copy(lnbc_bf, lnbc)
        rtmp = alloc("rtmp", [1, 512])
        btmp = alloc("btmp", [1, 512])
        brow = alloc("brow", [1, 512])
        b2row = alloc("b2row", [1, DM], BF16)
        for o0 in (0, 512):
            mb.dma(rtmp, w["b2"][i].rearrange("(o n) -> o n", o=1)[:, o0:o0 + 512])
            mb.copy(b2row[:, o0:o0 + 512], rtmp)
        crow = alloc("crow", [1, DIN2], BF16)
        b1cols = alloc("b1cols", [128, 2 * NJ])
        beffd = p.dscratch(f"beff{i}", [DIN2])
        for o0 in range(0, DIN2, 512):
            o1 = min(o0 + 512, DIN2)
            n = o1 - o0
            for c in range(8):
                mb.mm((p.psA[0:1, 0:n], "cb"), p.cp.ones_col_bf, w1[:, c * DIN2 + o0:c * DIN2 + o1],
                      start=(c == 0), stop=(c == 7))
            mb.ts(crow[:, o0:o1], (p.psA[0:1, 0:n], "cb"), -1.0, None, OP.mult)
            for c in range(8):
                mb.mm((p.psB[0:1, 0:n], "bb"), lnbc_bf[:, c:c + 1], w1[:, c * DIN2 + o0:c * DIN2 + o1],
                      start=(c == 0), stop=(c == 7))
            mb.dma(btmp[:, 0:n], w["b1"][i].rearrange("(o n) -> o n", o=1)[:, o0:o1])
            mb.tt(brow[:, 0:n], (p.psB[0:1, 0:n], "bb"), btmp[:, 0:n], OP.add)
            mb.dma(beffd.rearrange("(o n) -> o n", o=1)[:, o0:o1], brow[:, 0:n])
        bd2 = beffd.rearrange("(n o) -> n o", o=1)
        for half, q0 in ((0, 0), (DIN, NJ)):
            for jj in range(NJ):
                kw = min(128, DIN - jj * 128)
                mb.dma(b1cols[0:kw, q0 + jj:q0 + jj + 1], bd2[half + jj * 128:half + jj * 128 + kw, :])
        xraw = alloc("xraw", [128, 8 * 257], BF16)
        xt2 = [alloc(f"xt2{u}", [128, DM]) for u in range(2)]
        outb = [alloc(f"outb{u}", [128, DM]) for u in range(2)]
        sqb = [alloc(f"sqb{u}", [128, 256], BF16) for u in range(2)]
        tub = [alloc(f"tub{u}", [128, 256]) for u in range(2)]
        tgb = [alloc(f"tgb{u}", [128, 256]) for u in range(2)]
        ggb = [alloc(f"ggb{u}", [128, 256]) for u in range(2)]
        hbb = [alloc(f"hbb{u}", [128, 256], BF16) for u in range(3)]
        rstdb = alloc("rstdb", [128, 256])
        murow = alloc("murow", [1, 256], BF16)
        qrow = alloc("qrow", [1, 256])
        vrow = alloc("vrow", [1, 256])
        rrow = alloc("rrow", [1, 256])
        for c in range(8):
            mb.memset(xraw[:, c * 257:c * 257 + 1], 0.0)

        def xv(c):
            off = 1 if c < 4 else 0
            return xraw[:, c * 257 + off:c * 257 + off + 256]

        psH = [p.psA, p.psB, p.psC]
        for s in range(T // 256):
            if s > 0:
                for c in range(8):
                    mb.copy(xraw[:, c * 257:c * 257 + 1], xraw[:, c * 257 + 256:c * 257 + 257], eng="pool")
            for u in range(2):
                mb.dma(xt2[u], xout_tm[(s * 2 + u) * 128:(s * 2 + u + 1) * 128, :])
                for c in range(8):
                    ph = psH[(u * 8 + c) % 3]
                    mb.transpose((ph[:, 0:128], "xT"), xt2[u][:, c * 128:(c + 1) * 128], p.cp.ident)
                    mb.copy(xraw[:, c * 257 + 1 + u * 128:c * 257 + 129 + u * 128], (ph[:, 0:128], "xT"),
                            eng="act")
            for c in range(8):
                mb.tt(sqb[c % 2], xv(c), xv(c), OP.mult)
                mb.mm((p.psA[0:1, 0:256], "srow"), p.cp.ones_col_bf, xv(c), start=(c == 0), stop=(c == 7))
                mb.mm((p.psB[0:1, 0:256], "qrow"), p.cp.ones_col_bf, sqb[c % 2], start=(c == 0), stop=(c == 7))
            mb.ts(murow, (p.psA[0:1, 0:256], "srow"), 1.0 / DM, None, OP.mult)
            mb.ts(qrow, (p.psB[0:1, 0:256], "qrow"), 1.0 / DM, None, OP.mult)
            mb.ts(vrow, (p.psA[0:1, 0:256], "srow"), 1.0 / DM, None, OP.mult)
            mb.tt(vrow, vrow, vrow, OP.mult)
            mb.tt(vrow, qrow, vrow, OP.subtract)
            mb.ts(vrow, vrow, 1e-5, None, OP.add)
            mb.act(rrow, vrow, AF.Sqrt)
            mb.reciprocal(rrow, rrow)
            mb.mm((p.psC[:, 0:256], "rb"), p.cp.ones_row, rrow, start=True, stop=True)
            mb.copy(rstdb, (p.psC[:, 0:256], "rb"), eng="act")
            for jj in range(NJ + 1):
                if jj < NJ:
                    kw = min(128, DIN - jj * 128)
                    ph = psH[jj % 3]
                    for half, pc0 in ((0, 0), (DIN, 256)):
                        for c in range(8):
                            mb.mm((ph[0:kw, pc0:pc0 + 256], "h"),
                                  w1[:, c * DIN2 + half + jj * 128:c * DIN2 + half + jj * 128 + kw],
                                  xv(c), start=(c == 0), stop=False)
                        mb.mm((ph[0:kw, pc0:pc0 + 256], "h"),
                              crow[:, half + jj * 128:half + jj * 128 + kw], murow, start=False, stop=True)
                    u2, g2, e2, hbx = tub[jj % 2], tgb[jj % 2], ggb[jj % 2], hbb[jj % 3]
                    mb.tt(u2[0:kw, :], (ph[0:kw, 0:256], "h"), rstdb[0:kw, :], OP.mult)
                    mb.tt(g2[0:kw, :], (ph[0:kw, 256:512], "h"), rstdb[0:kw, :], OP.mult)
                    mb.act(e2[0:kw, :], g2[0:kw, :], AF.Gelu, bias=b1cols[0:kw, NJ + jj:NJ + jj + 1])
                    mb.stt(hbx[0:kw, :], u2[0:kw, :], b1cols[0:kw, jj:jj + 1], e2[0:kw, :], OP.add, OP.mult)
                jjp = jj - 1
                if jjp >= 0:
                    kwp = min(128, DIN - jjp * 128)
                    hbp = hbb[jjp % 3]
                    for ts_ in range(2):
                        for oc in range(2):
                            bnk = ts_ * 2 + oc
                            mb.mm((p.psE[:, bnk * 512:(bnk + 1) * 512], f"o{bnk}"),
                                  hbp[0:kwp, ts_ * 128:(ts_ + 1) * 128],
                                  w2[0:kwp, jjp * DM + oc * 512:jjp * DM + oc * 512 + 512],
                                  start=(jjp == 0), stop=False)
            for ts_ in range(2):
                for oc in range(2):
                    bnk = ts_ * 2 + oc
                    mb.mm((p.psE[:, bnk * 512:(bnk + 1) * 512], f"o{bnk}"),
                          p.cp.ones_row_bf, b2row[:, oc * 512:(oc + 1) * 512], start=False, stop=True)
            for ts_ in range(2):
                for oc in range(2):
                    bnk = ts_ * 2 + oc
                    mb.tt(outb[ts_][:, oc * 512:(oc + 1) * 512], (p.psE[:, bnk * 512:(bnk + 1) * 512], f"o{bnk}"),
                          xt2[ts_][:, oc * 512:(oc + 1) * 512], OP.add)
                mb.dma(xout_tm[(s * 2 + ts_) * 128:(s * 2 + ts_ + 1) * 128, :], outb[ts_], eng="pool")


def emit_l1_attn(p, w, xnfm, a1_tm, o_bf, cs):
    mb, T, NT = p.mb, p.T, p.NT
    CQ = 128
    with p.stage() as alloc:
        wq = wload_bf(p, alloc, w["Wq"], DM, CQ, "wq")
        wk = wload_bf(p, alloc, w["Wk"], DM, CQ, "wk")
        wv = wload_bf(p, alloc, w["Wv"], DM, CQ, "wv")
        wg = wload_bf(p, alloc, w["Wg"], DM, CQ, "wg")
        wab = alloc("wab", [128, 16], BF16)
        stg = alloc("stgw", [128, 2])
        for k in range(8):
            mb.dma(stg[:, 0:1], w["Wa"][k * 128:(k + 1) * 128, :])
            mb.dma(stg[:, 1:2], w["Wb"][k * 128:(k + 1) * 128, :])
            mb.copy(wab[:, 2 * k:2 * k + 2], stg)
        brow = alloc("brow", [1, 2 * CQ + 2])
        mb.dma(brow[:, 0:CQ], w["bq"].rearrange("(o n) -> o n", o=1))
        mb.dma(brow[:, CQ:2 * CQ], w["bk"].rearrange("(o n) -> o n", o=1))
        mb.dma(brow[:, 2 * CQ:2 * CQ + 1], w["ba"].rearrange("(o n) -> o n", o=1))
        mb.dma(brow[:, 2 * CQ + 1:2 * CQ + 2], w["bb"].rearrange("(o n) -> o n", o=1))
        bq_rep = alloc("bqrep", [128, CQ]); bk_rep = alloc("bkrep", [128, CQ])
        bab_rep = alloc("babrep", [128, 2])
        mb.mm(p.psC[:, 0:CQ], p.cp.ones_row, brow[:, 0:CQ], start=True, stop=True)
        mb.copy(bq_rep, p.psC[:, 0:CQ])
        mb.mm(p.psC[:, 0:CQ], p.cp.ones_row, brow[:, CQ:2 * CQ], start=True, stop=True)
        mb.copy(bk_rep, p.psC[:, 0:CQ])
        mb.mm(p.psC[:, 0:2], p.cp.ones_row, brow[:, 2 * CQ:2 * CQ + 2], start=True, stop=True)
        mb.copy(bab_rep, p.psC[:, 0:2])
        bv_rep = rep_row(p, alloc, w["bv"], CQ, "bv")
        bg_rep = rep_row(p, alloc, w["bg"], CQ, "bg")
        qtm = alloc("qtm", [128, NT * CQ])
        ktm = alloc("ktm", [128, NT * CQ], BF16)
        vtm = alloc("vtm", [128, NT * CQ])
        gtm = alloc("gtm", [128, NT * CQ])
        abtm = alloc("abtm", [128, NT * 2])
        xfms = [alloc(f"xfm{j}", [128, 1024], BF16) for j in range(2)]
        tqs = [alloc(f"tq{j}", [128, CQ]) for j in range(2)]
        colss = [alloc(f"cols{j}", [128, 4]) for j in range(2)]
        for t in range(NT):
            xfm, tq, cols = xfms[t % 2], tqs[t % 2], colss[t % 2]
            for k in range(8):
                mb.dma(xfm[:, k * 128:(k + 1) * 128], xnfm[k * 128:(k + 1) * 128, t * 128:(t + 1) * 128])
            for k in range(8):
                for dst, wt, nc_ in ((p.psA, wq, CQ), (p.psB, wk, CQ), (p.psC, wv, CQ)):
                    mb.mm(dst[:, 0:nc_], xfm[:, k * 128:(k + 1) * 128], wt[:, k * CQ:(k + 1) * CQ],
                          start=(k == 0), stop=(k == 7))
                mb.mm((p.psE[:, 0:CQ], "g"), xfm[:, k * 128:(k + 1) * 128], wg[:, k * CQ:(k + 1) * CQ],
                      start=(k == 0), stop=(k == 7))
                mb.mm((p.psE[:, 512:514], "ab"), xfm[:, k * 128:(k + 1) * 128], wab[:, 2 * k:2 * k + 2],
                      start=(k == 0), stop=(k == 7))
            mb.tt(qtm[:, t * CQ:(t + 1) * CQ], p.psA[:, 0:CQ], bq_rep, OP.add)
            mb.tt(tq, p.psB[:, 0:CQ], bk_rep, OP.add)
            mb.act(tq if False else (tq, "sq"), tq, AF.Square, accum_out=cols[:, 0:1]) if False else None
            mb.reduce(cols[:, 0:1], tq.rearrange("p (a n) -> p a n", a=1), OP.add, negate=False) if False else None
            mb.act(vtm[:, t * CQ:(t + 1) * CQ], tq, AF.Square, accum_out=cols[:, 0:1])
            mb.act(cols[:, 1:2], cols[:, 0:1], AF.Sqrt)
            mb.ts(cols[:, 1:2], cols[:, 1:2], 1e-12, None, OP.max)
            mb.reciprocal(cols[:, 2:3], cols[:, 1:2])
            mb.ts(ktm[:, t * CQ:(t + 1) * CQ], tq, cols[:, 2:3], None, OP.mult)
            mb.tt(vtm[:, t * CQ:(t + 1) * CQ], p.psC[:, 0:CQ], bv_rep, OP.add)
            mb.stt(gtm[:, t * CQ:(t + 1) * CQ], (p.psE[:, 0:CQ], "g"), 1.0, bg_rep, OP.mult, OP.add)
            mb.act(gtm[:, t * CQ:(t + 1) * CQ], gtm[:, t * CQ:(t + 1) * CQ], AF.Sigmoid)
            mb.tt(abtm[:, t * 2:(t + 1) * 2], (p.psE[:, 512:514], "ab"), bab_rep, OP.add)
        lna = alloc("lna", [128, NT])
        lnb = alloc("lnb", [128, NT])
        bcol = alloc("bcol", [128, NT])
        Lm = alloc("Lm", [128, NT])
        Ecol = alloc("Ecol", [128, NT])
        bE = alloc("bE", [128, NT])
        Dfac = alloc("Dfac", [128, NT])
        tmp = alloc("tmp", [128, NT])
        tmp2 = alloc("tmp2", [128, NT])
        def neg_softplus_neg(dst, src):
            mb.act(tmp, src, AF.Abs)
            mb.act(tmp, tmp, AF.Exp, scale=-1.0)
            mb.act(tmp, tmp, AF.Ln, bias=1.0)
            mb.ts(tmp2, src, 0.0, None, OP.min)
            mb.tt(dst, tmp2, tmp, OP.subtract)
        ua = abtm.rearrange("p (t c) -> p c t", c=2)[:, 0]
        ub = abtm.rearrange("p (t c) -> p c t", c=2)[:, 1]
        neg_softplus_neg(lna, ua)
        neg_softplus_neg(lnb, ub)
        mb.act(bcol, ub, AF.Sigmoid)
        lna_bf = alloc("lnabf", [128, NT], BF16)
        mb.copy(lna_bf, lna)
        mb.mm(p.psA[:, 0:NT], p.cp.cumul_bf if False else p.cp.cumul, lna_bf, start=True, stop=True) if False else None
        mb.mm(p.psB[:, 0:NT], p.cp.cumul, lna, start=True, stop=True)
        mb.copy(Lm, p.psB[:, 0:NT])
        mb.act(Ecol, Lm, AF.Exp)
        mb.tt(bE, bcol, Ecol, OP.mult)
        LmCrow = alloc("LmCrow", [1, NT])
        mb.mm(p.psA[0:1, 0:NT], p.cp.ones_col, lna, start=True, stop=True)
        mb.copy(LmCrow, p.psA[0:1, 0:NT])
        LmCrep = alloc("LmCrep", [128, NT])
        mb.mm(p.psA[:, 0:NT], p.cp.ones_row, LmCrow, start=True, stop=True)
        mb.copy(LmCrep, p.psA[:, 0:NT])
        mb.tt(tmp, LmCrep, Lm, OP.subtract)
        mb.act(Dfac, tmp, AF.Exp)
        ECrow = alloc("ECrow", [1, NT])
        mb.act(ECrow, LmCrow, AF.Exp)
        qe_bf = alloc("qebf", [128, NT * CQ], BF16)
        for t in range(NT):
            mb.ts(qe_bf[:, t * CQ:(t + 1) * CQ], qtm[:, t * CQ:(t + 1) * CQ], Ecol[:, t:t + 1], None, OP.mult)
        Z = alloc("Z", [128, 128])
        Zbf = alloc("Zbf", [128, 128], BF16)
        Ztbf = alloc("Ztbf", [128, 128], BF16)
        mb.memset(Z, 0.0)
        mb.memset(Zbf, 0.0)
        mb.memset(Ztbf, 0.0)
        kfm = alloc("kfm", [128, 128], BF16)
        qfm = alloc("qfm", [128, 128], BF16)
        qfmp = alloc("qfmp", [128, 128], BF16)
        qtmb = alloc("qtmb", [128, 128], BF16)
        colb = alloc("colb", [128, 128])
        E2a = alloc("E2a", [128, 128])
        E2b = alloc("E2b", [128, 128])
        E2p = alloc("E2p", [128, 128])
        kkb = alloc("kkb", [128, 128], BF16)
        Nmat = alloc("Nmat", [128, 128], BF16)
        NTm = alloc("NTm", [128, 128], BF16)
        lbcol = alloc("lbcol", [128, 1])
        nLm = alloc("nLm", [128, 1])
        G = [alloc(f"G{b}", [64, 64], BF16) for b in range(2)]
        Gt = [alloc(f"Gt{b}", [64, 64], BF16) for b in range(2)]
        Spow = [alloc(f"Spow{b}", [64, 2 * 64], BF16) for b in range(2)]
        Ncp = [alloc(f"Ncp{b}", [64, 2 * 64], BF16) for b in range(2)]
        ptmp = alloc("ptmp", [128, 128])
        bhat = alloc("bhat", [128, 128])
        bhat_bf = alloc("bbf", [128, 128], BF16)
        uhat_bf = alloc("ubf", [128, 128], BF16)
        du_bf = alloc("dubf", [128, 128], BF16)
        ofm_t = alloc("ofmt", [128, 128])
        obf = alloc("obf", [128, 128], BF16)
        Ut_bf = alloc("Utbf", [128, 128], BF16)
        Wt_bf = alloc("Wtbf", [128, 128], BF16)
        eccol = alloc("eccol", [128, 1])
        ident64 = p.cp.ident_bf[0:64, 0:64]
        for c in range(NT):
            ksl = ktm[:, c * CQ:(c + 1) * CQ]
            mb.transpose(p.psT[:, 0:128], ksl, p.cp.ident_bf)
            mb.copy(kfm, p.psT[:, 0:128])
            mb.transpose(p.psT[:, 0:128], qe_bf[:, c * CQ:(c + 1) * CQ], p.cp.ident_bf)
            mb.copy(qfm, p.psT[:, 0:128])
            mb.copy(qtmb, qtm[:, c * CQ:(c + 1) * CQ])
            mb.transpose(p.psT[:, 0:128], qtmb, p.cp.ident_bf)
            mb.copy(qfmp, p.psT[:, 0:128])
            mb.tt(tmp[:, 0:1], Lm[:, c:c + 1], lnb[:, c:c + 1], OP.add)
            mb.ts(colb, p.cp.ident if False else colb, 0.0, None, OP.mult) if False else None
            mb.ts(colb, p.cp.mask_le, 0.0, Lm[:, c:c + 1], OP.mult, OP.add)
            mb.transpose(p.psA[:, 0:128], colb, p.cp.ident)
            mb.copy(E2a, p.psA[:, 0:128])
            mb.ts(colb, p.cp.mask_le, 0.0, tmp[:, 0:1], OP.mult, OP.add)
            mb.transpose(p.psA[:, 0:128], colb, p.cp.ident)
            mb.copy(E2p, p.psA[:, 0:128])
            mb.ts(nLm, Lm[:, c:c + 1], -1.0, None, OP.mult)
            mb.ts(ptmp, E2a, -1.0, None, OP.mult)
            mb.ts(ptmp, ptmp, tmp[:, 0:1], 0.0, OP.add, OP.min)
            mb.act(E2b, ptmp, AF.Exp)
            mb.ts(ptmp, E2p, nLm, 0.0, OP.add, OP.min)
            mb.act(E2p, ptmp, AF.Exp)
            mb.ts(ptmp, E2a, nLm, 0.0, OP.add, OP.min)
            mb.act(E2a, ptmp, AF.Exp)
            mb.mm(p.psA[:, 0:128], kfm, kfm, start=True, stop=True)
            mb.copy(kkb, p.psA[:, 0:128])
            mb.tt(ptmp, E2b, p.cp.mask_lt_neg, OP.mult)
            mb.tt(Nmat, ptmp, kkb, OP.mult)
            mb.tt(ptmp, E2p, p.cp.mask_st_neg, OP.mult)
            mb.tt(NTm, ptmp, kkb, OP.mult)
            mb.tt(E2a, E2a, p.cp.mask_le, OP.mult)
            for b in range(2):
                o = b * 64
                if b == 0:
                    mb.copy(Ncp[b][:, 0:64], NTm[0:64, 0:64])
                    mb.copy(Ncp[b][:, 64:128], Nmat[0:64, 0:64])
                else:
                    mb.dma(Ncp[b][:, 0:64], NTm[64:128, 64:128])
                    mb.dma(Ncp[b][:, 64:128], Nmat[64:128, 64:128])
                Yt = Ncp[b][:, 0:64]
                Y = Ncp[b][:, 64:128]
                mb.tt(G[b], Yt, ident64, OP.add)
                mb.tt(Gt[b], Y, ident64, OP.add)
                mb.copy(Spow[b][:, 0:64], Yt)
                mb.copy(Spow[b][:, 64:128], Y)
                for lev in range(5):
                    mb.mm(p.psB[0:64, 0:64], Spow[b][:, 64:128], Spow[b][:, 0:64], start=True, stop=True)
                    mb.mm(p.psB[0:64, 64:128], Spow[b][:, 0:64], Spow[b][:, 64:128], start=True, stop=True)
                    mb.copy(Spow[b][:, :], p.psB[0:64, 0:128])
                    mb.mm(p.psC[0:64, 0:64], Gt[b], Spow[b][:, 0:64], start=True, stop=True)
                    mb.mm(p.psC[0:64, 64:128], Spow[b][:, 0:64], Gt[b], start=True, stop=True)
                    mb.tt(G[b], G[b], p.psC[0:64, 0:64], OP.add)
                    mb.tt(Gt[b], Gt[b], p.psC[0:64, 64:128], OP.add)
            mb.mm(p.psA[:, 0:128], kfm, Zbf, start=True, stop=True)
            mb.ts(bhat, vtm[:, c * CQ:(c + 1) * CQ], bcol[:, c:c + 1], None, OP.mult)
            mb.stt(bhat, p.psA[:, 0:128], bE[:, c:c + 1], bhat, OP.mult, OP.subtract)
            mb.ts(bhat, bhat, -1.0, None, OP.mult)
            mb.copy(bhat_bf, bhat)
            mb.mm(p.psB[0:64, 0:128], G[0], bhat_bf[0:64, :], start=True, stop=True)
            mb.copy(uhat_bf[0:64, :], p.psB[0:64, 0:128])
            mb.mm(p.psC[0:64, 0:128], NTm[0:64, 64:128], uhat_bf[0:64, :], start=True, stop=True)
            mb.dma(ptmp[0:64, :], bhat[64:128, :])
            mb.tt(ptmp[0:64, :], p.psC[0:64, 0:128], ptmp[0:64, :], OP.add)
            mb.copy(bhat_bf[0:64, :], ptmp[0:64, :])
            mb.mm(p.psB[0:64, 0:128], G[1], bhat_bf[0:64, :], start=True, stop=True)
            mb.copy(uhat_bf[64:128, :], p.psB[0:64, 0:128])
            mb.transpose(p.psT[:, 0:128], uhat_bf, p.cp.ident_bf)
            mb.copy(Ut_bf, p.psT[:, 0:128])
            mb.mm(p.psA[:, 0:128], Ut_bf, qfmp, start=True, stop=True)
            mb.tt(ptmp, p.psA[:, 0:128], E2a, OP.mult)
            mb.copy(Wt_bf, ptmp)
            mb.mm((p.psE[:, 0:128], "o"), Wt_bf, ktm[:, c * CQ:(c + 1) * CQ], start=True, stop=False)
            mb.mm((p.psE[:, 0:128], "o"), qfm, Ztbf, start=False, stop=True)
            mb.tt(ofm_t, (p.psE[:, 0:128], "o"), gtm[:, c * CQ:(c + 1) * CQ], OP.mult)
            mb.copy(obf, ofm_t)
            mb.dma(o_bf[c * 128:(c + 1) * 128, :], obf, eng="pool")
            mb.ts(du_bf, uhat_bf, Dfac[:, c:c + 1], None, OP.mult)
            mb.mm(p.psB[:, 0:128], ksl, du_bf, start=True, stop=True)
            mb.mm(p.psC[:, 0:1], p.cp.ones_row, ECrow[:, c:c + 1], start=True, stop=True)
            mb.copy(eccol, p.psC[:, 0:1])
            mb.stt(Z, Z, eccol, p.psB[:, 0:128], OP.mult, OP.add)
            mb.copy(Zbf, Z)
            mb.transpose(p.psT[:, 0:128], Zbf, p.cp.ident_bf)
            mb.copy(Ztbf, p.psT[:, 0:128])
    with p.stage() as alloc:
        wo = wload_bf(p, alloc, w["Wo"], CQ, DM, "wo")
        borep = rep_row(p, alloc, w["bo"], DM, "bo")
        oin = alloc("oin", [128, 128], BF16)
        ofm2 = alloc("ofm2", [128, 128], BF16)
        xo = alloc("xo", [128, DM])
        for t in range(NT):
            mb.dma(oin, o_bf[t * 128:(t + 1) * 128, :])
            mb.transpose(p.psT[:, 0:128], oin, p.cp.ident_bf)
            mb.copy(ofm2, p.psT[:, 0:128])
            for oc in range(2):
                mb.mm(p.psA, ofm2, wo[:, oc * 512:(oc + 1) * 512], start=True, stop=True)
                mb.tt(xo[:, oc * 512:(oc + 1) * 512], p.psA, borep[:, oc * 512:(oc + 1) * 512], OP.add)
            mb.dma(a1_tm[t * 128:(t + 1) * 128, :], xo, eng="pool")


def emit_l0_attn(p, w, x_tm, xrfm, a0_tm, cs, scr):
    mb, T, NT = p.mb, p.T, p.NT
    H, HD2 = NH, 2 * NH
    p.vfm = p.salloc("l0_vfm", [32, T])
    with p.stage() as alloc:
        rmsc = bias_cols(p, alloc, w["rms_w"], DM, "rms")
        win = wload_bf(p, alloc, w["in_proj_w"], DM, 6176, "inprj")
        for k in range(8):
            mb.ts(win[:, k * 6176:(k + 1) * 6176], win[:, k * 6176:(k + 1) * 6176], rmsc[:, k:k + 1], None, OP.mult)
        bcols = alloc("bcols", [128, 49])
        dma_slow(p, bcols[:, 0:48], w["in_proj_b"][0:6144].rearrange("(a b) -> b a", b=128))
        dma_slow(p, bcols[0:32, 48:49], w["in_proj_b"][6144:6176].rearrange("(a b) -> b a", b=32))
        cwc = alloc("cwc", [128, 16 * 4])
        for o in range(16):
            mb.dma(cwc[:, o * 4:(o + 1) * 4], w["conv_w"][o * 128:(o + 1) * 128, :])
        cbc = bias_cols(p, alloc, w["conv_b"], DI, "convb")
        xss = [alloc(f"x{j}", [128, 8 * 512], BF16) for j in range(2)]
        zb = alloc("zb", [128, 512])
        zb2 = alloc("zb2", [128, 512])
        xgs = alloc("xgs", [128, 16 * 515])
        xcb = alloc("xcb", [128, 512], BF16)
        accs = [alloc(f"acc{j}", [128, 512]) for j in range(2)]
        ktmps = [alloc(f"ktmp{j}", [128, 512]) for j in range(2)]
        ktts = [alloc(f"ktt{j}", [128, 128]) for j in range(2)]
        knos = [alloc(f"kno{j}", [128, 128], BF16) for j in range(2)]
        kno2s = [alloc(f"kno2{j}", [128, 128], BF16) for j in range(2)]
        ksq = alloc("ksq", [128, 2 * 128])
        ssq = alloc("ssq", [128, 2])
        scl = alloc("scl", [128, 8])
        for o in range(16):
            mb.memset(xgs[:, o * 515:o * 515 + 3], 0.0)
        for s0 in range(0, T, 512):
            xs = xss[(s0 // 512) % 2]
            for k in range(8):
                mb.dma(xs[:, k * 512:(k + 1) * 512], xrfm[k * 128:(k + 1) * 128, s0:s0 + 512])
            for o in range(16):
                psz = p.psA if o % 2 == 0 else p.psE[:, 1536:2048]
                zbx = zb if o % 2 == 0 else zb2
                for k in range(8):
                    mb.mm(psz, win[:, k * 6176 + o * 128:k * 6176 + o * 128 + 128],
                          xs[:, k * 512:(k + 1) * 512], start=(k == 0), stop=(k == 7))
                mb.ts(zbx, psz, bcols[:, o:o + 1], None, OP.add)
                mb.dma(scr["zfm"][o * 128:(o + 1) * 128, s0:s0 + 512], zbx, eng="pool")
            for o in range(16):
                psg = p.psB if o % 2 == 0 else p.psE[:, 0:512]
                acc = accs[o % 2]
                for k in range(8):
                    mb.mm(psg, win[:, k * 6176 + 2048 + o * 128:k * 6176 + 2048 + o * 128 + 128],
                          xs[:, k * 512:(k + 1) * 512], start=(k == 0), stop=(k == 7))
                mb.ts(xgs[:, o * 515 + 3:o * 515 + 515], psg, bcols[:, 16 + o:17 + o], None, OP.add)
                mb.ts(acc, xgs[:, o * 515 + 3:o * 515 + 515], cwc[:, o * 4 + 3:o * 4 + 4], None, OP.mult)
                for j in range(3):
                    mb.stt(acc, xgs[:, o * 515 + j:o * 515 + j + 512], cwc[:, o * 4 + j:o * 4 + j + 1],
                           acc, OP.mult, OP.add)
                mb.copy(xgs[:, o * 515:o * 515 + 3], xgs[:, o * 515 + 512:o * 515 + 515])
                mb.act(xcb, acc, AF.Silu, bias=cbc[:, o:o + 1])
                mb.dma(scr["xcfm"][o * 128:(o + 1) * 128, s0:s0 + 512], xcb, eng="pool")
            for o in range(16):
                ktmp = ktmps[o % 2]
                for k in range(8):
                    mb.mm(p.psC, win[:, k * 6176 + 4096 + o * 128:k * 6176 + 4096 + o * 128 + 128],
                          xs[:, k * 512:(k + 1) * 512], start=(k == 0), stop=(k == 7))
                mb.ts(ktmp, p.psC, bcols[:, 32 + o:33 + o], None, OP.add)
                for q in range(4):
                    jq = q % 2
                    ktt, kno, kno2 = ktts[jq], knos[jq], kno2s[jq]
                    pst = p.psE[:, 512 + jq * 512:512 + jq * 512 + 128]
                    mb.transpose((pst, f"kt{jq}"), ktmp[:, q * 128:(q + 1) * 128], p.cp.ident)
                    mb.copy(ktt, (pst, f"kt{jq}"))
                    mb.act(ksq[:, jq * 128:(jq + 1) * 128], ktt, AF.Square, accum_out=ssq[:, jq:jq + 1])
                    mb.act(scl[:, jq * 4:jq * 4 + 1], ssq[:, jq:jq + 1], AF.Sqrt)
                    mb.ts(scl[:, jq * 4 + 1:jq * 4 + 2], scl[:, jq * 4:jq * 4 + 1], 1e-6, None, OP.add)
                    mb.reciprocal(scl[:, jq * 4 + 2:jq * 4 + 3], scl[:, jq * 4 + 1:jq * 4 + 2])
                    mb.ts(kno, ktt, scl[:, jq * 4 + 2:jq * 4 + 3], None, OP.mult)
                    mb.dma(scr["kntm"][(s0 + q * 128):(s0 + (q + 1) * 128), o * 128:(o + 1) * 128],
                           kno, eng="pool")
                    ps2 = p.psT[:, jq * 128:(jq + 1) * 128]
                    mb.transpose((ps2, f"t{jq}"), kno, p.cp.ident_bf)
                    mb.copy(kno2, (ps2, f"t{jq}"))
                    mb.dma(scr["knfm"][o * 128:(o + 1) * 128, s0 + q * 128:s0 + (q + 1) * 128],
                           kno2, eng="pool")
            for k in range(8):
                mb.mm(p.psB[0:32, :], win[:, k * 6176 + 6144:k * 6176 + 6176],
                      xs[:, k * 512:(k + 1) * 512], start=(k == 0), stop=(k == 7))
            mb.ts(p.vfm[:, s0:s0 + 512], p.psB[0:32, :], bcols[0:32, 48:49], None, OP.add)
    with p.stage() as alloc:
        wq = wload_bf(p, alloc, w["Q_w"], DI, DI, "qw")
        xss = [alloc(f"x{j}", [128, 16 * 512], BF16) for j in range(2)]
        ktmps = [alloc(f"ktmp{j}", [128, 512]) for j in range(2)]
        ktts = [alloc(f"ktt{j}", [128, 128]) for j in range(2)]
        knos = [alloc(f"kno{j}", [128, 128], BF16) for j in range(2)]
        kno2s = [alloc(f"kno2{j}", [128, 128], BF16) for j in range(2)]
        ssq = alloc("ssq", [128, 2])
        scl = alloc("scl", [128, 8])
        acc = alloc("acc", [128, 2 * 128])
        for s0 in range(0, T, 512):
            xs = xss[(s0 // 512) % 2]
            for k in range(16):
                mb.dma(xs[:, k * 512:(k + 1) * 512], scr["xcfm"][k * 128:(k + 1) * 128, s0:s0 + 512])
            for o in range(16):
                psq = p.psA if o % 2 == 0 else p.psE[:, 1536:2048]
                ktmp = ktmps[o % 2]
                for k in range(16):
                    mb.mm(psq, wq[:, k * DI + o * 128:k * DI + o * 128 + 128],
                          xs[:, k * 512:(k + 1) * 512], start=(k == 0), stop=(k == 15))
                mb.copy(ktmp, psq)
                for q in range(4):
                    jq = q % 2
                    ktt, kno, kno2 = ktts[jq], knos[jq], kno2s[jq]
                    pst = p.psB if jq == 0 else p.psC
                    mb.transpose(pst[:, 0:128], ktmp[:, q * 128:(q + 1) * 128], p.cp.ident)
                    mb.copy(ktt, pst[:, 0:128])
                    mb.act(acc[:, jq * 128:(jq + 1) * 128], ktt, AF.Square, accum_out=ssq[:, jq:jq + 1])
                    mb.act(scl[:, jq * 4:jq * 4 + 1], ssq[:, jq:jq + 1], AF.Sqrt)
                    mb.ts(scl[:, jq * 4 + 1:jq * 4 + 2], scl[:, jq * 4:jq * 4 + 1], 1e-6, None, OP.add)
                    mb.reciprocal(scl[:, jq * 4 + 2:jq * 4 + 3], scl[:, jq * 4 + 1:jq * 4 + 2])
                    mb.ts(kno, ktt, scl[:, jq * 4 + 2:jq * 4 + 3], None, OP.mult)
                    ps2 = p.psT[:, jq * 128:(jq + 1) * 128]
                    mb.transpose((ps2, f"t{jq}"), kno, p.cp.ident_bf)
                    mb.copy(kno2, (ps2, f"t{jq}"))
                    mb.dma(scr["qnfm"][o * 128:(o + 1) * 128, s0 + q * 128:s0 + (q + 1) * 128],
                           kno2, eng="pool")


def emit_l0_gates_and_scan(p, w, cs, scr, retr_tm):
    mb, T, NT = p.mb, p.T, p.NT
    NT16 = NT * 16
    g_lm = p.salloc("l0_lm", [128, NT16])
    g_E = p.salloc("l0_E", [128, NT16])
    g_cos = p.salloc("l0_cos", [128, NT16])
    g_sin = p.salloc("l0_sin", [128, NT16])
    g_beta = p.salloc("l0_beta", [128, NT16])
    g_selC2 = p.salloc("l0_selC2", [128, NT * 32])
    g_bv = p.salloc("l0_bv", [128, NT * 32])
    g_bE2 = p.salloc("l0_bE2", [128, NT * 32])
    g_E2d = p.salloc("l0_E2d", [128, NT * 32])
    g_Df2 = p.salloc("l0_Df2", [128, NT * 32])
    g_ECrow = p.salloc("l0_ECrow", [1, NT16])
    g_cosC = p.salloc("l0_cosC", [1, NT16])
    g_sinC = p.salloc("l0_sinC", [1, NT16])
    with p.stage() as alloc:
        wg = wload_bf(p, alloc, w["gate_w"], DI, 112, "gw")
        gb_rep = rep_row(p, alloc, w["gate_b"], 112, "gb")
        gates = alloc("gates", [128, NT * 112])
        xtg = [alloc(f"xt{j}", [128, 16 * 128], BF16) for j in range(2)]
        for t in range(NT):
            xt = xtg[t % 2]
            psg = p.psA if t % 2 == 0 else p.psB
            for k in range(16):
                mb.dma(xt[:, k * 128:(k + 1) * 128], scr["xcfm"][k * 128:(k + 1) * 128, t * 128:(t + 1) * 128])
            for k in range(16):
                mb.mm(psg[:, 0:112], xt[:, k * 128:(k + 1) * 128], wg[:, k * 112:(k + 1) * 112],
                      start=(k == 0), stop=(k == 15))
            mb.tt(gates[:, t * 112:(t + 1) * 112], psg[:, 0:112], gb_rep[:, 0:112], OP.add)
        gv = gates.rearrange("p (t g h) -> p t g h", g=7, h=16)
        ty = {}
        for gi, nm in enumerate(["alpha", "omega", "selB", "selC", "dt", "betar", "rr"]):
            ty[nm] = alloc(nm, [128, NT16])
            mb.copy(ty[nm].rearrange("p (t h) -> p t h", h=16), gv[:, :, gi, :])
        t1 = alloc("t1", [128, NT16]); t2 = alloc("t2", [128, NT16]); t3 = alloc("t3", [128, NT16])
        t4 = alloc("t4", [128, NT16]); t5 = alloc("t5", [128, NT16])
        fr = alloc("fr", [128, NT16])
        mb.dma(fr, cs["c_l0_freqs"])
        def softplus(dst, src):
            mb.act(t1, src, AF.Abs)
            mb.act(t1, t1, AF.Exp, scale=-1.0)
            mb.act(t1, t1, AF.Ln, bias=1.0)
            mb.ts(t2, src, 0.0, None, OP.max)
            mb.tt(dst, t2, t1, OP.add)
        softplus(t3, ty["dt"])
        ldsrow = alloc("ldsrow", [1, 16])
        mb.dma(ldsrow, w["log_dt_scale"].rearrange("(o n) -> o n", o=1))
        mb.act(ldsrow, ldsrow, AF.Exp)
        ldsrep = alloc("ldsrep", [128, NT16])
        mb.mm(p.psC[:, 0:16], p.cp.ones_row, ldsrow, start=True, stop=True)
        mb.copy(ldsrep[:, 0:16], p.psC[:, 0:16])
        wdt = 16
        while wdt < NT16:
            cc = min(wdt, NT16 - wdt)
            mb.copy(ldsrep[:, wdt:wdt + cc], ldsrep[:, 0:cc])
            wdt += cc
        dt = alloc("dtv", [128, NT16])
        mb.tt(dt, t3, ldsrep, OP.mult)
        mb.ts(dt, dt, 1e-4, None, OP.add)
        al = alloc("al", [128, NT16])
        softplus(al, ty["alpha"])
        mb.act(t3, ty["omega"], AF.Tanh)
        mb.ts(t3, t3, 1.0, None, OP.add)
        om = alloc("om", [128, NT16])
        mb.tt(om, t3, fr, OP.mult)
        ah = alloc("ah", [128, NT16]); wh = alloc("wh", [128, NT16])
        mb.stt(ah, dt, 0.5, al, OP.mult, OP.mult)
        mb.stt(wh, dt, 0.5, om, OP.mult, OP.mult)
        mb.act(t1, ah, AF.Square)
        mb.act(t2, wh, AF.Square)
        mb.ts(t3, ah, 1.0, None, OP.add)
        mb.act(t3, t3, AF.Square)
        det = alloc("det", [128, NT16])
        mb.tt(det, t3, t2, OP.add)
        mb.reciprocal(det, det)
        mb.tt(t3, t1, t2, OP.add)
        mb.ts(t3, t3, -1.0, 1.0, OP.mult, OP.add)
        cc_ = alloc("cc", [128, NT16]); ss_ = alloc("ss", [128, NT16])
        mb.tt(cc_, t3, det, OP.mult)
        mb.stt(ss_, wh, 2.0, det, OP.mult, OP.mult)
        mb.act(t1, cc_, AF.Square)
        mb.act(t2, ss_, AF.Square)
        mb.tt(t3, t1, t2, OP.add)
        mb.ts(t3, t3, 1e-16, None, OP.max)
        mb.act(t4, t3, AF.Ln)
        mb.ts(t5, t4, 0.5, None, OP.mult)
        mb.ts(t5, t5, -18.420680743952367, 0.0, OP.max, OP.min)
        mb.act(t1, ty["rr"], AF.Sigmoid)
        lm2 = alloc("lm2", [128, NT16])
        mb.stt(lm2, t1, 8.0, t5, OP.mult, OP.mult)
        mag2 = alloc("mag2", [128, NT16])
        mb.act(mag2, lm2, AF.Exp)
        mb.act(t1, mag2, AF.Square)
        mb.ts(t1, t1, -1.0, 1.0, OP.mult, OP.add)
        mb.ts(t1, t1, 0.0, 1.0, OP.max, OP.min)
        mb.ts(t1, t1, 1e-6, None, OP.add)
        mb.act(t1, t1, AF.Sqrt)
        mb.act(t2, ty["selB"], AF.Sigmoid)
        vp = alloc("vp", [128, NT16])
        mb.tt(vp, t2, t1, OP.mult)
        mb.act(g_beta, ty["betar"], AF.Sigmoid)
        mb.act(t1, ty["selC"], AF.Sigmoid)
        for h2 in range(2):
            mb.copy(g_selC2.rearrange("p (t h c) -> p t h c", h=16, c=2)[:, :, :, h2],
                    t1.rearrange("p (t h) -> p t h", h=16))
        rmag = alloc("rmag", [128, NT16])
        mb.act(rmag, t4, AF.Exp, scale=-0.5)
        cth = alloc("cth", [128, NT16]); sth = alloc("sth", [128, NT16])
        mb.tt(cth, cc_, rmag, OP.mult)
        mb.tt(sth, ss_, rmag, OP.mult)
        mb.ts(t1, cth, 1.0, None, OP.add)
        mb.act(t2, t1, AF.Square)
        mb.act(t3, sth, AF.Square)
        mb.tt(t2, t2, t3, OP.add)
        mb.ts(t2, t2, 1e-30, None, OP.max)
        mb.act(t2, t2, AF.Sqrt)
        mb.tt(t1, t1, t2, OP.add)
        mb.ts(t2, t1, 1e-20, None, OP.is_gt)
        mb.ts(t1, t1, 1e-30, None, OP.max)
        mb.reciprocal(t1, t1)
        mb.tt(t1, sth, t1, OP.mult)
        mb.act(t1, t1, AF.Arctan)
        th = alloc("th", [128, NT16])
        mb.ts(th, t1, -4.0, None, OP.mult)
        mb.ts(t3, t2, -1.0, 1.0, OP.mult, OP.add)
        mb.ts(t3, t3, -np.pi, None, OP.mult)
        mb.tt(th, th, t2, OP.mult)
        mb.tt(th, th, t3, OP.add)
        mb.mm(p.psA[:, 0:NT16] if NT16 <= 512 else p.psA, p.cp.cumul, lm2, start=True, stop=True)
        mb.copy(g_lm, p.psA[:, 0:NT16])
        mb.act(g_E, g_lm, AF.Exp)
        Th = alloc("Th", [128, NT16])
        mb.mm(p.psB[:, 0:NT16], p.cp.cumul, th, start=True, stop=True)
        mb.copy(Th, p.psB[:, 0:NT16])
        iT = alloc("iT", [128, NT16], I32)
        kf = alloc("kf", [128, NT16])
        mb.ts(t1, Th, 1.0 / (2 * np.pi), None, OP.mult)
        mb.copy(iT, t1)
        mb.copy(kf, iT)
        C1, C2 = 6.28125, 2 * np.pi - 6.28125
        mb.stt(t2, kf, -C1, Th, OP.mult, OP.add)
        mb.stt(t2, kf, -C2, t2, OP.mult, OP.add)
        mb.ts(t3, t2, np.pi, None, OP.is_gt)
        mb.stt(t2, t3, -2 * np.pi, t2, OP.mult, OP.add)
        mb.ts(t3, t2, -np.pi, None, OP.is_lt)
        mb.stt(t2, t3, 2 * np.pi, t2, OP.mult, OP.add)
        mb.act(g_sin, t2, AF.Sin)
        mb.act(t3, t2, AF.Abs)
        mb.act(g_cos, t3, AF.Sin, scale=-1.0, bias=np.pi / 2)
        mb.mm(p.psA[0:1, 0:NT16], p.cp.ones_col, lm2, start=True, stop=True)
        mb.copy(g_ECrow, p.psA[0:1, 0:NT16])
        mb.act(g_ECrow, g_ECrow, AF.Exp)
        mb.mm(p.psB[0:1, 0:NT16], p.cp.ones_col, lm2, start=True, stop=True)
        LmCrow = alloc("LmCrow", [1, NT16])
        mb.copy(LmCrow, p.psB[0:1, 0:NT16])
        LmCrep = alloc("LmCrep", [128, NT16])
        mb.mm(p.psC[:, 0:NT16], p.cp.ones_row, LmCrow, start=True, stop=True)
        mb.copy(LmCrep, p.psC[:, 0:NT16])
        mb.tt(t1, LmCrep, g_lm, OP.subtract)
        Df = alloc("Df", [128, NT16])
        mb.act(Df, t1, AF.Exp)
        mb.mm(p.psA[0:1, 0:NT16], p.cp.ones_col, th, start=True, stop=True)
        ThC = alloc("ThC", [1, NT16])
        mb.copy(ThC, p.psA[0:1, 0:NT16])
        r1 = alloc("r1", [1, NT16]); r2 = alloc("r2", [1, NT16]); r3 = alloc("r3", [1, NT16])
        iR = alloc("iR", [1, NT16], I32)
        mb.ts(r1, ThC, 1.0 / (2 * np.pi), None, OP.mult)
        mb.copy(iR, r1)
        mb.copy(r1, iR)
        mb.stt(r2, r1, -C1, ThC, OP.mult, OP.add)
        mb.stt(r2, r1, -C2, r2, OP.mult, OP.add)
        mb.ts(r3, r2, np.pi, None, OP.is_gt)
        mb.stt(r2, r3, -2 * np.pi, r2, OP.mult, OP.add)
        mb.ts(r3, r2, -np.pi, None, OP.is_lt)
        mb.stt(r2, r3, 2 * np.pi, r2, OP.mult, OP.add)
        mb.act(g_sinC, r2, AF.Sin)
        mb.act(r3, r2, AF.Abs)
        mb.act(g_cosC, r3, AF.Sin, scale=-1.0, bias=np.pi / 2)
        for h2 in range(2):
            mb.copy(g_bE2.rearrange("p (t h c) -> p t h c", h=16, c=2)[:, :, :, h2],
                    g_E.rearrange("p (t h) -> p t h", h=16))
            mb.copy(g_Df2.rearrange("p (t h c) -> p t h c", h=16, c=2)[:, :, :, h2],
                    Df.rearrange("p (t h) -> p t h", h=16))
        mb.copy(g_E2d, g_bE2)
        bdup = alloc("bdup", [128, NT * 32])
        for h2 in range(2):
            mb.copy(bdup.rearrange("p (t h c) -> p t h c", h=16, c=2)[:, :, :, h2],
                    g_beta.rearrange("p (t h) -> p t h", h=16))
        mb.tt(g_bE2, g_bE2, bdup, OP.mult)
        vg = alloc("vg", [128, NT * 32])
        for c in range(NT):
            mb.transpose(p.psA[:, 0:32], p.vfm[:, c * 128:(c + 1) * 128], p.cp.ident[0:32, 0:32])
            mb.copy(vg[:, c * 32:(c + 1) * 32], p.psA[:, 0:32])
        vpd = alloc("vpd", [128, NT * 32])
        for h2 in range(2):
            mb.copy(vpd.rearrange("p (t h c) -> p t h c", h=16, c=2)[:, :, :, h2],
                    vp.rearrange("p (t h) -> p t h", h=16))
        mb.tt(vg, vg, vpd, OP.mult)
        vre = vg.rearrange("p (t h c) -> p c t h", c=2, h=16)[:, 0]
        vim = vg.rearrange("p (t h c) -> p c t h", c=2, h=16)[:, 1]
        wre = g_bv.rearrange("p (t h c) -> p c t h", c=2, h=16)[:, 0]
        wim = g_bv.rearrange("p (t h c) -> p c t h", c=2, h=16)[:, 1]
        cthv = g_cos.rearrange("p (t h) -> p t h", h=16)
        sthv = g_sin.rearrange("p (t h) -> p t h", h=16)
        mb.tt(wre, vre, cthv, OP.mult)
        mb.tt(t1.rearrange("p (t h) -> p t h", h=16), vim, sthv, OP.mult)
        mb.tt(wre, wre, t1.rearrange("p (t h) -> p t h", h=16), OP.add)
        mb.tt(wim, vim, cthv, OP.mult)
        mb.tt(t1.rearrange("p (t h) -> p t h", h=16), vre, sthv, OP.mult)
        mb.tt(wim, wim, t1.rearrange("p (t h) -> p t h", h=16), OP.subtract)
        mb.tt(g_bv, g_bv, bdup, OP.mult)
    with p.stage() as alloc:
        NT16 = NT * 16
        Kfm = alloc("Kfm", [128, 16 * 128], BF16)
        Qfm = alloc("Qfm", [128, 16 * 128], BF16)
        Ktm = alloc("Ktm", [128, 16 * 128], BF16)
        Khtm = alloc("Khtm", [128, 16 * 128], BF16)
        Khfm = alloc("Khfm", [128, 16 * 128], BF16)
        colb = alloc("colb", [128, 128])
        Atile = alloc("Atile", [128, 2048])
        nrep = alloc("nrep", [128, 2048])
        e2a = alloc("e2a", [128, 2048])
        kkall = alloc("kkall", [128, 2048])
        NTd = alloc("NTd", [128, 2048], BF16)
        NT12 = alloc("NT12", [128, 16 * 64], BF16)
        Nall = alloc("Nall", [128, 2048], BF16)
        Lh = alloc("Lh", [128, 2048], BF16)
        Gall = alloc("Gall", [128, 2048], BF16)
        Gtall = alloc("Gtall", [128, 2048], BF16)
        Sall = alloc("Sall", [128, 2048], BF16)
        Stall = alloc("Stall", [128, 2048], BF16)
        G2b0 = alloc("G2b0", [64, 16 * 64], BF16)
        mkneg = alloc("mkneg", [128, 2048])
        mkle = alloc("mkle", [128, 2048])
        mkblk = alloc("mkblk", [128, 2048])
        idrep = alloc("idrep", [128, 2048], BF16)
        idrepf = alloc("idrepf", [128, 2048])
        mb.dma(mkneg, cs["c_mask_st_neg16"])
        mb.dma(mkle, cs["c_mask_le16"])
        mb.dma(mkblk, cs["c_blkdiag16"])
        mb.dma(idrepf, cs["c_identrep16"])
        mb.copy(idrep, idrepf)
        s0 = alloc("s0", [128, 32])
        s0bf = alloc("s0bf", [128, 32], BF16)
        mb.memset(s0, 0.0)
        mb.memset(s0bf, 0.0)
        bb = alloc("bb", [128, 32])
        bbbf = alloc("bbbf", [128, 32], BF16)
        uu = alloc("uu", [128, 32])
        uubf = alloc("uubf", [128, 32], BF16)
        w2c = alloc("w2c", [64, 32], BF16)
        h64 = alloc("h64", [64, 32])
        h128 = alloc("h128", [128, 32])
        yv = alloc("yv", [128, 32])
        y2 = alloc("y2", [128, 32])
        dub = alloc("dub", [128, 32], BF16)
        ecd = alloc("ecd", [128, 32])
        ccd = alloc("ccd", [128, 32])
        ssd = alloc("ssd", [128, 32])
        rows3 = alloc("rows3", [1, 96])
        snew = alloc("snew", [128, 32])
        for c in range(NT):
            for h in range(16):
                mb.dma(Kfm[:, h * 128:(h + 1) * 128], scr["knfm"][h * 128:(h + 1) * 128, c * 128:(c + 1) * 128])
                mb.dma(Qfm[:, h * 128:(h + 1) * 128], scr["qnfm"][h * 128:(h + 1) * 128, c * 128:(c + 1) * 128])
                mb.dma(Ktm[:, h * 128:(h + 1) * 128], scr["kntm"][c * 128:(c + 1) * 128, h * 128:(h + 1) * 128])
            for h in range(16):
                mb.ts(Khtm[:, h * 128:(h + 1) * 128], Ktm[:, h * 128:(h + 1) * 128],
                      g_beta[:, c * 16 + h:c * 16 + h + 1], None, OP.mult)
                pst = p.psT[:, (h % 4) * 128:(h % 4 + 1) * 128]
                mb.transpose((pst, f"t{h % 4}"), Khtm[:, h * 128:(h + 1) * 128], p.cp.ident_bf)
                mb.copy(Khfm[:, h * 128:(h + 1) * 128], (pst, f"t{h % 4}"), eng="act")
            for h in range(16):
                mb.ts(colb, mkle[:, 0:128], 0.0, g_lm[:, c * 16 + h:c * 16 + h + 1], OP.mult, OP.add)
                mb.transpose((p.psE[:, h * 128:(h + 1) * 128], f"q{h % 4}"), colb, p.cp.ident)
                mb.ts(nrep[:, h * 128:(h + 1) * 128], mkle[:, 0:128], 0.0,
                      g_lm[:, c * 16 + h:c * 16 + h + 1], OP.mult, OP.add)
            mb.copy(Atile, p.psE, eng="act")
            mb.tt(e2a, Atile, nrep, OP.subtract)
            mb.ts(e2a, e2a, 0.0, None, OP.min)
            mb.act(e2a, e2a, AF.Exp)
            for h in range(16):
                mb.mm((p.psE[:, h * 128:(h + 1) * 128], f"q{h % 4}"),
                      Kfm[:, h * 128:(h + 1) * 128], Khfm[:, h * 128:(h + 1) * 128],
                      start=True, stop=True)
            mb.copy(kkall, p.psE, eng="act")
            mb.tt(kkall, kkall, e2a, OP.mult)
            mb.tt(NTd, kkall, mkneg, OP.mult)
            for h in range(16):
                mb.copy(NT12[0:64, h * 64:(h + 1) * 64], NTd[0:64, h * 128 + 64:h * 128 + 128])
            mb.tt(NTd, NTd, mkblk, OP.mult)
            for g4 in range(4):
                for hh in range(4):
                    h = g4 * 4 + hh
                    mb.transpose((p.psT[:, hh * 128:(hh + 1) * 128], f"t{hh}"),
                                 NTd[:, h * 128:(h + 1) * 128], p.cp.ident_bf)
                mb.copy(Nall[:, g4 * 512:(g4 + 1) * 512], p.psT, eng="act")
            for h in range(16):
                mb.mm((p.psE[:, h * 128:(h + 1) * 128], f"q{h % 4}"),
                      Kfm[:, h * 128:(h + 1) * 128], Qfm[:, h * 128:(h + 1) * 128],
                      start=True, stop=True)
            mb.copy(kkall, p.psE, eng="act")
            mb.tt(kkall, kkall, e2a, OP.mult)
            mb.tt(Lh, kkall, mkle, OP.mult)
            mb.tt(Gall, NTd, idrep, OP.add)
            mb.tt(Gtall, Nall, idrep, OP.add)
            mb.copy(Sall, NTd)
            mb.copy(Stall, Nall)
            for lev in range(5):
                for h in range(16):
                    mb.mm((p.psE[:, h * 128:(h + 1) * 128], f"q{h % 4}"),
                          Stall[:, h * 128:(h + 1) * 128], Sall[:, h * 128:(h + 1) * 128],
                          start=True, stop=True)
                mb.copy(nrep, p.psE, eng="act")
                for h in range(16):
                    mb.mm((p.psE[:, h * 128:(h + 1) * 128], f"q{h % 4}"),
                          Sall[:, h * 128:(h + 1) * 128], Stall[:, h * 128:(h + 1) * 128],
                          start=True, stop=True)
                mb.copy(Stall, p.psE, eng="act")
                mb.copy(Sall, nrep)
                for h in range(16):
                    mb.mm((p.psE[:, h * 128:(h + 1) * 128], f"q{h % 4}"),
                          Gtall[:, h * 128:(h + 1) * 128], Sall[:, h * 128:(h + 1) * 128],
                          start=True, stop=True)
                mb.stt(Gall, p.psE, 1.0, Gall, OP.mult, OP.add)
                for h in range(16):
                    mb.mm((p.psE[:, h * 128:(h + 1) * 128], f"q{h % 4}"),
                          Sall[:, h * 128:(h + 1) * 128], Gtall[:, h * 128:(h + 1) * 128],
                          start=True, stop=True)
                mb.stt(Gtall, p.psE, 1.0, Gtall, OP.mult, OP.add)
            for h in range(16):
                mb.dma(G2b0[:, h * 64:(h + 1) * 64], Gall[64:128, h * 128 + 64:h * 128 + 128])
            for h in range(16):
                mb.mm(p.psA[:, h * 2:h * 2 + 2], Kfm[:, h * 128:(h + 1) * 128],
                      s0bf[:, h * 2:h * 2 + 2], start=True, stop=True)
            mb.tt(bb, p.psA[:, 0:32], g_bE2[:, c * 32:(c + 1) * 32], OP.mult)
            mb.tt(bb, g_bv[:, c * 32:(c + 1) * 32], bb, OP.subtract)
            mb.copy(bbbf, bb)
            for h in range(16):
                mb.mm(p.psB[:, h * 2:h * 2 + 2], Gall[:, h * 128:(h + 1) * 128],
                      bbbf[:, h * 2:h * 2 + 2], start=True, stop=True)
            mb.copy(uu, p.psB[:, 0:32])
            mb.copy(uubf, uu)
            for h in range(16):
                mb.mm(p.psC[0:64, h * 2:h * 2 + 2], NT12[0:64, h * 64:(h + 1) * 64],
                      uubf[0:64, h * 2:h * 2 + 2], start=True, stop=True)
            mb.copy(w2c, p.psC[0:64, 0:32])
            for h in range(16):
                mb.mm(p.psB[0:64, h * 2:h * 2 + 2], G2b0[:, h * 64:(h + 1) * 64],
                      w2c[:, h * 2:h * 2 + 2], start=True, stop=True)
            mb.copy(h64, p.psB[0:64, 0:32])
            mb.dma(h128[64:128, :], h64)
            mb.tt(uu[64:128, :], uu[64:128, :], h128[64:128, :], OP.add)
            mb.copy(uubf, uu)
            for h in range(16):
                mb.mm(p.psA[:, h * 2:h * 2 + 2], Qfm[:, h * 128:(h + 1) * 128],
                      s0bf[:, h * 2:h * 2 + 2], start=True, stop=True)
            mb.tt(yv, p.psA[:, 0:32], g_E2d[:, c * 32:(c + 1) * 32], OP.mult)
            for h in range(16):
                mb.mm(p.psB[:, h * 2:h * 2 + 2], Lh[:, h * 128:(h + 1) * 128],
                      uubf[:, h * 2:h * 2 + 2], start=True, stop=True)
            mb.tt(yv, yv, p.psB[:, 0:32], OP.add)
            cv = g_cos.rearrange("p (t h) -> p t h", h=16)[:, c]
            sv = g_sin.rearrange("p (t h) -> p t h", h=16)[:, c]
            yre = yv.rearrange("p (h k) -> p k h", k=2, h=16)[:, 0]
            yim = yv.rearrange("p (h k) -> p k h", k=2, h=16)[:, 1]
            ore = y2.rearrange("p (h k) -> p k h", k=2, h=16)[:, 0]
            oim = y2.rearrange("p (h k) -> p k h", k=2, h=16)[:, 1]
            tt16 = snew.rearrange("p (h k) -> p k h", k=2, h=16)[:, 0]
            mb.tt(ore, yre, cv, OP.mult)
            mb.tt(tt16, yim, sv, OP.mult)
            mb.tt(ore, ore, tt16, OP.subtract)
            mb.tt(oim, yim, cv, OP.mult)
            mb.tt(tt16, yre, sv, OP.mult)
            mb.tt(oim, oim, tt16, OP.add)
            mb.tt(y2, y2, g_selC2[:, c * 32:(c + 1) * 32], OP.mult)
            mb.copy(retr_tm[:, c * 32:(c + 1) * 32], y2)
            mb.ts(dub, uubf, 1.0, None, OP.mult) if False else None
            mb.tt(dub, uubf, g_Df2[:, c * 32:(c + 1) * 32], OP.mult)
            for h in range(16):
                mb.mm(p.psC[:, h * 2:h * 2 + 2], Ktm[:, h * 128:(h + 1) * 128],
                      dub[:, h * 2:h * 2 + 2], start=True, stop=True)
            mb.copy(rows3[:, 0:32][:, 0:16] if False else rows3[:, 0:16], g_ECrow[:, c * 16:(c + 1) * 16])
            mb.copy(rows3[:, 16:32], g_cosC[:, c * 16:(c + 1) * 16])
            mb.copy(rows3[:, 32:48], g_sinC[:, c * 16:(c + 1) * 16])
            mb.mm(p.psB[:, 0:48], p.cp.ones_row, rows3[:, 0:48], start=True, stop=True)
            for h2 in range(2):
                mb.copy(ecd.rearrange("p (h k) -> p k h", k=2, h=16)[:, h2], p.psB[:, 0:16])
                mb.copy(ccd.rearrange("p (h k) -> p k h", k=2, h=16)[:, h2], p.psB[:, 16:32])
                mb.copy(ssd.rearrange("p (h k) -> p k h", k=2, h=16)[:, h2], p.psB[:, 32:48])
            mb.stt(snew, s0, 1.0, snew, OP.mult, OP.bypass) if False else None
            mb.tt(snew, s0, ecd, OP.mult)
            mb.tt(snew, snew, p.psC[:, 0:32], OP.add)
            sre = snew.rearrange("p (h k) -> p k h", k=2, h=16)[:, 0]
            sim_ = snew.rearrange("p (h k) -> p k h", k=2, h=16)[:, 1]
            zre = s0.rearrange("p (h k) -> p k h", k=2, h=16)[:, 0]
            zim = s0.rearrange("p (h k) -> p k h", k=2, h=16)[:, 1]
            c16 = ccd.rearrange("p (h k) -> p k h", k=2, h=16)[:, 0]
            s16 = ssd.rearrange("p (h k) -> p k h", k=2, h=16)[:, 0]
            t16b = uu.rearrange("p (h k) -> p k h", k=2, h=16)[:, 0]
            mb.tt(zre, sre, c16, OP.mult)
            mb.tt(t16b, sim_, s16, OP.mult)
            mb.tt(zre, zre, t16b, OP.subtract)
            mb.tt(zim, sim_, c16, OP.mult)
            mb.tt(t16b, sre, s16, OP.mult)
            mb.tt(zim, zim, t16b, OP.add)
            mb.copy(s0bf, s0)
    return


def emit_l0_post(p, w, cs, scr, retr_tm, xn_tm, a0_tm):
    mb, T, NT = p.mb, p.T, p.NT
    with p.stage() as alloc:
        retrfm = alloc("retrfm", [32, T], BF16)
        rtb = alloc("rtb", [128, 32], BF16)
        for c in range(NT):
            mb.copy(rtb, retr_tm[:, c * 32:(c + 1) * 32])
            mb.transpose(p.psT[0:32, 0:128], rtb, p.cp.ident_bf)
            mb.copy(retrfm[:, c * 128:(c + 1) * 128], p.psT[0:32, 0:128])
        rw = wload_bf(p, alloc, w["readout_w"], 32, DI, "rw")
        ssum = alloc("ssum", [128, 16])
        ssq = alloc("ssq", [128, 16])
        cA = alloc("cA", [128, 1]); cB = alloc("cB", [128, 1])
        ybuf = alloc("ybuf", [128, 512], BF16)
        sqs = alloc("sqs", [128, 512])
        mb.memset(ssum, 0.0)
        mb.memset(ssq, 0.0)
        for s0 in range(0, T, 512):
            for o in range(16):
                mb.mm(p.psA, rw[0:32, o * 128:(o + 1) * 128], retrfm[:, s0:s0 + 512],
                      start=True, stop=True)
                mb.ts(ybuf, p.psA, 1.0, 0.0, OP.mult, OP.add, accum_out=cA)
                mb.act(sqs, p.psA, AF.Square, accum_out=cB)
                mb.dma(scr["yfm"][o * 128:(o + 1) * 128, s0:s0 + 512], ybuf, eng="pool")
                mb.tt(ssum[:, o:o + 1], ssum[:, o:o + 1], cA, OP.add)
                mb.tt(ssq[:, o:o + 1], ssq[:, o:o + 1], cB, OP.add)
        mb.mm(p.psB[0:1, 0:16], p.cp.ones_col, ssum, start=True, stop=True)
        mb.mm(p.psC[0:1, 0:16], p.cp.ones_col, ssq, start=True, stop=True)
        grow = alloc("grow", [1, 48])
        mb.ts(grow[:, 0:16], p.psB[0:1, 0:16], 1.0 / (T * 128), None, OP.mult)
        mb.ts(grow[:, 16:32], p.psC[0:1, 0:16], 1.0 / (T * 128), None, OP.mult)
        mb.act(grow[:, 32:48], grow[:, 0:16], AF.Square)
        mb.tt(grow[:, 16:32], grow[:, 16:32], grow[:, 32:48], OP.subtract)
        mb.ts(grow[:, 16:32], grow[:, 16:32], 1e-5, None, OP.add)
        mb.act(grow[:, 16:32], grow[:, 16:32], AF.Sqrt)
        mb.reciprocal(grow[:, 16:32], grow[:, 16:32])
        gcols = alloc("gcols", [128, 32])
        mb.mm(p.psB[:, 0:32], p.cp.ones_row, grow[:, 0:32], start=True, stop=True)
        mb.copy(gcols, p.psB[:, 0:32])
        wo = wload_bf(p, alloc, w["out_w"], DI, DM, "ow")
        gnw = bias_cols(p, alloc, w["gn_w"], DI, "gnw")
        gnb = bias_cols(p, alloc, w["gn_b"], DI, "gnb")
        dpc = bias_cols(p, alloc, w["D_param"], DI, "dp")
        y5 = alloc("y5", [128, 16 * 512], BF16)
        zb = alloc("zbuf", [128, 512])
        xcb = alloc("xcbuf", [128, 512], BF16)
        t0_ = alloc("t0", [128, 512])
        xnt = alloc("xnt", [128, DM])
        a0b = alloc("a0b", [128, DM])
        for s0 in range(0, T, 512):
            for o in range(16):
                mb.dma(ybuf, scr["yfm"][o * 128:(o + 1) * 128, s0:s0 + 512])
                mb.dma(zb, scr["zfm"][o * 128:(o + 1) * 128, s0:s0 + 512])
                mb.dma(xcb, scr["xcfm"][o * 128:(o + 1) * 128, s0:s0 + 512])
                mb.ts(t0_, ybuf, gcols[:, o:o + 1], gcols[:, 16 + o:17 + o], OP.subtract, OP.mult)
                mb.ts(t0_, t0_, gnw[:, o:o + 1], gnb[:, o:o + 1], OP.mult, OP.add)
                mb.act(zb, zb, AF.Silu)
                mb.tt(t0_, t0_, zb, OP.mult)
                mb.stt(t0_, xcb, dpc[:, o:o + 1], t0_, OP.mult, OP.add)
                mb.copy(y5[:, o * 512:(o + 1) * 512], t0_)
            for q in range(4):
                t = s0 // 128 + q
                mb.dma(xnt, xn_tm[t * 128:(t + 1) * 128, :])
                for oc in range(2):
                    for k in range(16):
                        mb.mm((p.psE[:, oc * 512:(oc + 1) * 512], f"o{oc}"),
                              y5[:, k * 512 + q * 128:k * 512 + q * 128 + 128],
                              wo[:, k * DM + oc * 512:k * DM + oc * 512 + 512],
                              start=(k == 0), stop=(k == 15))
                    mb.tt(a0b[:, oc * 512:(oc + 1) * 512], (p.psE[:, oc * 512:(oc + 1) * 512], f"o{oc}"),
                          xnt[:, oc * 512:(oc + 1) * 512], OP.add)
                mb.dma(a0_tm[t * 128:(t + 1) * 128, :], a0b, eng="pool")


def emit_ln_simple(p, xin_tm, out_fm, out_tm):
    mb = p.mb
    with p.stage() as alloc:
        xt = alloc("x", [128, DM])
        bs = alloc("bs", [128, 12])
        mv = alloc("mv", [128, 2])
        cols = alloc("cols", [128, 3])
        xn = alloc("xn", [128, DM])
        xnb = alloc("xnb", [128, DM], BF16)
        tp = alloc("tp", [128, 128], BF16)
        for t in range(p.NT):
            mb.dma(xt, xin_tm[t * 128:(t + 1) * 128, :])
            for s in range(2):
                mb.bn_stats(bs[:, 6 * s:6 * s + 6], xt[:, 512 * s:512 * (s + 1)])
            mb.bn_aggr(mv, bs)
            mb.ts(cols[:, 0:1], mv[:, 1:2], 1e-5, None, OP.add)
            mb.act(cols[:, 1:2], cols[:, 0:1], AF.Sqrt)
            mb.reciprocal(cols[:, 2:3], cols[:, 1:2])
            mb.ts(xn, xt, mv[:, 0:1], cols[:, 2:3], OP.subtract, OP.mult)
            if out_tm is not None:
                mb.dma(out_tm[t * 128:(t + 1) * 128, :], xn, eng="pool")
            mb.copy(xnb, xn)
            for c in range(DM // 128):
                mb.transpose(p.psT[:, 0:128], xnb[:, c * 128:(c + 1) * 128], p.cp.ident_bf)
                mb.copy(tp, p.psT[:, 0:128])
                mb.dma(out_fm[c * 128:(c + 1) * 128, t * 128:(t + 1) * 128], tp, eng="pool")


def emit_add3(p, a_tm, b_tm, c_tm, out_tm):
    mb = p.mb
    with p.stage() as alloc:
        ta = alloc("ta", [128, DM]); tb = alloc("tb", [128, DM]); tc = alloc("tc", [128, DM])
        for t in range(p.NT):
            mb.dma(ta, a_tm[t * 128:(t + 1) * 128, :])
            mb.dma(tb, b_tm[t * 128:(t + 1) * 128, :], eng="act")
            mb.dma(tc, c_tm[t * 128:(t + 1) * 128, :], eng="sp")
            mb.tt(ta, ta, tb, OP.add)
            mb.tt(ta, ta, tc, OP.add)
            mb.dma(out_tm[t * 128:(t + 1) * 128, :], ta, eng="pool")


def build_full(T):
    p = Prog(T)
    NT = p.NT
    x_d = p.din("x", [T, DM])
    names_l0a = ["rms_w", "in_proj_w", "in_proj_b", "conv_w", "conv_b", "gate_w", "gate_b",
                 "log_dt_scale", "Q_w", "readout_w", "out_w", "gn_w", "gn_b", "D_param"]
    names_l0f = ["enc_W", "enc_b", "s2c_W", "s2c_b", "rW1", "rb1", "rW2", "rb2", "We1", "be1",
                 "We2", "be2", "c2s_W", "c2s_b", "dec_W", "dec_b", "ln_g", "ln_b"]
    names_l1a = ["Wq", "bq", "Wk", "bk", "Wv", "bv", "Wa", "ba", "Wb", "bb", "Wg", "bg", "Wo", "bo"]
    names_l1f = ["Wqkv", "q_scale", "k_scale", "Wo", "ln_g", "ln_b", "W1", "b1", "W2", "b2"]
    SH = {
        "rms_w": (DM,), "in_proj_w": (DM, 6176), "in_proj_b": (6176,), "conv_w": (DI, 4),
        "conv_b": (DI,), "gate_w": (DI, 112), "gate_b": (112,), "log_dt_scale": (16,),
        "Q_w": (DI, DI), "readout_w": (32, DI), "out_w": (DI, DM), "gn_w": (DI,), "gn_b": (DI,),
        "D_param": (DI,),
        "enc_W": (DM, DI), "enc_b": (DI,), "s2c_W": (DI, 64), "s2c_b": (64,), "rW1": (64, 64),
        "rb1": (64,), "rW2": (64, 8), "rb2": (8,), "We1": (8, 64, 1024), "be1": (8, 1024),
        "We2": (8, 1024, 64), "be2": (8, 64), "c2s_W": (64, DI), "c2s_b": (DI,),
        "dec_W": (DI, DM), "dec_b": (DM,), "ln_g": (DM,), "ln_b": (DM,),
        "Wq": (DM, 128), "bq": (128,), "Wk": (DM, 128), "bk": (128,), "Wv": (DM, 128),
        "bv": (128,), "Wa": (DM, 1), "ba": (1,), "Wb": (DM, 1), "bb": (1,), "Wg": (DM, 128),
        "bg": (128,), "Wo": (128, DM), "bo": (DM,),
    }
    SH1F = {"Wqkv": (6, DM, 3072), "q_scale": (6, 64), "k_scale": (6, 64), "Wo": (6, 1024, DM),
            "ln_g": (6, DM), "ln_b": (6, DM), "W1": (6, DM, 5460), "b1": (6, 5460),
            "W2": (6, 2730, DM), "b2": (6, DM)}
    w0a = {k: p.din("L0_attn_" + k, SH[k]) for k in names_l0a}
    w0f = {k: p.din("L0_ffn_" + k, SH[k]) for k in names_l0f}
    w1a = {k: p.din("L1_attn_" + k, SH[k]) for k in names_l1a}
    w1f = {k: p.din("L1_ffn_" + k, SH1F[k]) for k in names_l1f}
    cs = {k: p.din(k, v.shape) for k, v in host_constants(T).items()}
    out_d = p.dout("out", [T, DM])
    scr = {
        "zfm": p.dscratch("zfm", [DI, T]),
        "xcfm": p.dscratch("xcfm", [DI, T], BF16),
        "knfm": p.dscratch("knfm", [DI, T], BF16),
        "kntm": p.dscratch("kntm", [T, DI], BF16),
        "qnfm": p.dscratch("qnfm", [DI, T], BF16),
        "yfm": p.dscratch("yfm", [DI, T], BF16),
    }
    xn_tm = p.dscratch("xn_tm", [T, DM])
    xnfm = p.dscratch("xnfm", [DM, T], BF16)
    xrfm = p.dscratch("xrfm", [DM, T], BF16)
    spike = p.dscratch("spike", [DI, T], BF16)
    spike2 = p.dscratch("spike2", [DI, T], BF16)
    a0_tm = p.dscratch("a0_tm", [T, DM])
    f0_tm = p.dscratch("f0_tm", [T, DM])
    x1_tm = p.dscratch("x1_tm", [T, DM])
    xn1fm = p.dscratch("xn1fm", [DM, T], BF16)
    a1_tm = p.dscratch("a1_tm", [T, DM])
    l1o = p.dscratch("l1o", [T, 128], BF16)
    l1x_tm = p.dscratch("l1x_tm", [T, DM])
    p.cp = CPool(p, cs)
    emit_ln0(p, x_d, xnfm, xrfm, xn_tm)
    with p.phase() as ph:
        p.salloc = ph
        retr = ph("retr", [128, NT * 32])
        emit_l0_attn(p, w0a, xn_tm, xrfm, a0_tm, cs, scr)
        emit_l0_gates_and_scan(p, w0a, cs, scr, retr)
        emit_l0_post(p, w0a, cs, scr, retr, xn_tm, a0_tm)
    with p.phase() as ph:
        p.salloc = ph
        emit_l0_ffn(p, w0f, xnfm, f0_tm, spike, spike2)
    emit_add3(p, x_d, a0_tm, f0_tm, x1_tm)
    emit_ln_simple(p, x1_tm, xn1fm, l1x_tm)
    emit_l1_attn(p, w1a, xn1fm, a1_tm, l1o, cs)
    for i in range(6):
        emit_l1_ffn_layer(p, w1f, i, l1x_tm, l1x_tm, cs)
    emit_add3(p, x1_tm, a1_tm, l1x_tm, out_d)
    p.mb.barrier("sp")
    return p


def kernel(**inputs):
    T = 4096
    p = build_full(T)
    p.mb.build()
    consts = host_constants(T)
    base = {}
    for name in p.inputs:
        if name == "x":
            continue
        if name in consts:
            base[name] = consts[name]
        else:
            base[name] = np.ascontiguousarray(np.asarray(inputs[name], dtype=np.float32))
    x = np.asarray(inputs["L0_attn_x"], dtype=np.float32)
    in_maps = []
    for c in range(8):
        m = dict(base)
        m["x"] = np.ascontiguousarray(x[c])
        in_maps.append(m)
    res = run_bass_kernel_spmd(p.nc, in_maps, core_ids=list(range(8)))
    out = np.stack([np.asarray(r["out"], dtype=np.float32) for r in res.results], axis=0)
    return out
